# version 02
# baseline (speedup 1.0000x reference)
import math
import numpy as np
import ml_dtypes
import concourse.bass as bass
import concourse.mybir as mybir
from concourse.bass_utils import run_bass_kernel_spmd

F32 = mybir.dt.float32
BF16 = mybir.dt.bfloat16
I32 = mybir.dt.int32
U8 = mybir.dt.uint8
AF = mybir.ActivationFunctionType
ALU = mybir.AluOpType
AX = mybir.AxisListType

P = 128
D = 1024
DC = D // P
T_CORE = 2048
SEQ = 8192
NCORES = 8
H0 = 8
FFN0_H = 3072
FFN1_H = 2816
H1 = 16
DH1 = 64
HPC = 4
MASK_C = 1000.0

bf16 = ml_dtypes.bfloat16



def _pp(b):
    return np.ascontiguousarray(b.reshape(-1, P).T.astype(np.float32))


def _row(b):
    return np.ascontiguousarray(b.reshape(1, -1).astype(np.float32))


def make_constants():
    c = {}
    c["ident_bf"] = np.eye(P, dtype=np.float32).astype(bf16)
    c["ident_f32"] = np.eye(P, dtype=np.float32)
    c["negCI_bf"] = (-MASK_C * np.eye(P, dtype=np.float32)).astype(bf16)
    kk, qq = np.meshgrid(np.arange(P), np.arange(P), indexing="ij")
    sameseg = (kk // 4) == (qq // 4)
    a0 = (~(sameseg & (kk != qq))).astype(np.float32)
    c["maskA_bf"] = a0.astype(bf16)
    ii = np.arange(P)[:, None]
    cp = np.arange(896)[None, :]
    c["U_big_bf"] = (ii > (cp - 384)).astype(np.float32).astype(bf16)
    c["ones_row_f32"] = np.ones((1, 512), dtype=np.float32)
    c["ones_col_f32"] = np.ones((1, P), dtype=np.float32)
    c["ones_col_bf"] = np.ones((P, 1), dtype=np.float32).astype(bf16)
    c["maskpos"] = np.full((P, 1), 0x7FFFFFFF, dtype=np.int32)
    c["eps_col"] = np.full((P, 1), 1e-5, dtype=np.float32)
    c["masksign"] = np.full((P, 1), -2147483648, dtype=np.int32)
    return c


def prep_inputs(inputs):
    g = {}
    x = np.ascontiguousarray(np.asarray(inputs["L0_attn_x"], dtype=np.float32)
                             .reshape(16384, D))
    g["qkwT"] = np.ascontiguousarray(inputs["L0_attn_proj_qk_w"].T.astype(np.float32))
    g["vwT"] = np.ascontiguousarray(inputs["L0_attn_proj_v_w"].T.astype(np.float32))
    wq, wk, wv = np.split(np.asarray(inputs["L0_attn_in_proj_w"]), 3, axis=0)
    g["wqT"] = np.ascontiguousarray(wq.T.astype(np.float32))
    g["wkT"] = np.ascontiguousarray(wk.T.astype(np.float32))
    g["wvT"] = np.ascontiguousarray(wv.T.astype(np.float32))
    g["owT"] = np.ascontiguousarray(inputs["L0_attn_out_proj_w"].T.astype(np.float32))
    bq, bk, bv = np.split(np.asarray(inputs["L0_attn_in_proj_b"]), 3)
    g["qk_b"] = _row(np.asarray(inputs["L0_attn_proj_qk_b"]))
    g["v_b"] = _row(np.asarray(inputs["L0_attn_proj_v_b"]))
    g["q_b"] = _row(bq)
    g["k_b"] = _row(bk)
    g["bv_row"] = _row(bv)
    g["ob_b"] = _row(np.asarray(inputs["L0_attn_out_proj_b"]))
    g["wg0T"] = np.ascontiguousarray(np.asarray(inputs["L0_ffn_w_gate"]).T)
    g["wu0T"] = np.ascontiguousarray(np.asarray(inputs["L0_ffn_w_up"]).T)
    g["wd0T"] = np.ascontiguousarray(np.asarray(inputs["L0_ffn_w_down"]).T)
    g["wg1"] = np.ascontiguousarray(np.asarray(inputs["L1_ffn_w_gate"], dtype=np.float32))
    g["wu1"] = np.ascontiguousarray(np.asarray(inputs["L1_ffn_w_up"], dtype=np.float32))
    g["wd1"] = np.ascontiguousarray(np.asarray(inputs["L1_ffn_w_down"], dtype=np.float32))
    g.update(make_constants())

    Wq = np.asarray(inputs["L1_attn_Wq"], dtype=np.float32)
    Wk = np.asarray(inputs["L1_attn_Wk"], dtype=np.float32)
    Wv = np.asarray(inputs["L1_attn_Wv"], dtype=np.float32)
    Wo = np.asarray(inputs["L1_attn_Wout"], dtype=np.float32)

    per_core = []
    for r in range(NCORES):
        m = dict(g)
        m["x_slice"] = np.ascontiguousarray(x[r * T_CORE:(r + 1) * T_CORE])
        hs = 4 * (r % 4)
        m["wq1"] = np.ascontiguousarray(Wq[:, hs:hs + HPC, :].reshape(D, HPC * DH1))
        m["wk1"] = np.ascontiguousarray(Wk[:, hs:hs + HPC, :].reshape(D, HPC * DH1))
        m["wv1"] = np.ascontiguousarray(Wv[:, hs:hs + HPC, :].reshape(D, HPC * DH1))
        m["wo1"] = np.ascontiguousarray(Wo[hs:hs + HPC].reshape(HPC * DH1, D))
        per_core.append(m)
    return per_core



class Arena:

    def __init__(self, nc, nbytes, name="arena"):
        self.nc = nc
        self.t = nc.alloc_sbuf_tensor(name, [P, nbytes], U8)
        self.nbytes = nbytes
        self.off = 0
        self.marks = []

    def alloc(self, shape, dtype, name=None):
        esz = mybir.dt.size(dtype)
        free = 1
        for s in shape[1:]:
            free *= s
        nb = free * esz
        off = (self.off + 31) & ~31
        assert off + nb <= self.nbytes, f"arena overflow: {name} {off}+{nb}"
        self.off = off + nb
        v = self.t[0:shape[0], off:off + nb].bitcast(dtype)
        if len(shape) > 2:
            pat = " ".join(f"d{i}" for i in range(1, len(shape)))
            v = v.rearrange(f"p ({pat}) -> p {pat}",
                            **{f"d{i}": shape[i] for i in range(1, len(shape))})
        return v

    def push(self):
        self.marks.append(self.off)

    def pop(self):
        self.off = self.marks.pop()


class Sem:

    def __init__(self, nc, name):
        self.h = nc.alloc_semaphore(name) if hasattr(nc, "alloc_semaphore") else None
        if self.h is None:
            self.cm = nc.semaphore(name)
            self.h = self.cm.__enter__()
        self.n = 0

    def inc(self, k=1):
        self.n += k
        return self.h


def barrier(block, sems):
    targets = [(sems[k], sems[k].n) for k in ("pe", "dv", "ac", "gp", "ld", "st", "cc")]
    for e in ("tensor", "vector", "scalar", "gpsimd", "sync"):
        sec = getattr(block, e)

        @sec
        def _(eng):
            for s, v in targets:
                if v > 0:
                    eng.wait_ge(s.h, v)



def emit_fp4_quant(eng, x, out_bf, tmp, nblk, nparts=P, scale_src=None):
    n = nblk * 16
    bmax = tmp["bmax"][0:nparts, 0:nblk]
    scale = tmp["scale"][0:nparts, 0:nblk]
    rscale = tmp["rscale"][0:nparts, 0:nblk]
    t = tmp["t"][0:nparts, 0:n]
    ta = tmp["ta"][0:nparts, 0:n]
    sb_t = tmp["sb"][0:nparts, 0:n]
    y1 = tmp["y1"][0:nparts, 0:n]
    y2 = tmp["y2"][0:nparts, 0:n]
    y3 = tmp["y3"][0:nparts, 0:n]
    c1 = tmp["c1"][0:nparts, 0:n]
    c2 = tmp["c2"][0:nparts, 0:n]
    c3 = tmp["c3"][0:nparts, 0:n]
    s12 = tmp["s12"][0:nparts, 0:n]
    qn = tmp["qn"][0:nparts, 0:n]
    maskpos, masksign = tmp["maskpos"], tmp["masksign"]
    xv = x.rearrange("p (b k) -> p b k", k=16)
    sw = tmp["selfwait"]
    sw(eng, eng.tensor_reduce(out=bmax, in_=xv, axis=AX.X, op=ALU.max,
                              apply_absolute_value=True))
    sw(eng, eng.tensor_scalar(out=scale, in0=bmax, scalar1=1.0 / 6.0, scalar2=1e-12,
                              op0=ALU.mult, op1=ALU.max))
    sw(eng, eng.reciprocal(out=rscale, in_=scale))
    rs_exp = rscale.unsqueeze(2).broadcast_to([nparts, nblk, 16])
    sc_exp = scale.unsqueeze(2).broadcast_to([nparts, nblk, 16])
    eng.tensor_tensor(out=t.rearrange("p (b k) -> p b k", k=16), in0=xv,
                      in1=rs_exp, op=ALU.mult)
    eng.tensor_scalar(out=ta.bitcast(I32), in0=t.bitcast(I32),
                      scalar1=maskpos[0:nparts], scalar2=None, op0=ALU.bitwise_and)
    eng.tensor_scalar(out=y1, in0=ta, scalar1=2.0, scalar2=128.0, op0=ALU.mult, op1=ALU.add)
    eng.tensor_scalar(out=y2, in0=ta, scalar1=128.0, scalar2=None, op0=ALU.add)
    eng.tensor_scalar(out=y3, in0=ta, scalar1=0.5, scalar2=128.0, op0=ALU.mult, op1=ALU.add)
    eng.tensor_scalar(out=c1, in0=y1, scalar1=132.0, scalar2=0.5, op0=ALU.min, op1=ALU.mult)
    eng.tensor_scalar(out=c2, in0=y2, scalar1=130.0, scalar2=132.0, op0=ALU.max, op1=ALU.min)
    eng.tensor_scalar(out=c3, in0=y3, scalar1=130.0, scalar2=2.0, op0=ALU.max, op1=ALU.mult)
    eng.tensor_tensor(out=s12, in0=c1, in1=c2, op=ALU.add)
    eng.scalar_tensor_tensor(out=qn, in0=c3, scalar=-454.0, in1=s12,
                             op0=ALU.add, op1=ALU.add)
    eng.tensor_scalar(out=sb_t.bitcast(I32), in0=t.bitcast(I32),
                      scalar1=masksign[0:nparts], scalar2=None, op0=ALU.bitwise_and)
    eng.tensor_tensor(out=qn.bitcast(I32), in0=qn.bitcast(I32),
                      in1=sb_t.bitcast(I32), op=ALU.bitwise_or)
    return eng.tensor_tensor(out=out_bf.rearrange("p (b k) -> p b k", k=16),
                              in0=qn.rearrange("p (b k) -> p b k", k=16),
                              in1=sc_exp, op=ALU.mult)


FP4_TMP_OPS = 18


def alloc_fp4_tmp(ar, n, cmaps, sem=None):
    tmp = {}
    if sem is not None:
        def sw(eng, inst, sem=sem):
            inst.then_inc(sem.inc(), 1)
            eng.wait_ge(sem.h, sem.n)
        tmp["selfwait"] = sw
    tmp["bmax"] = ar.alloc([P, n // 16], F32, "q_bmax")
    tmp["scale"] = ar.alloc([P, n // 16], F32, "q_scale")
    tmp["rscale"] = ar.alloc([P, n // 16], F32, "q_rscale")
    for k in ("t", "ta", "s12", "qn"):
        tmp[k] = ar.alloc([P, n], F32, "q_" + k)
    tmp["sb"] = tmp["ta"]
    for k in ("y1", "y2", "y3", "c1", "c2", "c3"):
        tmp[k] = ar.alloc([P, n], BF16, "q_" + k)
    tmp["maskpos"] = cmaps["maskpos"]
    tmp["masksign"] = cmaps["masksign"]
    return tmp




class Builder:
    def __init__(self, nc):
        self.nc = nc
        self.ar = Arena(nc, 207 * 1024)
        self.psall = nc.alloc_psum_tensor("psall", [P, 4096], F32)
        self.input_names = []
        self.sems = {}
        for name in ("ld", "gp", "pe", "dv", "ac", "st", "bar", "cc", "sc"):
            self.sems[name] = Sem(nc, "s_" + name)

    def psv(self, bank, nbanks=1, dtype=F32, shape=None):
        v = self.psall[:, bank * 512:(bank + nbanks) * 512]
        if dtype != F32:
            v = v.bitcast(dtype)
        if shape is not None:
            assert len(shape) == 2
            v = v[0:shape[0], 0:shape[1]]
        return v

    def din(self, name, shape, dtype):
        self.input_names.append(name)
        return self.nc.dram_tensor(name, list(shape), dtype, kind="ExternalInput").ap()

    def dout(self, name, shape, dtype):
        return self.nc.dram_tensor(name, list(shape), dtype, kind="ExternalOutput").ap()

    def dint(self, name, shape, dtype):
        return self.nc.dram_tensor(name, list(shape), dtype, kind="Internal").ap()


def fm_view(dram_ap):
    return dram_ap.rearrange("(c p) t -> p c t", p=P)


def emit_const_loads(b, blk, cm, loads):
    s_ld = b.sems["ld"]

    @blk.sync
    def _(sync):
        for name, src in loads:
            sync.dma_start(out=cm[name], in_=src).then_inc(s_ld.inc(16), 16)


def emit_ln_quant(b, blk, stage, src_d, xn_d, cm, xq_fm=None, fp4tmp=None, dbg_d=None):
    nc, ar, sems = b.nc, b.ar, b.sems
    s_ld, s_pe, s_dv, s_ac, s_st = sems["ld"], sems["pe"], sems["dv"], sems["ac"], sems["st"]
    NT = T_CORE // P
    ar.push()
    x_t = [ar.alloc([P, D], F32, "ln_x0"), ar.alloc([P, D], F32, "ln_x1")]
    st6 = [ar.alloc([P, 12], F32, "ln_st0"), ar.alloc([P, 12], F32, "ln_st1")]
    sq4 = ar.alloc([P, 4], F32, "ln_sq4")
    mg4 = ar.alloc([P, 4], F32, "ln_mg4")
    M24 = ar.alloc([P, 4], F32, "ln_M24")
    ex2a = ar.alloc([P, 4], F32, "ln_ex2a")
    esum2 = ar.alloc([P, 1], F32, "ln_es2")
    msq2 = ar.alloc([P, 1], F32, "ln_msq2")
    ex2 = ar.alloc([P, 4], F32, "ln_ex2")
    msum = [ar.alloc([P, 1], F32, "ln_ms0"), ar.alloc([P, 1], F32, "ln_ms1")]
    esum = ar.alloc([P, 1], F32, "ln_es")
    msq = ar.alloc([P, 1], F32, "ln_msq")
    var = [ar.alloc([P, 1], F32, "ln_va0"), ar.alloc([P, 1], F32, "ln_va1")]
    mneg = [ar.alloc([P, 1], F32, "ln_mn0"), ar.alloc([P, 1], F32, "ln_mn1")]
    lv = [ar.alloc([P, 1], F32, "ln_lv0"), ar.alloc([P, 1], F32, "ln_lv1")]
    rstd = [ar.alloc([P, 1], F32, "ln_rs0"), ar.alloc([P, 1], F32, "ln_rs1")]
    nm = [ar.alloc([P, 1], F32, "ln_nm0"), ar.alloc([P, 1], F32, "ln_nm1")]
    xn_bf = [ar.alloc([P, D], BF16, "ln_xb0"), ar.alloc([P, D], BF16, "ln_xb1")]
    quant = xq_fm is not None
    if quant:
        xn_f = [ar.alloc([P, D], F32, "ln_xf0"), ar.alloc([P, D], F32, "ln_xf1")]
        xq_t = [ar.alloc([P, D], BF16, "ln_xq0"), ar.alloc([P, D], BF16, "ln_xq1")]
        xq_t2 = [ar.alloc([P, D], BF16, "ln_xr0"), ar.alloc([P, D], BF16, "ln_xr1")]
    fm_t = [ar.alloc([P, D], BF16, "ln_fm0"), ar.alloc([P, D], BF16, "ln_fm1")]

    xv = src_d.rearrange("(n p) d -> n p d", p=P)
    xnv = fm_view(xn_d)
    xqv = fm_view(xq_fm) if quant else None

    mk = [{} for _ in range(NT)]

    for i in range(NT):
        j = i % 2

        @blk.sync
        def _(sync, i=i, j=j):
            if i >= 2:
                sync.wait_ge(s_dv.h, mk[i - 2]["x_done"])
            sync.dma_start(out=x_t[j], in_=xv[i]).then_inc(s_ld.inc(16), 16)

        ld_i = s_ld.n

        @blk.vector
        def _(vector, i=i, j=j, ld_i=ld_i):
            vector.wait_ge(s_ld.h, ld_i)
            if i >= 2:
                vector.wait_ge(s_ac.h, mk[i - 2]["stats_done"])
            def step(inst):
                inst.then_inc(s_dv.inc(), 1)
                vector.wait_ge(s_dv.h, s_dv.n)
            step(vector.bn_stats(out=st6[j][:, 0:6], in_=x_t[j][:, 0:512]))
            step(vector.bn_stats(out=st6[j][:, 6:12], in_=x_t[j][:, 512:1024]))
            mg0 = st6[j].rearrange("p (g k) -> p g k", k=3)[:, :, 1]
            M2g0 = st6[j].rearrange("p (g k) -> p g k", k=3)[:, :, 2]
            step(vector.tensor_copy(out=mg4, in_=mg0))
            step(vector.tensor_copy(out=M24, in_=M2g0))
            step(vector.tensor_tensor(out=sq4, in0=mg4, in1=mg4, op=ALU.mult))
            step(vector.tensor_scalar(out=ex2a, in0=M24, scalar1=1.0 / 256.0,
                                      scalar2=None, op0=ALU.mult))
            step(vector.tensor_tensor(out=ex2, in0=ex2a, in1=sq4, op=ALU.add))
            step(vector.reduce_sum(out=msum[j], in_=mg4, axis=AX.X))
            step(vector.reduce_sum(out=esum, in_=ex2, axis=AX.X))
            step(vector.tensor_scalar(out=mneg[j], in0=msum[j], scalar1=-0.25,
                                      scalar2=None, op0=ALU.mult))
            step(vector.tensor_tensor(out=msq, in0=msum[j], in1=msum[j], op=ALU.mult))
            step(vector.tensor_scalar(out=esum2, in0=esum, scalar1=0.25, scalar2=None,
                                      op0=ALU.mult))
            step(vector.tensor_scalar(out=msq2, in0=msq, scalar1=-0.0625, scalar2=None,
                                      op0=ALU.mult))
            step(vector.tensor_tensor(out=var[j], in0=esum2, in1=msq2, op=ALU.add))

        mk[i]["stats"] = s_dv.n

        @blk.scalar
        def _(scalar, i=i, j=j):
            scalar.wait_ge(s_dv.h, mk[i]["stats"])
            scalar.activation(out=lv[j], in_=var[j], func=AF.Ln, bias=cm["eps_col"][:, 0:1]).then_inc(s_ac.inc(), 1)
            scalar.activation(out=rstd[j], in_=lv[j], func=AF.Exp, scale=-0.5).then_inc(s_ac.inc(), 1)
        mk[i]["stats_done"] = s_ac.n

        if dbg_d is not None:
            @blk.sync
            def _(sync, i=i, j=j):
                sync.wait_ge(s_ac.h, mk[i]["stats_done"])
                for col, t_ in enumerate([msum[j], esum, esum2, msq, msq2, var[j], mneg[j], rstd[j]]):
                    sync.dma_start(out=dbg_d[:, i * 8 + col:i * 8 + col + 1],
                                   in_=t_[:, 0:1]).then_inc(s_st.inc(16), 16)

        @blk.vector
        def _(vector, i=i, j=j):
            vector.wait_ge(s_ac.h, mk[i]["stats_done"])
            vector.tensor_scalar(out=nm[j], in0=mneg[j], scalar1=rstd[j][:, 0:1],
                                 scalar2=None, op0=ALU.mult).then_inc(s_dv.inc(), 1)
            vector.wait_ge(s_dv.h, s_dv.n)
            if i >= 2:
                vector.wait_ge(s_pe.h, mk[i - 2]["fm_tr"])
            vector.tensor_scalar(out=xn_bf[j], in0=x_t[j], scalar1=rstd[j][:, 0:1],
                                 scalar2=nm[j][:, 0:1], op0=ALU.mult,
                                 op1=ALU.add).then_inc(s_dv.inc(), 1)
            if quant:
                vector.tensor_scalar(out=xn_f[j], in0=x_t[j], scalar1=rstd[j][:, 0:1],
                                     scalar2=nm[j][:, 0:1], op0=ALU.mult,
                                     op1=ALU.add).then_inc(s_dv.inc(), 1)
        mk[i]["x_done"] = s_dv.n

        @blk.tensor
        def _(tensor, i=i, j=j):
            tensor.wait_ge(s_dv.h, mk[i]["x_done"] - (1 if quant else 0))
            if i >= 2:
                tensor.wait_ge(s_dv.h, mk[i - 2]["fm_evac"])
            pv = b.psv(6 + j, 1, BF16)
            for c in range(DC):
                li = tensor.transpose(pv[:, c * P:(c + 1) * P], xn_bf[j][:, c * P:(c + 1) * P],
                                      cm["ident_bf"])
            li.then_inc(s_pe.inc(), 1)
        mk[i]["fm_tr"] = s_pe.n

        @blk.vector
        def _(vector, i=i, j=j):
            vector.wait_ge(s_pe.h, mk[i]["fm_tr"])
            if i >= 2:
                vector.wait_ge(s_st.h, mk[i - 2]["fm_store"])
            pv = b.psv(6 + j, 1, BF16)
            vector.tensor_copy(out=fm_t[j], in_=pv).then_inc(s_dv.inc(), 1)
        mk[i]["fm_evac"] = s_dv.n

        @blk.sync
        def _(sync, i=i, j=j):
            sync.wait_ge(s_dv.h, mk[i]["fm_evac"])
            sync.dma_start(out=xnv[:, :, i * P:(i + 1) * P],
                           in_=fm_t[j].rearrange("p (c t) -> p c t", t=P)).then_inc(s_st.inc(16), 16)
        mk[i]["fm_store"] = s_st.n

        if quant:
            @blk.vector
            def _(vector, i=i, j=j):
                if i >= 2:
                    vector.wait_ge(s_pe.h, mk[i - 2]["xq_tr"])
                li = emit_fp4_quant(vector, xn_f[j], xq_t[j], fp4tmp, D // 16)
                li.then_inc(s_dv.inc(FP4_TMP_OPS), FP4_TMP_OPS)
            mk[i]["quant"] = s_dv.n

            @blk.tensor
            def _(tensor, i=i, j=j):
                tensor.wait_ge(s_dv.h, mk[i]["quant"])
                if i >= 2:
                    tensor.wait_ge(s_dv.h, mk[i - 2]["xq_evac"])
                pv = b.psv(4 + j, 1, BF16)
                for c in range(DC):
                    li = tensor.transpose(pv[:, c * P:(c + 1) * P], xq_t[j][:, c * P:(c + 1) * P],
                                          cm["ident_bf"])
                li.then_inc(s_pe.inc(), 1)
            mk[i]["xq_tr"] = s_pe.n

            @blk.vector
            def _(vector, i=i, j=j):
                vector.wait_ge(s_pe.h, mk[i]["xq_tr"])
                if i >= 2:
                    vector.wait_ge(s_st.h, mk[i - 2]["xq_store"])
                pv = b.psv(4 + j, 1, BF16)
                vector.tensor_copy(out=xq_t2[j], in_=pv).then_inc(s_dv.inc(), 1)
            mk[i]["xq_evac"] = s_dv.n

            @blk.sync
            def _(sync, i=i, j=j):
                sync.wait_ge(s_dv.h, mk[i]["xq_evac"])
                sync.dma_start(out=xqv[:, :, i * P:(i + 1) * P],
                               in_=xq_t2[j].rearrange("p (c t) -> p c t", t=P)).then_inc(s_st.inc(16), 16)
            mk[i]["xq_store"] = s_st.n

    barrier(blk, sems)
    ar.pop()


def emit_matmul_chain(tensor, b, bank, lhsT_sb, rhs_sb, nk, extra=None, n=512):
    pv = b.psv(bank, 1, F32, [P, n]) if n <= 512 else None
    for k in range(nk):
        li = tensor.matmul(pv, lhsT_sb(k), rhs_sb(k), start=(k == 0),
                           stop=(k == nk - 1 and not extra))
    if extra:
        for idx, (l_ap, r_ap) in enumerate(extra):
            li = tensor.matmul(pv, l_ap, r_ap, start=False, stop=(idx == len(extra) - 1))
    return li


def emit_attn0(b, blk, cm, xn_d, wdram, out0_fm):
    nc, ar, sems = b.nc, b.ar, b.sems
    s_ld, s_gp, s_pe, s_dv, s_ac, s_st = (sems["ld"], sems["gp"], sems["pe"],
                                          sems["dv"], sems["ac"], sems["st"])
    ar.push()
    wnames = ["qkwT", "vwT", "wqT", "wkT", "wvT", "owT"]
    wsb = {n: ar.alloc([P, DC * D], BF16, "w_" + n) for n in wnames}
    xn_t = ar.alloc([P, DC, 512], BF16, "a2_xn")
    qkin = ar.alloc([P, DC * 512], BF16, "a2_qkin")
    vin = ar.alloc([P, DC * 512], BF16, "a2_vin")
    q_t = ar.alloc([P, H0 * 512], BF16, "a2_q")
    k_t = ar.alloc([P, H0 * 512], BF16, "a2_k")
    vtok = ar.alloc([P, 4 * D], BF16, "a2_vtok")
    attnO = qkin
    P_sb = vin[:, 0:512]
    Pn_sb = vin[:, 512:1024]
    bcb = vin[:, 1024:1536]
    linv = vin[0:1, 2048:3072].bitcast(F32)

    gp0 = s_gp.n

    @blk.gpsimd
    def _(gpsimd):
        for n in wnames:
            src = fm_view(wdram[n])
            gpsimd.dma_start(out=wsb[n].rearrange("p (k o) -> p k o", o=D),
                             in_=src).then_inc(s_gp.inc(16), 16)

    xnv = fm_view(xn_d)
    scale0 = 1.0 / math.sqrt(P)

    for g in range(4):
        ld_g = s_ld.n

        @blk.sync
        def _(sync, g=g):
            if g >= 1:
                sync.wait_ge(s_dv.h, s_dv.n)
            sync.dma_start(out=xn_t, in_=xnv[:, :, g * 512:(g + 1) * 512]).then_inc(s_ld.inc(16), 16)

        def lhs_w(name, c):
            return lambda k, name=name, c=c: wsb[name][:, k * D + c * P: k * D + (c + 1) * P]

        @blk.tensor
        def _(tensor, g=g, ld_g=ld_g):
            tensor.wait_ge(s_ld.h, ld_g + 16)
            tensor.wait_ge(s_gp.h, gp0 + 6 * 16)
            for ci, (wname, bname) in enumerate([("qkwT", "qk_b"), ("vwT", "v_b")]):
                for c in range(DC):
                    bank = (ci * DC + c) % 2
                    if ci * DC + c >= 2:
                        tensor.wait_ge(s_ac.h, s_ac.n - 1)
                    emit_matmul_chain(tensor, b, bank, lhs_w(wname, c),
                                      lambda k: xn_t[:, k, :], DC,
                                      extra=[(cm[bname][:, c * P:(c + 1) * P], cm["ones_row_f32"])]
                                      ).then_inc(s_pe.inc(), 1)
                    s_ac.n += 1
            s_ac.n -= 16

        @blk.scalar
        def _(scalar, g=g):
            for ci, dst in enumerate([qkin, vin]):
                for c in range(DC):
                    bank = (ci * DC + c) % 2
                    scalar.wait_ge(s_pe.h, s_pe.n - 16 + ci * DC + c + 1)
                    scalar.activation(out=dst[:, c * 512:(c + 1) * 512],
                                      in_=b.psv(bank, 1, F32),
                                      func=AF.Copy).then_inc(s_ac.inc(), 1)

        barrier(blk, sems)
        @blk.tensor
        def _(tensor, g=g):
            for ci, (wname, bname) in enumerate([("wqT", "q_b"), ("wkT", "k_b")]):
                for c in range(DC):
                    bank = (ci * DC + c) % 2
                    if ci * DC + c >= 2:
                        tensor.wait_ge(s_ac.h, s_ac.n - 1)
                    emit_matmul_chain(tensor, b, bank, lhs_w(wname, c),
                                      lambda k: qkin[:, k * 512:(k + 1) * 512], DC,
                                      extra=[(cm[bname][:, c * P:(c + 1) * P], cm["ones_row_f32"])]
                                      ).then_inc(s_pe.inc(), 1)
                    s_ac.n += 1
            s_ac.n -= 16

        @blk.scalar
        def _(scalar, g=g):
            for ci, dst in enumerate([q_t, k_t]):
                for c in range(DC):
                    bank = (ci * DC + c) % 2
                    scalar.wait_ge(s_pe.h, s_pe.n - 16 + ci * DC + c + 1)
                    scalar.activation(out=dst[:, c * 512:(c + 1) * 512],
                                      in_=b.psv(bank, 1, F32),
                                      func=AF.Copy).then_inc(s_ac.inc(), 1)

        @blk.tensor
        def _(tensor, g=g):
            for sb in range(4):
                for half in range(2):
                    u = sb * 2 + half
                    bank = 2 + u % 2
                    if u >= 2:
                        tensor.wait_ge(s_dv.h, s_dv.n - 1)
                    pv = b.psv(bank, 1, F32)
                    for k in range(DC):
                        tensor.matmul(pv, vin[:, k * 512 + sb * P: k * 512 + (sb + 1) * P],
                                      wsb["wvT"][:, k * D + half * 512:k * D + (half + 1) * 512],
                                      start=(k == 0), stop=False)
                    tensor.matmul(pv, cm["ones_col_f32"],
                                  cm["bv_row"][:, half * 512:(half + 1) * 512],
                                  start=False, stop=True).then_inc(s_pe.inc(), 1)
                    s_dv.n += 1
            s_dv.n -= 8

        @blk.vector
        def _(vector, g=g):
            for sb in range(4):
                for half in range(2):
                    u = sb * 2 + half
                    bank = 2 + u % 2
                    vector.wait_ge(s_pe.h, s_pe.n - 8 + u + 1)
                    vector.tensor_copy(out=vtok[:, sb * D + half * 512: sb * D + (half + 1) * 512],
                                       in_=b.psv(bank, 1, F32)).then_inc(s_dv.inc(), 1)

        barrier(blk, sems)

        for h in range(H0):
            pe_h = s_pe.n

            @blk.tensor
            def _(tensor, g=g, h=h):
                if h >= 1:
                    tensor.wait_ge(s_dv.h, s_dv.n)
                Sv = b.psv(4, 1, F32)
                for sb in range(4):
                    tensor.matmul(Sv[:, sb * P:(sb + 1) * P],
                                  k_t[:, h * 512 + sb * P: h * 512 + (sb + 1) * P],
                                  q_t[:, h * 512 + sb * P: h * 512 + (sb + 1) * P],
                                  start=(sb == 0), stop=False)
                    li = tensor.matmul(Sv[:, sb * P:(sb + 1) * P], cm["negCI_bf"], cm["maskA_bf"],
                                       start=False, stop=(sb == 3))
                li.then_inc(s_pe.inc(), 1)

            @blk.scalar
            def _(scalar, g=g, h=h, pe_h=pe_h):
                scalar.wait_ge(s_pe.h, pe_h + 1)
                scalar.activation(out=P_sb, in_=b.psv(4, 1, F32), func=AF.Exp,
                                  scale=scale0).then_inc(s_ac.inc(), 1)

            ac_h = s_ac.n

            @blk.tensor
            def _(tensor, g=g, h=h, ac_h=ac_h):
                tensor.wait_ge(s_ac.h, ac_h)
                lv = b.psv(5, 1, F32, [1, 512])
                for sb in range(4):
                    li = tensor.matmul(lv[:, sb * P:(sb + 1) * P], cm["ones_col_bf"],
                                       P_sb[:, sb * P:(sb + 1) * P], start=(sb == 0),
                                       stop=(sb == 3))
                li.then_inc(s_pe.inc(), 1)

            pe_l = s_pe.n

            @blk.vector
            def _(vector, g=g, h=h, pe_l=pe_l):
                vector.wait_ge(s_pe.h, pe_l)
                vector.reciprocal(out=linv, in_=b.psv(5, 1, F32, [1, 512])).then_inc(s_dv.inc(), 1)

            dv_l = s_dv.n

            @blk.tensor
            def _(tensor, g=g, h=h, dv_l=dv_l):
                tensor.wait_ge(s_dv.h, dv_l)
                tensor.matmul(b.psv(6, 1, F32), cm["ones_col_f32"], linv,
                              start=True, stop=True).then_inc(s_pe.inc(), 1)

            pe_bc = s_pe.n

            @blk.scalar
            def _(scalar, g=g, h=h, pe_bc=pe_bc):
                scalar.wait_ge(s_pe.h, pe_bc)
                scalar.activation(out=bcb, in_=b.psv(6, 1, F32), func=AF.Copy).then_inc(s_ac.inc(), 1)

            ac_bc = s_ac.n

            @blk.vector
            def _(vector, g=g, h=h, ac_bc=ac_bc):
                vector.wait_ge(s_ac.h, ac_bc)
                vector.tensor_tensor(out=Pn_sb, in0=P_sb, in1=bcb, op=ALU.mult).then_inc(s_dv.inc(), 1)

            dv_pn = s_dv.n

            @blk.tensor
            def _(tensor, g=g, h=h, dv_pn=dv_pn):
                tensor.wait_ge(s_dv.h, dv_pn)
                av = b.psv(7, 1, F32)
                for sb in range(4):
                    li = tensor.matmul(av[:, sb * P:(sb + 1) * P],
                                       vtok[:, sb * D + h * P: sb * D + (h + 1) * P],
                                       Pn_sb[:, sb * P:(sb + 1) * P], start=(sb == 0),
                                       stop=(sb == 3))
                li.then_inc(s_pe.inc(), 1)

            pe_av = s_pe.n

            @blk.vector
            def _(vector, g=g, h=h, pe_av=pe_av):
                vector.wait_ge(s_pe.h, pe_av)
                vector.tensor_copy(out=attnO[:, h * 512:(h + 1) * 512],
                                   in_=b.psv(7, 1, F32)).then_inc(s_dv.inc(), 1)

        barrier(blk, sems)

        @blk.tensor
        def _(tensor, g=g):
            for c in range(DC):
                bank = c % 2
                if c >= 2:
                    tensor.wait_ge(s_ac.h, s_ac.n - 1)
                emit_matmul_chain(tensor, b, bank, lhs_w("owT", c),
                                  lambda k: attnO[:, k * 512:(k + 1) * 512], DC,
                                  extra=[(cm["ob_b"][:, c * P:(c + 1) * P], cm["ones_row_f32"])]
                                  ).then_inc(s_pe.inc(), 1)
                s_ac.n += 1
            s_ac.n -= 8

        @blk.scalar
        def _(scalar, g=g):
            for c in range(DC):
                bank = c % 2
                scalar.wait_ge(s_pe.h, s_pe.n - 8 + c + 1)
                scalar.activation(
                    out=out0_fm.rearrange("p (c t) -> p c t", t=T_CORE)[:, c, g * 512:(g + 1) * 512],
                    in_=b.psv(bank, 1, F32),
                    func=AF.Copy).then_inc(s_ac.inc(), 1)

        barrier(blk, sems)

    ar.pop()


def emit_ffn_combine(b, blk, cm, stage, xn_src, w_dram, x_res_d, out_res_d, other_fm,
                     ds_d, xq_fm=None, fp4tmp=None):
    nc, ar, sems = b.nc, b.ar, b.sems
    s_ld, s_gp, s_pe, s_dv, s_ac, s_st = (sems["ld"], sems["gp"], sems["pe"],
                                          sems["dv"], sems["ac"], sems["st"])
    HID = FFN0_H if stage == "A" else FFN1_H
    HH = HID // 2
    HC = HH // P
    ar.push()
    wg_sb = ar.alloc([P, DC * HH], BF16, "f_wg")
    wu_sb = ar.alloc([P, DC * HH], BF16, "f_wu")
    wd_sb = ar.alloc([P, HC * D], BF16, "f_wd")
    if stage == "A":
        stage_i = ar.alloc([P, HH], F32, "f_stg")
        g_sb = ar.alloc([P, HC * 512], BF16, "f_g")
        u_sb = ar.alloc([P, HC * 512], BF16, "f_u")
        h_fm = ar.alloc([P, HC * 512], BF16, "f_h")
        xn_t = ar.alloc([P, DC, 512], BF16, "f_xn")
    else:
        xq_t = ar.alloc([P, DC, 512], BF16, "f_xqt")
        g1 = ar.alloc([P, HH], F32, "f_gt")
        u1 = ar.alloc([P, HH], F32, "f_ut")
        h1 = ar.alloc([P, HH], F32, "f_ht")
        hq1 = ar.alloc([P, HH], BF16, "f_hqt")
        g_t = [g1, g1]; u_t = [u1, u1]; h_t = [h1, h1]; hq_t = [hq1, hq1]
        h_fm = ar.alloc([P, HC * 512], BF16, "f_hfm")
        oth_t = ar.alloc([P, DC * 256], F32, "f_oth")
    ds_t = ar.alloc([P, DC * 256], F32, "f_ds")
    sfm = ar.alloc([P, DC * 256], F32, "f_sfm")
    sfm2 = ds_t
    x0_t = sfm[:, 0:D]
    x1_t = sfm[:, D:2 * D]

    xnv = fm_view(xn_src) if stage == "A" else None
    dsv = fm_view(ds_d)
    othv = fm_view(other_fm) if stage == "B" else None
    x0v = x_res_d.rearrange("(n p) d -> n p d", p=P)
    o0v = out_res_d.rearrange("(n p) d -> n p d", p=P)
    xqv_d = fm_view(xq_fm) if stage == "B" else None

    for half in range(2):
        if stage == "A":
            for wname, wsb_, nk, wid in (("wg", wg_sb, DC, HH), ("wu", wu_sb, DC, HH),
                                         ("wd", wd_sb, HC, D)):
                for k in range(nk):
                    gp_k = s_gp.n

                    @blk.gpsimd
                    def _(gpsimd, half=half, k=k, wname=wname, wid=wid):
                        gpsimd.wait_ge(s_dv.h, s_dv.n)
                        if wname == "wd":
                            srcv = w_dram["wd"][half * HH + k * P: half * HH + (k + 1) * P, :]
                        else:
                            srcv = w_dram[wname][k * P:(k + 1) * P, half * HH:(half + 1) * HH]
                        gpsimd.dma_start(out=stage_i[:, 0:wid], in_=srcv).then_inc(s_gp.inc(16), 16)

                    @blk.vector
                    def _(vector, k=k, gp_k=gp_k, wsb_=wsb_, wid=wid):
                        vector.wait_ge(s_gp.h, gp_k + 16)
                        vector.tensor_scalar(out=wsb_[:, k * wid:(k + 1) * wid],
                                             in0=stage_i[:, 0:wid],
                                             scalar1=1.0 / 127.0, scalar2=None,
                                             op0=ALU.mult).then_inc(s_dv.inc(), 1)
        else:
            @blk.gpsimd
            def _(gpsimd, half=half):
                gpsimd.dma_start(
                    out=wg_sb.rearrange("p (k o) -> p k o", o=HH),
                    in_=fm_view(w_dram["wg"])[:, :, half * HH:(half + 1) * HH]
                ).then_inc(s_gp.inc(16), 16)
                gpsimd.dma_start(
                    out=wu_sb.rearrange("p (k o) -> p k o", o=HH),
                    in_=fm_view(w_dram["wu"])[:, :, half * HH:(half + 1) * HH]
                ).then_inc(s_gp.inc(16), 16)
                gpsimd.dma_start(
                    out=wd_sb.rearrange("p (k o) -> p k o", o=D),
                    in_=fm_view(w_dram["wd"])[:, half * HC:(half + 1) * HC, :]
                ).then_inc(s_gp.inc(16), 16)

        barrier(blk, sems)

        for tt in range(4):
            if stage == "A":
                ld_t = s_ld.n

                @blk.sync
                def _(sync, half=half, tt=tt):
                    sync.wait_ge(s_pe.h, s_pe.n)
                    sync.dma_start(out=xn_t, in_=xnv[:, :, tt * 512:(tt + 1) * 512]).then_inc(s_ld.inc(16), 16)

                @blk.tensor
                def _(tensor, half=half, tt=tt, ld_t=ld_t):
                    tensor.wait_ge(s_ld.h, ld_t + 16)
                    tensor.wait_ge(s_dv.h, s_dv.n)
                    for ci, wsb_ in enumerate([wg_sb, wu_sb]):
                        for hc in range(HC):
                            u = ci * HC + hc
                            bank = u % 2
                            if u >= 2:
                                tensor.wait_ge(s_ac.h, s_ac.n - 1)
                            pv = b.psv(bank, 1, F32)
                            for k in range(DC):
                                li = tensor.matmul(pv, wsb_[:, k * HH + hc * P: k * HH + (hc + 1) * P],
                                                   xn_t[:, k, :], start=(k == 0), stop=(k == DC - 1))
                            li.then_inc(s_pe.inc(), 1)
                            s_ac.n += 1
                    s_ac.n -= 2 * HC

                @blk.scalar
                def _(scalar, half=half, tt=tt):
                    for ci, (dst, fn) in enumerate([(g_sb, AF.Silu), (u_sb, AF.Copy)]):
                        for hc in range(HC):
                            u = ci * HC + hc
                            bank = u % 2
                            scalar.wait_ge(s_pe.h, s_pe.n - 2 * HC + u + 1)
                            scalar.activation(out=dst[:, hc * 512:(hc + 1) * 512],
                                              in_=b.psv(bank, 1, F32), func=fn).then_inc(s_ac.inc(), 1)

                ac_gu = s_ac.n

                @blk.vector
                def _(vector, half=half, tt=tt, ac_gu=ac_gu):
                    vector.wait_ge(s_ac.h, ac_gu)
                    vector.wait_ge(s_pe.h, s_pe.n)
                    vector.tensor_tensor(out=h_fm, in0=g_sb, in1=u_sb, op=ALU.mult).then_inc(s_dv.inc(), 1)
            else:
                ld_q = s_ld.n

                @blk.sync
                def _(sync, tt=tt):
                    sync.wait_ge(s_pe.h, s_pe.n)
                    sync.dma_start(out=xq_t, in_=xqv_d[:, :, tt * 512:(tt + 1) * 512]).then_inc(s_ld.inc(16), 16)

                dv_marks = {}
                for ts in range(4):
                    t0 = tt * 512 + ts * P
                    j = ts % 2

                    @blk.tensor
                    def _(tensor, half=half, tt=tt, ts=ts, t0=t0, ld_q=ld_q):
                        tensor.wait_ge(s_ld.h, ld_q + 16)
                        tensor.wait_ge(s_ac.h, s_ac.n)
                        if ts == 0 and not (half == 0 and tt == 0):
                            tensor.wait_ge(s_dv.h, s_dv.n)
                        for ci in range(2):
                            pv = b.psall[:, ci * 1536:ci * 1536 + HH]
                            for ht in range(3):
                                n = min(512, HH - ht * 512)
                                for k in range(DC):
                                    li = tensor.matmul(pv[:, ht * 512:ht * 512 + n],
                                                       xq_t[:, k, ts * P:(ts + 1) * P],
                                                       (wg_sb if ci == 0 else wu_sb)[:, k * HH + ht * 512: k * HH + ht * 512 + n],
                                                       start=(k == 0), stop=(k == DC - 1))
                        li.then_inc(s_pe.inc(), 1)

                    pe_gu = s_pe.n

                    @blk.scalar
                    def _(scalar, half=half, tt=tt, ts=ts, j=j, pe_gu=pe_gu):
                        scalar.wait_ge(s_pe.h, pe_gu)
                        if (half, tt, ts) in dv_marks:
                            pass
                        mk = dv_marks.get(("h_tt", ts - 1))
                        if mk is not None:
                            scalar.wait_ge(s_dv.h, mk)
                        scalar.activation(out=g_t[j], in_=b.psall[:, 0:HH], func=AF.Silu).then_inc(s_ac.inc(), 1)
                        scalar.activation(out=u_t[j], in_=b.psall[:, 1536:1536 + HH],
                                          func=AF.Copy).then_inc(s_ac.inc(), 1)

                    ac_gu = s_ac.n

                    @blk.vector
                    def _(vector, half=half, tt=tt, ts=ts, j=j, ac_gu=ac_gu):
                        vector.wait_ge(s_ac.h, ac_gu)
                        vector.tensor_tensor(out=h_t[j], in0=g_t[j], in1=u_t[j], op=ALU.mult).then_inc(s_dv.inc(), 1)
                        dv_marks[("h_tt", ts)] = s_dv.n
                        mk_tr = dv_marks.get(("pe_tr", ts - 1))
                        if mk_tr is not None:
                            vector.wait_ge(s_pe.h, mk_tr)
                        li = emit_fp4_quant(vector, h_t[j], hq_t[j], fp4tmp, HH // 16)
                        li.then_inc(s_dv.inc(FP4_TMP_OPS), FP4_TMP_OPS)

                    dv_q = s_dv.n
                    dv_marks[("gu", ts)] = dv_q

                    @blk.tensor
                    def _(tensor, half=half, tt=tt, ts=ts, j=j, dv_q=dv_q):
                        tensor.wait_ge(s_dv.h, dv_q)
                        if not (half == 0 and tt == 0 and ts == 0):
                            tensor.wait_ge(s_dv.h, s_dv.n)
                        pv = b.psv(6, 2, BF16)
                        for k in range(HC):
                            li = tensor.transpose(pv[:, k * P:(k + 1) * P],
                                                  hq_t[j][:, k * P:(k + 1) * P], cm["ident_bf"])
                        li.then_inc(s_pe.inc(), 1)

                    pe_tr = s_pe.n
                    dv_marks[("pe_tr", ts)] = pe_tr

                    @blk.vector
                    def _(vector, half=half, tt=tt, ts=ts, pe_tr=pe_tr):
                        vector.wait_ge(s_pe.h, pe_tr)
                        pv = b.psv(6, 2, BF16)
                        vector.tensor_copy(
                            out=h_fm.rearrange("p (k t) -> p k t", t=512)[:, :, ts * P:(ts + 1) * P],
                            in_=pv.rearrange("p (k t) -> p k t", t=P)[:, 0:HC, :]).then_inc(s_dv.inc(), 1)

            dv_h = s_dv.n

            for ttq in range(2):
                @blk.tensor
                def _(tensor, half=half, tt=tt, ttq=ttq, dv_h=dv_h):
                    tensor.wait_ge(s_dv.h, dv_h)
                    if not (half == 0 and tt == 0 and ttq == 0):
                        tensor.wait_ge(s_dv.h, s_dv.n)
                        tensor.wait_ge(s_ac.h, s_ac.n)
                    dbank = 2 if stage == "A" else 0
                    dpv = b.psv(dbank, 4, F32)
                    for c in range(DC):
                        for k in range(HC):
                            li = tensor.matmul(
                                dpv[:, c * 256:(c + 1) * 256],
                                wd_sb[:, k * D + c * P: k * D + (c + 1) * P],
                                h_fm[:, k * 512 + ttq * 256: k * 512 + ttq * 256 + 256],
                                start=(k == 0 and c % 2 == 0),
                                stop=(k == HC - 1 and c == DC - 1))
                    li.then_inc(s_pe.inc(), 1)

                pe_d = s_pe.n
                toff = tt * 512 + ttq * 256

                if half == 0:
                    @blk.scalar
                    def _(scalar, tt=tt, ttq=ttq, pe_d=pe_d):
                        scalar.wait_ge(s_pe.h, pe_d)
                        scalar.wait_ge(s_st.h, s_st.n)
                        scalar.activation(out=ds_t, in_=b.psv(2 if stage == "A" else 0, 4, F32),
                                          func=AF.Copy).then_inc(s_ac.inc(), 1)

                    ac_d = s_ac.n

                    @blk.sync
                    def _(sync, ac_d=ac_d, toff=toff):
                        sync.wait_ge(s_ac.h, ac_d)
                        sync.dma_start(
                            out=dsv[:, :, toff:toff + 256],
                            in_=ds_t.rearrange("p (c t) -> p c t", t=256)).then_inc(s_st.inc(16), 16)


                else:
                    @blk.sync
                    def _(sync, toff=toff, stage=stage):
                        sync.wait_ge(s_dv.h, s_dv.n)
                        sync.dma_start(out=ds_t,
                                       in_=dsv[:, :, toff:toff + 256]).then_inc(s_ld.inc(16), 16)
                        if stage == "B":
                            sync.dma_start(out=oth_t,
                                           in_=othv[:, :, toff:toff + 256]).then_inc(s_ld.inc(16), 16)

                    ld_c2 = s_ld.n

                    @blk.vector
                    def _(vector, half=half, tt=tt, ttq=ttq, pe_d=pe_d, ld_c2=ld_c2, toff=toff):
                        vector.wait_ge(s_pe.h, pe_d)
                        vector.wait_ge(s_ld.h, ld_c2)
                        vector.tensor_tensor(out=sfm, in0=b.psv(2 if stage == "A" else 0, 4, F32),
                                             in1=ds_t, op=ALU.add).then_inc(s_dv.inc(), 1)
                        if stage == "A":
                            o3 = other_fm.rearrange("p (c t) -> p c t", t=T_CORE)[:, :, toff:toff + 256]
                        else:
                            o3 = oth_t.rearrange("p (c t) -> p c t", t=256)
                        vector.tensor_tensor(out=sfm2.rearrange("p (c t) -> p c t", t=256),
                                             in0=sfm.rearrange("p (c t) -> p c t", t=256),
                                             in1=o3, op=ALU.add).then_inc(s_dv.inc(), 1)

                    dv_s = s_dv.n

                    for tk in range(2):
                        ti = toff // P + tk

                        @blk.tensor
                        def _(tensor, tk=tk, dv_s=dv_s, half=half, tt=tt, ttq=ttq, stage=stage):
                            tensor.wait_ge(s_dv.h, dv_s)
                            first = (tt == 0 and ttq == 0 and tk == 0)
                            if not first:
                                tensor.wait_ge(s_dv.h, s_dv.n)
                            pv = b.psv((0 if stage == "A" else 4) if tk == 0 else 6, 2, F32)
                            for c in range(DC):
                                li = tensor.transpose(
                                    pv[:, c * P:(c + 1) * P],
                                    sfm2.rearrange("p (c t) -> p c t", t=256)[:, c, tk * P:(tk + 1) * P],
                                    cm["ident_f32"])
                            li.then_inc(s_pe.inc(), 1)

                        pe_tr = s_pe.n

                        @blk.sync
                        def _(sync, ti=ti):
                            sync.wait_ge(s_dv.h, s_dv.n)
                            sync.dma_start(out=x0_t, in_=x0v[ti]).then_inc(s_ld.inc(16), 16)

                        ld_x2 = s_ld.n

                        @blk.vector
                        def _(vector, tk=tk, pe_tr=pe_tr, ld_x2=ld_x2):
                            vector.wait_ge(s_pe.h, pe_tr)
                            vector.wait_ge(s_ld.h, ld_x2)
                            vector.wait_ge(s_st.h, s_st.n)
                            vector.tensor_tensor(out=x1_t,
                                                 in0=b.psv((0 if stage == "A" else 4) if tk == 0 else 6, 2, F32),
                                                 in1=x0_t, op=ALU.add).then_inc(s_dv.inc(), 1)

                        dv_x1 = s_dv.n

                        @blk.sync
                        def _(sync, ti=ti, dv_x1=dv_x1):
                            sync.wait_ge(s_dv.h, dv_x1)
                            sync.dma_start(out=o0v[ti], in_=x1_t).then_inc(s_st.inc(16), 16)

        barrier(blk, sems)
    ar.pop()


def emit_fp4_weight_quant(b, blk, cm, w_in, wT_out, rows, cols, fp4tmp):
    nc, ar, sems = b.nc, b.ar, b.sems
    s_ld, s_pe, s_dv, s_ac, s_st = (sems["ld"], sems["pe"], sems["dv"], sems["ac"], sems["st"])
    RC = rows // P
    CW = 1408 if cols % 1408 == 0 else cols
    NSUB = cols // CW
    ar.push()
    wstg = [ar.alloc([P, cols], F32, "wq_s0"), ar.alloc([P, cols], F32, "wq_s1")]
    wqt = ar.alloc([P, cols], BF16, "wq_q")
    wqfm = [ar.alloc([P, cols], BF16, "wq_f0"), ar.alloc([P, cols], BF16, "wq_f1")]
    dst = wT_out.rearrange("(j p) o -> p j o", p=P)

    for r in range(RC):
        j = r % 2
        ld_r = s_ld.n

        @blk.sync
        def _(sync, r=r, j=j):
            if r >= 2:
                sync.wait_ge(s_dv.h, s_dv.n)
            sync.dma_start(out=wstg[j], in_=w_in[r * P:(r + 1) * P, :]).then_inc(s_ld.inc(16), 16)

        @blk.vector
        def _(vector, r=r, j=j, ld_r=ld_r):
            vector.wait_ge(s_ld.h, ld_r + 16)
            if r >= 1:
                vector.wait_ge(s_pe.h, s_pe.n)
            for s in range(NSUB):
                li = emit_fp4_quant(vector, wstg[j][:, s * CW:(s + 1) * CW],
                                    wqt[:, s * CW:(s + 1) * CW], fp4tmp, CW // 16)
            li.then_inc(s_dv.inc(NSUB * FP4_TMP_OPS), NSUB * FP4_TMP_OPS)

        dv_r = s_dv.n

        @blk.tensor
        def _(tensor, r=r, j=j, dv_r=dv_r):
            tensor.wait_ge(s_dv.h, dv_r)
            if r >= 2:
                tensor.wait_ge(s_dv.h, s_dv.n)
            pv = b.psv((r % 2) * 3, 3, BF16)
            for c in range(cols // P):
                li = tensor.transpose(pv[:, c * P:(c + 1) * P], wqt[:, c * P:(c + 1) * P],
                                      cm["ident_bf"])
            li.then_inc(s_pe.inc(), 1)

        pe_r = s_pe.n

        @blk.vector
        def _(vector, r=r, j=j, pe_r=pe_r):
            vector.wait_ge(s_pe.h, pe_r)
            if r >= 2:
                vector.wait_ge(s_st.h, s_st.n)
            pv = b.psv((r % 2) * 3, 3, BF16)
            vector.tensor_copy(out=wqfm[j], in_=pv[:, 0:cols]).then_inc(s_dv.inc(), 1)

        dv_f = s_dv.n

        @blk.sync
        def _(sync, r=r, j=j, dv_f=dv_f):
            sync.wait_ge(s_dv.h, dv_f)
            sync.dma_start(out=dst[:, :, r * P:(r + 1) * P],
                           in_=wqfm[j].rearrange("p (c t) -> p c t", t=P)).then_inc(s_st.inc(16), 16)

    barrier(blk, sems)
    ar.pop()


def emit_qkv1(b, blk, cm, ag_out_d, w_sb_map, q_sb, k_sb, vt_sb):
    nc, ar, sems = b.nc, b.ar, b.sems
    s_ld, s_gp, s_pe, s_dv, s_ac, s_st = (sems["ld"], sems["gp"], sems["pe"],
                                          sems["dv"], sems["ac"], sems["st"])
    ar.push()
    xg_t = [ar.alloc([P, DC, 512], BF16, "qk_xg0"), ar.alloc([P, DC, 512], BF16, "qk_xg1")]
    agv = ag_out_d.rearrange("(r c p) t -> p r c t", p=P, c=DC)

    @blk.vector
    def _(vector):
        ov = vt_sb.rearrange("p (n g e) -> p n g e", g=HPC, e=65)
        vector.memset(ov[:, :, :, 64:65], 1.0).then_inc(s_dv.inc(), 1)

    HD2 = HPC * DH1

    for tt in range(SEQ // 512):
        j = tt % 2
        rnk = (tt * 512) // T_CORE
        toff = (tt * 512) % T_CORE
        ld_t = s_ld.n

        @blk.sync
        def _(sync, tt=tt, j=j, rnk=rnk, toff=toff):
            if tt >= 2:
                sync.wait_ge(s_pe.h, s_pe.n)
            sync.dma_start(out=xg_t[j], in_=agv[:, rnk, :, toff:toff + 512]).then_inc(s_ld.inc(16), 16)

        @blk.tensor
        def _(tensor, tt=tt, j=j, ld_t=ld_t):
            tensor.wait_ge(s_ld.h, ld_t + 16)
            for ci, wname in enumerate(("wq1", "wk1")):
                for c in range(2):
                    u = ci * 2 + c
                    bank = u % 2
                    if not (tt == 0 and u < 2):
                        tensor.wait_ge(s_ac.h, s_ac.n - 1)
                    pv = b.psv(bank, 1, F32)
                    for k in range(DC):
                        li = tensor.matmul(pv, w_sb_map[wname][:, k * HD2 + c * P: k * HD2 + (c + 1) * P],
                                           xg_t[j][:, k, :], start=(k == 0), stop=(k == DC - 1))
                    li.then_inc(s_pe.inc(), 1)
                    s_ac.n += 1
            s_ac.n -= 4

        @blk.scalar
        def _(scalar, tt=tt):
            for ci, dst in enumerate((q_sb, k_sb)):
                for c in range(2):
                    u = ci * 2 + c
                    bank = u % 2
                    scalar.wait_ge(s_pe.h, s_pe.n - 4 + u + 1)
                    scale = 1.0 / 8.0 if ci == 0 else 1.0
                    scalar.activation(out=dst[:, c * SEQ + tt * 512: c * SEQ + tt * 512 + 512],
                                      in_=b.psv(bank, 1, F32), func=AF.Copy,
                                      scale=scale).then_inc(s_ac.inc(), 1)

        @blk.tensor
        def _(tensor, tt=tt, j=j):
            for sb in range(4):
                bank = 4 + sb % 2
                if not (tt == 0 and sb < 2):
                    tensor.wait_ge(s_dv.h, s_dv.n - 1)
                pv = b.psv(bank, 1, F32, [P, HD2])
                for k in range(DC):
                    li = tensor.matmul(pv, xg_t[j][:, k, sb * P:(sb + 1) * P],
                                       w_sb_map["wv1"][:, k * HD2:(k + 1) * HD2],
                                       start=(k == 0), stop=(k == DC - 1))
                li.then_inc(s_pe.inc(), 1)
                s_dv.n += 1
            s_dv.n -= 4

        @blk.vector
        def _(vector, tt=tt):
            for sb in range(4):
                bank = 4 + sb % 2
                vector.wait_ge(s_pe.h, s_pe.n - 4 + sb + 1)
                n = tt * 4 + sb
                dst = vt_sb.rearrange("p (m g e) -> p m g e", g=HPC, e=65)[:, n, :, 0:64]
                vector.tensor_copy(out=dst,
                                   in_=b.psv(bank, 1, F32, [P, HD2]).rearrange(
                                       "p (g e) -> p g e", e=64)).then_inc(s_dv.inc(), 1)

    barrier(blk, sems)
    ar.pop()


def emit_flash1(b, blk, cm, q_sb, k_sb, vt_sb, o_sb):
    nc, ar, sems = b.nc, b.ar, b.sems
    s_pe, s_dv, s_ac = sems["pe"], sems["dv"], sems["ac"]
    ar.push()
    P_h = [[ar.alloc([P, 512], BF16, f"fl_P{h}_{z}") for z in range(2)] for h in range(2)]
    linv = ar.alloc([1, 512], F32, "fl_linv")
    O_t = ar.alloc([64, 512], F32, "fl_O")

    for pair in range(2):
        for qi in range(SEQ // 512):
            nkb = (qi + 1) * 4
            pe0, ac0 = s_pe.n, s_ac.n
            dv_prev = s_dv.n

            for kb in range(nkb):
                z = kb % 2
                diag = kb >= nkb - 4
                doff = (qi * 512) - (kb * P)

                @blk.tensor
                def _(tensor, pair=pair, qi=qi, kb=kb, z=z, diag=diag, doff=doff,
                      pe0=pe0, ac0=ac0, dv_prev=dv_prev):
                    if kb >= 2:
                        tensor.wait_ge(s_ac.h, ac0 + 2 * kb - 2)
                    if kb == 0:
                        tensor.wait_ge(s_dv.h, dv_prev)
                    for h in range(2):
                        Sv = b.psv((0 if h == 0 else 2) + z, 1, F32)
                        lh = k_sb[h * 64:(h + 1) * 64,
                                  pair * SEQ + kb * P: pair * SEQ + (kb + 1) * P]
                        rh = q_sb[h * 64:(h + 1) * 64,
                                  pair * SEQ + qi * 512: pair * SEQ + (qi + 1) * 512]
                        li = tensor.matmul(Sv, lh, rh, start=True, stop=not diag,
                                           tile_position=(h * 64, 0))
                    if diag:
                        u_sl = cm["U_big_bf"][:, doff + 384: doff + 384 + 512]
                        for h in range(2):
                            Sv = b.psv((0 if h == 0 else 2) + z, 1, F32)
                            li = tensor.matmul(Sv, cm["negCI_bf"], u_sl, start=False, stop=True)
                    li.then_inc(s_pe.inc(), 1)

                pe_qk = s_pe.n

                @blk.scalar
                def _(scalar, pair=pair, qi=qi, kb=kb, z=z, pe_qk=pe_qk):
                    scalar.wait_ge(s_pe.h, pe_qk)
                    for h in range(2):
                        scalar.activation(out=P_h[h][z], in_=b.psv((0 if h == 0 else 2) + z, 1, F32),
                                          func=AF.Exp).then_inc(s_ac.inc(), 1)

                ac_e = s_ac.n

                @blk.tensor
                def _(tensor, pair=pair, qi=qi, kb=kb, z=z, nkb=nkb, ac_e=ac_e):
                    tensor.wait_ge(s_ac.h, ac_e)
                    for h in range(2):
                        g = pair * 2 + h
                        av = b.psv(6 + h, 1, F32, [65, 512])
                        li = tensor.matmul(av, vt_sb[:, kb * 260 + g * 65: kb * 260 + (g + 1) * 65],
                                           P_h[h][z], start=(kb == 0), stop=(kb == nkb - 1))
                    li.then_inc(s_pe.inc(), 1)

            for h in range(2):
                pe_av = s_pe.n
                dv_mark = s_dv.n

                @blk.vector
                def _(vector, h=h, pe_av=pe_av):
                    vector.wait_ge(s_pe.h, pe_av)
                    av = b.psv(6 + h, 1, F32, [65, 512])
                    vector.reciprocal(out=linv, in_=av[64:65, :]).then_inc(s_dv.inc(), 1)

                dv_r = s_dv.n

                @blk.tensor
                def _(tensor, h=h, dv_r=dv_r):
                    tensor.wait_ge(s_dv.h, dv_r)
                    tensor.matmul(b.psv(4, 1, F32, [64, 512]), cm["ones_col_f32"][:, 0:64],
                                  linv, start=True, stop=True).then_inc(s_pe.inc(), 1)

                pe_bc = s_pe.n

                @blk.scalar
                def _(scalar, h=h, pe_bc=pe_bc, dv_mark=dv_mark):
                    scalar.wait_ge(s_pe.h, pe_bc)
                    scalar.wait_ge(s_dv.h, dv_mark)
                    av = b.psv(6 + h, 1, F32, [65, 512])
                    scalar.activation(out=O_t, in_=av[0:64, :], func=AF.Copy).then_inc(s_ac.inc(), 1)

                ac_o = s_ac.n

                @blk.vector
                def _(vector, pair=pair, qi=qi, h=h, ac_o=ac_o, pe_bc=pe_bc):
                    vector.wait_ge(s_ac.h, ac_o)
                    vector.wait_ge(s_pe.h, pe_bc)
                    dst = o_sb[h * 64:(h + 1) * 64,
                               pair * SEQ + qi * 512: pair * SEQ + (qi + 1) * 512]
                    vector.tensor_tensor(out=dst, in0=O_t, in1=b.psv(4, 1, F32, [64, 512]),
                                         op=ALU.mult).then_inc(s_dv.inc(), 1)

    barrier(blk, sems)
    ar.pop()


def emit_outproj1(b, blk, cm, o_sb, wo_sb, rs_in_d):
    nc, ar, sems = b.nc, b.ar, b.sems
    s_pe, s_dv, s_ac, s_st = sems["pe"], sems["dv"], sems["ac"], sems["st"]
    ar.push()
    po_t = [ar.alloc([P, 512], F32, "op_t0"), ar.alloc([P, 512], F32, "op_t1")]
    rsv = rs_in_d.rearrange("(r c p) t -> p r c t", p=P, c=DC)

    for tt in range(SEQ // 512):
        rnk = (tt * 512) // T_CORE
        toff = (tt * 512) % T_CORE
        for c in range(DC):
            u = tt * DC + c
            bank = u % 2
            jj = u % 2

            @blk.tensor
            def _(tensor, tt=tt, c=c, u=u, bank=bank):
                if u >= 2:
                    tensor.wait_ge(s_ac.h, s_ac.n - 1)
                pv = b.psv(bank, 1, F32)
                for k in range(2):
                    li = tensor.matmul(pv, wo_sb[:, k * D + c * P: k * D + (c + 1) * P],
                                       o_sb[:, k * SEQ + tt * 512:k * SEQ + (tt + 1) * 512],
                                       start=(k == 0), stop=(k == 1))
                li.then_inc(s_pe.inc(), 1)

            pe_o = s_pe.n

            @blk.scalar
            def _(scalar, u=u, bank=bank, jj=jj, pe_o=pe_o):
                scalar.wait_ge(s_pe.h, pe_o)
                if u >= 2:
                    scalar.wait_ge(s_st.h, s_st.n - 16)
                scalar.activation(out=po_t[jj], in_=b.psv(bank, 1, F32),
                                  func=AF.Copy).then_inc(s_ac.inc(), 1)

            ac_o = s_ac.n

            @blk.sync
            def _(sync, rnk=rnk, toff=toff, c=c, jj=jj, ac_o=ac_o):
                sync.wait_ge(s_ac.h, ac_o)
                sync.dma_start(out=rsv[:, rnk, c, toff:toff + 512],
                               in_=po_t[jj]).then_inc(s_st.inc(16), 16)

    barrier(blk, sems)
    ar.pop()


def build_program(stages=("A", "B")):
    nc = bass.Bass("TRN2", target_bir_lowering=False)
    b = Builder(nc)
    ar, sems = b.ar, b.sems
    s_ld, s_gp, s_cc = sems["ld"], sems["gp"], sems["cc"]

    x_in = b.din("x_slice", [T_CORE, D], F32)
    wdram = {}
    for n in ("qkwT", "vwT", "wqT", "wkT", "wvT", "owT"):
        wdram[n] = b.din(n, [D, D], F32)
    bias_in = {}
    for n in ("qk_b", "v_b", "q_b", "k_b", "ob_b", "bv_row"):
        bias_in[n] = b.din(n, [1, D], F32)
    w0 = {"wg": b.din("wg0T", [D, FFN0_H], I32),
          "wu": b.din("wu0T", [D, FFN0_H], I32),
          "wd": b.din("wd0T", [FFN0_H, D], I32)}
    wg1 = b.din("wg1", [FFN1_H, D], F32)
    wu1 = b.din("wu1", [FFN1_H, D], F32)
    wd1 = b.din("wd1", [D, FFN1_H], F32)
    wq1 = b.din("wq1", [D, HPC * DH1], F32)
    wk1 = b.din("wk1", [D, HPC * DH1], F32)
    wv1 = b.din("wv1", [D, HPC * DH1], F32)
    wo1 = b.din("wo1", [HPC * DH1, D], F32)
    cin = {}
    cdefs = [("ident_bf", [P, P], BF16), ("ident_f32", [P, P], F32),
             ("negCI_bf", [P, P], BF16), ("maskA_bf", [P, P], BF16),
             ("U_big_bf", [P, 896], BF16), ("ones_row_f32", [1, 512], F32),
             ("ones_col_f32", [1, P], F32), ("ones_col_bf", [P, 1], BF16),
             ("maskpos", [P, 1], I32), ("masksign", [P, 1], I32),
             ("eps_col", [P, 1], F32)]
    for n, sh, dt in cdefs:
        cin[n] = b.din(n, sh, dt)

    only_a = "A" in stages and "B" not in stages
    only_b = "B" in stages and "A" not in stages
    if "B" in stages:
        out_d = b.dout("out_slice", [T_CORE, D], F32)
    if only_a:
        x1_d = b.dout("x1_slice", [T_CORE, D], F32)
    elif only_b:
        x1_d = b.din("x1_slice", [T_CORE, D], F32)
    else:
        x1_d = b.dint("x1_buf", [T_CORE, D], F32)
    mirrors = []

    def dtap(name, shape, dtype):
        t = b.dint(name, shape, dtype)
        if name in DEBUG_TAPS:
            mirrors.append((b.dout(name + "_tap", shape, dtype), t))
        return t

    def emit_mirrors(blk):
        if not mirrors:
            return
        @blk.sync
        def _(sync):
            for dst, srct in mirrors:
                sync.dma_start(out=dst, in_=srct).then_inc(b.sems["st"].inc(16), 16)
        barrier(blk, b.sems)
    xn_d = dtap("xn_buf", [D, T_CORE], BF16)
    ds_d = dtap("ds_buf", [D, T_CORE], F32)
    if "A" in stages and ("xn_buf" in DEBUG_TAPS or "ds_buf" in DEBUG_TAPS):
        pass
    if "B" in stages:
        xq_d = dtap("xq_buf", [D, T_CORE], BF16)
        ag_out_d = dtap("ag_out", [4 * D, T_CORE], BF16)
        rs_in_d = dtap("rs_in", [4 * D, T_CORE], F32)
        rs_out_d = dtap("rs_out", [D, T_CORE], F32)
        w1q = {"wg": dtap("wg1T_q", [D, FFN1_H], BF16),
               "wu": dtap("wu1T_q", [D, FFN1_H], BF16),
               "wd": dtap("wd1T_q", [FFN1_H, D], BF16)}

    cm = {n: ar.alloc(sh, dt, "c_" + n) for n, sh, dt in cdefs}
    for n in ("qk_b", "v_b", "q_b", "k_b", "ob_b", "bv_row"):
        cm[n] = ar.alloc([1, D], F32, "c_" + n)

    with nc.Block() as blk:
        @blk.sync
        def _(sync):
            for n, sh, dt in cdefs:
                sync.dma_start(out=cm[n], in_=cin[n]).then_inc(s_ld.inc(16), 16)
            for n in ("qk_b", "v_b", "q_b", "k_b", "ob_b", "bv_row"):
                sync.dma_start(out=cm[n], in_=bias_in[n]).then_inc(s_ld.inc(16), 16)

        barrier(blk, sems)

        if "A" in stages:
            emit_ln_quant(b, blk, "A", x_in, xn_d, cm, dbg_d=(b.dout("ln_dbg", [P, 8*16], F32) if "ln_dbg" in DEBUG_TAPS else None))
            ar.push()
            out0 = ar.alloc([P, DC * T_CORE], BF16, "out0")
            if "attn" in A_PHASES:
                emit_attn0(b, blk, cm, xn_d, wdram, out0)
                if "out0" in DEBUG_TAPS:
                    o0d = b.dout("out0", [P, DC * T_CORE], BF16)

                    @blk.sync
                    def _(sync):
                        sync.dma_start(out=o0d, in_=out0).then_inc(b.sems["st"].inc(16), 16)
                    barrier(blk, b.sems)
            if "ffn" in A_PHASES:
                emit_ffn_combine(b, blk, cm, "A", xn_d, w0, x_in, x1_d, out0, ds_d)
            else:
                @blk.sync
                def _(sync):
                    sync.dma_start(out=x1_d[0:1, 0:8],
                                   in_=cm["ones_row_f32"][0:1, 0:8]).then_inc(b.sems["st"].inc(16), 16)
                    sync.wait_ge(b.sems["st"].h, b.sems["st"].n)
            ar.pop()

        if "B" in stages:
            ar.push()
            fp4tmp_w = alloc_fp4_tmp(ar, 1408, cm, sems["dv"])
            emit_fp4_weight_quant(b, blk, cm, wg1, w1q["wg"], FFN1_H, D, fp4tmp_w)
            emit_fp4_weight_quant(b, blk, cm, wu1, w1q["wu"], FFN1_H, D, fp4tmp_w)
            emit_fp4_weight_quant(b, blk, cm, wd1, w1q["wd"], D, FFN1_H, fp4tmp_w)
            ar.pop()
            ar.push()
            fp4tmp_x = alloc_fp4_tmp(ar, D, cm, sems["dv"])
            emit_ln_quant(b, blk, "B", x1_d, xn_d, cm, xq_fm=xq_d, fp4tmp=fp4tmp_x)
            ar.pop()
            @blk.gpsimd
            def _(gpsimd):
                gpsimd.collective_compute(
                    "AllGather", ALU.bypass,
                    replica_groups=[[0, 1, 2, 3], [4, 5, 6, 7]],
                    ins=[xn_d], outs=[ag_out_d],
                ).then_inc(s_cc.inc(), 1)
                gpsimd.wait_ge(s_cc.h, s_cc.n)
            barrier(blk, sems)
            ar.push()
            w1sb = {}
            for n, src in (("wq1", wq1), ("wk1", wk1), ("wv1", wv1)):
                w1sb[n] = ar.alloc([P, DC * HPC * DH1], BF16, "w_" + n)
            wo_sb = ar.alloc([P, 2 * D], BF16, "w_wo1")

            @blk.gpsimd
            def _(gpsimd):
                for n, src in (("wq1", wq1), ("wk1", wk1), ("wv1", wv1)):
                    gpsimd.dma_start(out=w1sb[n].rearrange("p (k o) -> p k o", o=HPC * DH1),
                                     in_=src.rearrange("(k p) o -> p k o", p=P)).then_inc(s_gp.inc(16), 16)
                gpsimd.dma_start(out=wo_sb.rearrange("p (k o) -> p k o", o=D),
                                 in_=wo1.rearrange("(k p) o -> p k o", p=P)).then_inc(s_gp.inc(16), 16)
            barrier(blk, sems)

            q_sb = ar.alloc([P, 2 * SEQ], BF16, "q_sb")
            k_sb = ar.alloc([P, 2 * SEQ], BF16, "k_sb")
            vt_sb = ar.alloc([P, (SEQ // P) * 260], BF16, "vt_sb")
            o_sb = ar.alloc([P, 2 * SEQ], BF16, "o_sb")
            emit_qkv1(b, blk, cm, ag_out_d, w1sb, q_sb, k_sb, vt_sb)
            emit_flash1(b, blk, cm, q_sb, k_sb, vt_sb, o_sb)
            emit_outproj1(b, blk, cm, o_sb, wo_sb, rs_in_d)
            ar.pop()
            @blk.gpsimd
            def _(gpsimd):
                gpsimd.collective_compute(
                    "ReduceScatter", ALU.add,
                    replica_groups=[[0, 1, 2, 3], [4, 5, 6, 7]],
                    ins=[rs_in_d], outs=[rs_out_d],
                ).then_inc(s_cc.inc(), 1)
                gpsimd.wait_ge(s_cc.h, s_cc.n)
            barrier(blk, sems)
            ar.push()
            fp4tmp_h = alloc_fp4_tmp(ar, FFN1_H // 2, cm, sems["dv"])
            emit_ffn_combine(b, blk, cm, "B", None, w1q, x1_d, out_d, rs_out_d, ds_d,
                             xq_fm=xq_d, fp4tmp=fp4tmp_h)
            ar.pop()
        emit_mirrors(blk)

    nc._input_names = list(b.input_names)
    return nc


A_PHASES = ("attn", "ffn")
DEBUG_TAPS = ()

_PROG_CACHE = {}


def _get_prog(stages):
    key = tuple(stages)
    if key not in _PROG_CACHE:
        _PROG_CACHE[key] = build_program(stages)
    return _PROG_CACHE[key]


FUSED = True


def _run(nc, per_core):
    names = set(nc._input_names)
    maps = [{k: v for k, v in m.items() if k in names} for m in per_core]
    return run_bass_kernel_spmd(nc, maps, core_ids=list(range(NCORES)))


def kernel(**inputs):
    per_core = prep_inputs(inputs)
    if FUSED:
        nc = _get_prog(("A", "B"))
        res = _run(nc, per_core)
        out = np.concatenate([res.results[r]["out_slice"] for r in range(NCORES)], axis=0)
        return out.reshape(2, SEQ, D)
    nca = _get_prog(("A",))
    res_a = _run(nca, per_core)
    for r in range(NCORES):
        per_core[r]["x1_slice"] = np.ascontiguousarray(res_a.results[r]["x1_slice"])
    ncb = _get_prog(("B",))
    res_b = _run(ncb, per_core)
    out = np.concatenate([res_b.results[r]["out_slice"] for r in range(NCORES)], axis=0)
    return out.reshape(2, SEQ, D)


# version 03
# speedup vs baseline: 1.0338x; 1.0338x over previous
import math
import numpy as np
import ml_dtypes
import concourse.bass as bass
import concourse.mybir as mybir
from concourse.bass_utils import run_bass_kernel_spmd

F32 = mybir.dt.float32
BF16 = mybir.dt.bfloat16
I32 = mybir.dt.int32
U8 = mybir.dt.uint8
AF = mybir.ActivationFunctionType
ALU = mybir.AluOpType
AX = mybir.AxisListType

P = 128
D = 1024
DC = D // P
T_CORE = 2048
SEQ = 8192
NCORES = 8
H0 = 8
FFN0_H = 3072
FFN1_H = 2816
H1 = 16
DH1 = 64
HPC = 4
MASK_C = 1000.0

bf16 = ml_dtypes.bfloat16



def _pp(b):
    return np.ascontiguousarray(b.reshape(-1, P).T.astype(np.float32))


def _row(b):
    return np.ascontiguousarray(b.reshape(1, -1).astype(np.float32))


def make_constants():
    c = {}
    c["ident_bf"] = np.eye(P, dtype=np.float32).astype(bf16)
    c["ident_f32"] = np.eye(P, dtype=np.float32)
    c["negCI_bf"] = (-MASK_C * np.eye(P, dtype=np.float32)).astype(bf16)
    kk, qq = np.meshgrid(np.arange(P), np.arange(P), indexing="ij")
    sameseg = (kk // 4) == (qq // 4)
    a0 = (~(sameseg & (kk != qq))).astype(np.float32)
    c["maskA_bf"] = a0.astype(bf16)
    ii = np.arange(P)[:, None]
    cp = np.arange(896)[None, :]
    c["U_big_bf"] = (ii > (cp - 384)).astype(np.float32).astype(bf16)
    c["ones_row_f32"] = np.ones((1, 512), dtype=np.float32)
    c["ones_col_f32"] = np.ones((1, P), dtype=np.float32)
    c["ones_col_bf"] = np.ones((P, 1), dtype=np.float32).astype(bf16)
    c["maskpos"] = np.full((P, 1), 0x7FFFFFFF, dtype=np.int32)
    c["eps_col"] = np.full((P, 1), 1e-5, dtype=np.float32)
    c["masksign"] = np.full((P, 1), -2147483648, dtype=np.int32)
    return c


def prep_inputs(inputs):
    g = {}
    x = np.ascontiguousarray(np.asarray(inputs["L0_attn_x"], dtype=np.float32)
                             .reshape(16384, D))
    g["qkwT"] = np.ascontiguousarray(inputs["L0_attn_proj_qk_w"].T.astype(np.float32))
    g["vwT"] = np.ascontiguousarray(inputs["L0_attn_proj_v_w"].T.astype(np.float32))
    wq, wk, wv = np.split(np.asarray(inputs["L0_attn_in_proj_w"]), 3, axis=0)
    g["wqT"] = np.ascontiguousarray(wq.T.astype(np.float32))
    g["wkT"] = np.ascontiguousarray(wk.T.astype(np.float32))
    g["wvT"] = np.ascontiguousarray(wv.T.astype(np.float32))
    g["owT"] = np.ascontiguousarray(inputs["L0_attn_out_proj_w"].T.astype(np.float32))
    bq, bk, bv = np.split(np.asarray(inputs["L0_attn_in_proj_b"]), 3)
    g["qk_b"] = _row(np.asarray(inputs["L0_attn_proj_qk_b"]))
    g["v_b"] = _row(np.asarray(inputs["L0_attn_proj_v_b"]))
    g["q_b"] = _row(bq)
    g["k_b"] = _row(bk)
    g["bv_row"] = _row(bv)
    g["ob_b"] = _row(np.asarray(inputs["L0_attn_out_proj_b"]))
    g["wg0T"] = np.ascontiguousarray(np.asarray(inputs["L0_ffn_w_gate"]).T)
    g["wu0T"] = np.ascontiguousarray(np.asarray(inputs["L0_ffn_w_up"]).T)
    g["wd0T"] = np.ascontiguousarray(np.asarray(inputs["L0_ffn_w_down"]).T)
    g["wg1"] = np.ascontiguousarray(np.asarray(inputs["L1_ffn_w_gate"], dtype=np.float32))
    g["wu1"] = np.ascontiguousarray(np.asarray(inputs["L1_ffn_w_up"], dtype=np.float32))
    g["wd1"] = np.ascontiguousarray(np.asarray(inputs["L1_ffn_w_down"], dtype=np.float32))
    g.update(make_constants())

    Wq = np.asarray(inputs["L1_attn_Wq"], dtype=np.float32)
    Wk = np.asarray(inputs["L1_attn_Wk"], dtype=np.float32)
    Wv = np.asarray(inputs["L1_attn_Wv"], dtype=np.float32)
    Wo = np.asarray(inputs["L1_attn_Wout"], dtype=np.float32)

    per_core = []
    for r in range(NCORES):
        m = dict(g)
        m["x_slice"] = np.ascontiguousarray(x[r * T_CORE:(r + 1) * T_CORE])
        hs = 4 * (r % 4)
        m["wq1"] = np.ascontiguousarray(Wq[:, hs:hs + HPC, :].reshape(D, HPC * DH1))
        m["wk1"] = np.ascontiguousarray(Wk[:, hs:hs + HPC, :].reshape(D, HPC * DH1))
        m["wv1"] = np.ascontiguousarray(Wv[:, hs:hs + HPC, :].reshape(D, HPC * DH1))
        m["wo1"] = np.ascontiguousarray(Wo[hs:hs + HPC].reshape(HPC * DH1, D))
        per_core.append(m)
    return per_core



class Arena:

    def __init__(self, nc, nbytes, name="arena"):
        self.nc = nc
        self.t = nc.alloc_sbuf_tensor(name, [P, nbytes], U8)
        self.nbytes = nbytes
        self.off = 0
        self.marks = []

    def alloc(self, shape, dtype, name=None):
        esz = mybir.dt.size(dtype)
        free = 1
        for s in shape[1:]:
            free *= s
        nb = free * esz
        off = (self.off + 31) & ~31
        assert off + nb <= self.nbytes, f"arena overflow: {name} {off}+{nb}"
        self.off = off + nb
        v = self.t[0:shape[0], off:off + nb].bitcast(dtype)
        if len(shape) > 2:
            pat = " ".join(f"d{i}" for i in range(1, len(shape)))
            v = v.rearrange(f"p ({pat}) -> p {pat}",
                            **{f"d{i}": shape[i] for i in range(1, len(shape))})
        return v

    def push(self):
        self.marks.append(self.off)

    def pop(self):
        self.off = self.marks.pop()


class Sem:

    def __init__(self, nc, name):
        self.h = nc.alloc_semaphore(name) if hasattr(nc, "alloc_semaphore") else None
        if self.h is None:
            self.cm = nc.semaphore(name)
            self.h = self.cm.__enter__()
        self.n = 0

    def inc(self, k=1):
        self.n += k
        return self.h


def barrier(block, sems):
    targets = [(sems[k], sems[k].n) for k in ("pe", "dv", "ac", "gp", "ld", "st", "cc")]
    for e in ("tensor", "vector", "scalar", "gpsimd", "sync"):
        sec = getattr(block, e)

        @sec
        def _(eng):
            for s, v in targets:
                if v > 0:
                    eng.wait_ge(s.h, v)



def emit_fp4_quant(eng, x, out_bf, tmp, nblk, nparts=P, scale_src=None):
    n = nblk * 16
    bmax = tmp["bmax"][0:nparts, 0:nblk]
    scale = tmp["scale"][0:nparts, 0:nblk]
    rscale = tmp["rscale"][0:nparts, 0:nblk]
    t = tmp["t"][0:nparts, 0:n]
    ta = tmp["ta"][0:nparts, 0:n]
    sb_t = tmp["sb"][0:nparts, 0:n]
    y1 = tmp["y1"][0:nparts, 0:n]
    y2 = tmp["y2"][0:nparts, 0:n]
    y3 = tmp["y3"][0:nparts, 0:n]
    c1 = tmp["c1"][0:nparts, 0:n]
    c2 = tmp["c2"][0:nparts, 0:n]
    c3 = tmp["c3"][0:nparts, 0:n]
    s12 = tmp["s12"][0:nparts, 0:n]
    qn = tmp["qn"][0:nparts, 0:n]
    maskpos, masksign = tmp["maskpos"], tmp["masksign"]
    xv = x.rearrange("p (b k) -> p b k", k=16)
    sw = tmp["selfwait"]
    sw(eng, eng.tensor_reduce(out=bmax, in_=xv, axis=AX.X, op=ALU.max,
                              apply_absolute_value=True))
    sw(eng, eng.tensor_scalar(out=scale, in0=bmax, scalar1=1.0 / 6.0, scalar2=1e-12,
                              op0=ALU.mult, op1=ALU.max))
    sw(eng, eng.reciprocal(out=rscale, in_=scale))
    rs_exp = rscale.unsqueeze(2).broadcast_to([nparts, nblk, 16])
    sc_exp = scale.unsqueeze(2).broadcast_to([nparts, nblk, 16])
    eng.tensor_tensor(out=t.rearrange("p (b k) -> p b k", k=16), in0=xv,
                      in1=rs_exp, op=ALU.mult)
    eng.tensor_scalar(out=ta.bitcast(I32), in0=t.bitcast(I32),
                      scalar1=maskpos[0:nparts], scalar2=None, op0=ALU.bitwise_and)
    eng.tensor_scalar(out=y1, in0=ta, scalar1=2.0, scalar2=128.0, op0=ALU.mult, op1=ALU.add)
    eng.tensor_scalar(out=y2, in0=ta, scalar1=128.0, scalar2=None, op0=ALU.add)
    eng.tensor_scalar(out=y3, in0=ta, scalar1=0.5, scalar2=128.0, op0=ALU.mult, op1=ALU.add)
    eng.tensor_scalar(out=c1, in0=y1, scalar1=132.0, scalar2=0.5, op0=ALU.min, op1=ALU.mult)
    eng.tensor_scalar(out=c2, in0=y2, scalar1=130.0, scalar2=132.0, op0=ALU.max, op1=ALU.min)
    eng.tensor_scalar(out=c3, in0=y3, scalar1=130.0, scalar2=2.0, op0=ALU.max, op1=ALU.mult)
    eng.tensor_tensor(out=s12, in0=c1, in1=c2, op=ALU.add)
    eng.scalar_tensor_tensor(out=qn, in0=c3, scalar=-454.0, in1=s12,
                             op0=ALU.add, op1=ALU.add)
    eng.tensor_scalar(out=sb_t.bitcast(I32), in0=t.bitcast(I32),
                      scalar1=masksign[0:nparts], scalar2=None, op0=ALU.bitwise_and)
    eng.tensor_tensor(out=qn.bitcast(I32), in0=qn.bitcast(I32),
                      in1=sb_t.bitcast(I32), op=ALU.bitwise_or)
    return eng.tensor_tensor(out=out_bf.rearrange("p (b k) -> p b k", k=16),
                              in0=qn.rearrange("p (b k) -> p b k", k=16),
                              in1=sc_exp, op=ALU.mult)


FP4_TMP_OPS = 18


def alloc_fp4_tmp(ar, n, cmaps, sem=None):
    tmp = {}
    if sem is not None:
        def sw(eng, inst, sem=sem):
            inst.then_inc(sem.inc(), 1)
            eng.wait_ge(sem.h, sem.n)
        tmp["selfwait"] = sw
    tmp["bmax"] = ar.alloc([P, n // 16], F32, "q_bmax")
    tmp["scale"] = ar.alloc([P, n // 16], F32, "q_scale")
    tmp["rscale"] = ar.alloc([P, n // 16], F32, "q_rscale")
    for k in ("t", "ta", "s12", "qn"):
        tmp[k] = ar.alloc([P, n], F32, "q_" + k)
    tmp["sb"] = tmp["ta"]
    for k in ("y1", "y2", "y3", "c1", "c2", "c3"):
        tmp[k] = ar.alloc([P, n], BF16, "q_" + k)
    tmp["maskpos"] = cmaps["maskpos"]
    tmp["masksign"] = cmaps["masksign"]
    return tmp




class Builder:
    def __init__(self, nc):
        self.nc = nc
        self.ar = Arena(nc, 207 * 1024)
        self.psall = nc.alloc_psum_tensor("psall", [P, 4096], F32)
        self.input_names = []
        self.sems = {}
        for name in ("ld", "gp", "pe", "dv", "ac", "st", "bar", "cc", "sc"):
            self.sems[name] = Sem(nc, "s_" + name)

    def psv(self, bank, nbanks=1, dtype=F32, shape=None):
        v = self.psall[:, bank * 512:(bank + nbanks) * 512]
        if dtype != F32:
            v = v.bitcast(dtype)
        if shape is not None:
            assert len(shape) == 2
            v = v[0:shape[0], 0:shape[1]]
        return v

    def din(self, name, shape, dtype):
        self.input_names.append(name)
        return self.nc.dram_tensor(name, list(shape), dtype, kind="ExternalInput").ap()

    def dout(self, name, shape, dtype):
        return self.nc.dram_tensor(name, list(shape), dtype, kind="ExternalOutput").ap()

    def dint(self, name, shape, dtype):
        return self.nc.dram_tensor(name, list(shape), dtype, kind="Internal").ap()


def fm_view(dram_ap):
    return dram_ap.rearrange("(c p) t -> p c t", p=P)


def emit_const_loads(b, blk, cm, loads):
    s_ld = b.sems["ld"]

    @blk.sync
    def _(sync):
        for name, src in loads:
            sync.dma_start(out=cm[name], in_=src).then_inc(s_ld.inc(16), 16)


def emit_ln_quant(b, blk, stage, src_d, xn_d, cm, xq_fm=None, fp4tmp=None, dbg_d=None):
    nc, ar, sems = b.nc, b.ar, b.sems
    s_ld, s_pe, s_dv, s_ac, s_st = sems["ld"], sems["pe"], sems["dv"], sems["ac"], sems["st"]
    NT = T_CORE // P
    ar.push()
    x_t = [ar.alloc([P, D], F32, "ln_x0"), ar.alloc([P, D], F32, "ln_x1")]
    st6 = [ar.alloc([P, 12], F32, "ln_st0"), ar.alloc([P, 12], F32, "ln_st1")]
    sq4 = ar.alloc([P, 4], F32, "ln_sq4")
    mg4 = ar.alloc([P, 4], F32, "ln_mg4")
    M24 = ar.alloc([P, 4], F32, "ln_M24")
    ex2a = ar.alloc([P, 4], F32, "ln_ex2a")
    esum2 = ar.alloc([P, 1], F32, "ln_es2")
    msq2 = ar.alloc([P, 1], F32, "ln_msq2")
    ex2 = ar.alloc([P, 4], F32, "ln_ex2")
    msum = [ar.alloc([P, 1], F32, "ln_ms0"), ar.alloc([P, 1], F32, "ln_ms1")]
    esum = ar.alloc([P, 1], F32, "ln_es")
    msq = ar.alloc([P, 1], F32, "ln_msq")
    var = [ar.alloc([P, 1], F32, "ln_va0"), ar.alloc([P, 1], F32, "ln_va1")]
    mneg = [ar.alloc([P, 1], F32, "ln_mn0"), ar.alloc([P, 1], F32, "ln_mn1")]
    lv = [ar.alloc([P, 1], F32, "ln_lv0"), ar.alloc([P, 1], F32, "ln_lv1")]
    rstd = [ar.alloc([P, 1], F32, "ln_rs0"), ar.alloc([P, 1], F32, "ln_rs1")]
    nm = [ar.alloc([P, 1], F32, "ln_nm0"), ar.alloc([P, 1], F32, "ln_nm1")]
    xn_bf = [ar.alloc([P, D], BF16, "ln_xb0"), ar.alloc([P, D], BF16, "ln_xb1")]
    quant = xq_fm is not None
    if quant:
        xn_f = [ar.alloc([P, D], F32, "ln_xf0"), ar.alloc([P, D], F32, "ln_xf1")]
        xq_t = [ar.alloc([P, D], BF16, "ln_xq0"), ar.alloc([P, D], BF16, "ln_xq1")]
        xq_t2 = [ar.alloc([P, D], BF16, "ln_xr0"), ar.alloc([P, D], BF16, "ln_xr1")]
    fm_t = [ar.alloc([P, D], BF16, "ln_fm0"), ar.alloc([P, D], BF16, "ln_fm1")]

    xv = src_d.rearrange("(n p) d -> n p d", p=P)
    xnv = fm_view(xn_d)
    xqv = fm_view(xq_fm) if quant else None

    mk = [{} for _ in range(NT)]

    for i in range(NT):
        j = i % 2

        @blk.sync
        def _(sync, i=i, j=j):
            if i >= 2:
                sync.wait_ge(s_dv.h, mk[i - 2]["x_done"])
            sync.dma_start(out=x_t[j], in_=xv[i]).then_inc(s_ld.inc(16), 16)

        ld_i = s_ld.n

        @blk.vector
        def _(vector, i=i, j=j, ld_i=ld_i):
            vector.wait_ge(s_ld.h, ld_i)
            if i >= 2:
                vector.wait_ge(s_ac.h, mk[i - 2]["stats_done"])
            def step(inst):
                inst.then_inc(s_dv.inc(), 1)
                vector.wait_ge(s_dv.h, s_dv.n)
            step(vector.bn_stats(out=st6[j][:, 0:6], in_=x_t[j][:, 0:512]))
            step(vector.bn_stats(out=st6[j][:, 6:12], in_=x_t[j][:, 512:1024]))
            mg0 = st6[j].rearrange("p (g k) -> p g k", k=3)[:, :, 1]
            M2g0 = st6[j].rearrange("p (g k) -> p g k", k=3)[:, :, 2]
            step(vector.tensor_copy(out=mg4, in_=mg0))
            step(vector.tensor_copy(out=M24, in_=M2g0))
            step(vector.tensor_tensor(out=sq4, in0=mg4, in1=mg4, op=ALU.mult))
            step(vector.tensor_scalar(out=ex2a, in0=M24, scalar1=1.0 / 256.0,
                                      scalar2=None, op0=ALU.mult))
            step(vector.tensor_tensor(out=ex2, in0=ex2a, in1=sq4, op=ALU.add))
            step(vector.reduce_sum(out=msum[j], in_=mg4, axis=AX.X))
            step(vector.reduce_sum(out=esum, in_=ex2, axis=AX.X))
            step(vector.tensor_scalar(out=mneg[j], in0=msum[j], scalar1=-0.25,
                                      scalar2=None, op0=ALU.mult))
            step(vector.tensor_tensor(out=msq, in0=msum[j], in1=msum[j], op=ALU.mult))
            step(vector.tensor_scalar(out=esum2, in0=esum, scalar1=0.25, scalar2=None,
                                      op0=ALU.mult))
            step(vector.tensor_scalar(out=msq2, in0=msq, scalar1=-0.0625, scalar2=None,
                                      op0=ALU.mult))
            step(vector.tensor_tensor(out=var[j], in0=esum2, in1=msq2, op=ALU.add))

        mk[i]["stats"] = s_dv.n

        @blk.scalar
        def _(scalar, i=i, j=j):
            scalar.wait_ge(s_dv.h, mk[i]["stats"])
            scalar.activation(out=lv[j], in_=var[j], func=AF.Ln, bias=cm["eps_col"][:, 0:1]).then_inc(s_ac.inc(), 1)
            scalar.activation(out=rstd[j], in_=lv[j], func=AF.Exp, scale=-0.5).then_inc(s_ac.inc(), 1)
        mk[i]["stats_done"] = s_ac.n

        if dbg_d is not None:
            @blk.sync
            def _(sync, i=i, j=j):
                sync.wait_ge(s_ac.h, mk[i]["stats_done"])
                for col, t_ in enumerate([msum[j], esum, esum2, msq, msq2, var[j], mneg[j], rstd[j]]):
                    sync.dma_start(out=dbg_d[:, i * 8 + col:i * 8 + col + 1],
                                   in_=t_[:, 0:1]).then_inc(s_st.inc(16), 16)

        @blk.vector
        def _(vector, i=i, j=j):
            vector.wait_ge(s_ac.h, mk[i]["stats_done"])
            vector.tensor_scalar(out=nm[j], in0=mneg[j], scalar1=rstd[j][:, 0:1],
                                 scalar2=None, op0=ALU.mult).then_inc(s_dv.inc(), 1)
            vector.wait_ge(s_dv.h, s_dv.n)
            if i >= 2:
                vector.wait_ge(s_pe.h, mk[i - 2]["fm_tr"])
            vector.tensor_scalar(out=xn_bf[j], in0=x_t[j], scalar1=rstd[j][:, 0:1],
                                 scalar2=nm[j][:, 0:1], op0=ALU.mult,
                                 op1=ALU.add).then_inc(s_dv.inc(), 1)
            if quant:
                vector.tensor_scalar(out=xn_f[j], in0=x_t[j], scalar1=rstd[j][:, 0:1],
                                     scalar2=nm[j][:, 0:1], op0=ALU.mult,
                                     op1=ALU.add).then_inc(s_dv.inc(), 1)
        mk[i]["x_done"] = s_dv.n

        @blk.tensor
        def _(tensor, i=i, j=j):
            tensor.wait_ge(s_dv.h, mk[i]["x_done"] - (1 if quant else 0))
            if i >= 2:
                tensor.wait_ge(s_dv.h, mk[i - 2]["fm_evac"])
            pv = b.psv(6 + j, 1, BF16)
            for c in range(DC):
                li = tensor.transpose(pv[:, c * P:(c + 1) * P], xn_bf[j][:, c * P:(c + 1) * P],
                                      cm["ident_bf"])
            li.then_inc(s_pe.inc(), 1)
        mk[i]["fm_tr"] = s_pe.n

        @blk.vector
        def _(vector, i=i, j=j):
            vector.wait_ge(s_pe.h, mk[i]["fm_tr"])
            if i >= 2:
                vector.wait_ge(s_st.h, mk[i - 2]["fm_store"])
            pv = b.psv(6 + j, 1, BF16)
            vector.tensor_copy(out=fm_t[j], in_=pv).then_inc(s_dv.inc(), 1)
        mk[i]["fm_evac"] = s_dv.n

        @blk.sync
        def _(sync, i=i, j=j):
            sync.wait_ge(s_dv.h, mk[i]["fm_evac"])
            sync.dma_start(out=xnv[:, :, i * P:(i + 1) * P],
                           in_=fm_t[j].rearrange("p (c t) -> p c t", t=P)).then_inc(s_st.inc(16), 16)
        mk[i]["fm_store"] = s_st.n

        if quant:
            @blk.vector
            def _(vector, i=i, j=j):
                if i >= 2:
                    vector.wait_ge(s_pe.h, mk[i - 2]["xq_tr"])
                li = emit_fp4_quant(vector, xn_f[j], xq_t[j], fp4tmp, D // 16)
                li.then_inc(s_dv.inc(FP4_TMP_OPS), FP4_TMP_OPS)
            mk[i]["quant"] = s_dv.n

            @blk.tensor
            def _(tensor, i=i, j=j):
                tensor.wait_ge(s_dv.h, mk[i]["quant"])
                if i >= 2:
                    tensor.wait_ge(s_dv.h, mk[i - 2]["xq_evac"])
                pv = b.psv(4 + j, 1, BF16)
                for c in range(DC):
                    li = tensor.transpose(pv[:, c * P:(c + 1) * P], xq_t[j][:, c * P:(c + 1) * P],
                                          cm["ident_bf"])
                li.then_inc(s_pe.inc(), 1)
            mk[i]["xq_tr"] = s_pe.n

            @blk.vector
            def _(vector, i=i, j=j):
                vector.wait_ge(s_pe.h, mk[i]["xq_tr"])
                if i >= 2:
                    vector.wait_ge(s_st.h, mk[i - 2]["xq_store"])
                pv = b.psv(4 + j, 1, BF16)
                vector.tensor_copy(out=xq_t2[j], in_=pv).then_inc(s_dv.inc(), 1)
            mk[i]["xq_evac"] = s_dv.n

            @blk.sync
            def _(sync, i=i, j=j):
                sync.wait_ge(s_dv.h, mk[i]["xq_evac"])
                sync.dma_start(out=xqv[:, :, i * P:(i + 1) * P],
                               in_=xq_t2[j].rearrange("p (c t) -> p c t", t=P)).then_inc(s_st.inc(16), 16)
            mk[i]["xq_store"] = s_st.n

    barrier(blk, sems)
    ar.pop()


def emit_matmul_chain(tensor, b, bank, lhsT_sb, rhs_sb, nk, extra=None, n=512):
    pv = b.psv(bank, 1, F32, [P, n]) if n <= 512 else None
    for k in range(nk):
        li = tensor.matmul(pv, lhsT_sb(k), rhs_sb(k), start=(k == 0),
                           stop=(k == nk - 1 and not extra))
    if extra:
        for idx, (l_ap, r_ap) in enumerate(extra):
            li = tensor.matmul(pv, l_ap, r_ap, start=False, stop=(idx == len(extra) - 1))
    return li


def emit_attn0(b, blk, cm, xn_d, wdram, out0_fm):
    nc, ar, sems = b.nc, b.ar, b.sems
    s_ld, s_gp, s_pe, s_dv, s_ac, s_st = (sems["ld"], sems["gp"], sems["pe"],
                                          sems["dv"], sems["ac"], sems["st"])
    ar.push()
    wnames = ["qkwT", "vwT", "wqT", "wkT", "wvT", "owT"]
    wsb = {n: ar.alloc([P, DC * D], BF16, "w_" + n) for n in wnames}
    xn_t = ar.alloc([P, DC, 512], BF16, "a2_xn")
    qkin = ar.alloc([P, DC * 512], BF16, "a2_qkin")
    vin = ar.alloc([P, DC * 512], BF16, "a2_vin")
    q_t = ar.alloc([P, H0 * 512], BF16, "a2_q")
    k_t = ar.alloc([P, H0 * 512], BF16, "a2_k")
    vtok = ar.alloc([P, 4 * D], BF16, "a2_vtok")
    attnO = qkin
    P_sb = vin[:, 0:512]
    Pn_sb = vin[:, 512:1024]
    bcb = vin[:, 1024:1536]
    linv = vin[0:1, 2048:3072].bitcast(F32)

    gp0 = s_gp.n

    @blk.gpsimd
    def _(gpsimd):
        for n in wnames:
            src = fm_view(wdram[n])
            gpsimd.dma_start(out=wsb[n].rearrange("p (k o) -> p k o", o=D),
                             in_=src).then_inc(s_gp.inc(16), 16)

    xnv = fm_view(xn_d)
    scale0 = 1.0 / math.sqrt(P)

    for g in range(4):
        ld_g = s_ld.n

        @blk.sync
        def _(sync, g=g):
            if g >= 1:
                sync.wait_ge(s_dv.h, s_dv.n)
            sync.dma_start(out=xn_t, in_=xnv[:, :, g * 512:(g + 1) * 512]).then_inc(s_ld.inc(16), 16)

        def lhs_w(name, c):
            return lambda k, name=name, c=c: wsb[name][:, k * D + c * P: k * D + (c + 1) * P]

        @blk.tensor
        def _(tensor, g=g, ld_g=ld_g):
            tensor.wait_ge(s_ld.h, ld_g + 16)
            tensor.wait_ge(s_gp.h, gp0 + 6 * 16)
            for ci, (wname, bname) in enumerate([("qkwT", "qk_b"), ("vwT", "v_b")]):
                for c in range(DC):
                    bank = (ci * DC + c) % 2
                    if ci * DC + c >= 2:
                        tensor.wait_ge(s_ac.h, s_ac.n - 1)
                    emit_matmul_chain(tensor, b, bank, lhs_w(wname, c),
                                      lambda k: xn_t[:, k, :], DC,
                                      extra=[(cm[bname][:, c * P:(c + 1) * P], cm["ones_row_f32"])]
                                      ).then_inc(s_pe.inc(), 1)
                    s_ac.n += 1
            s_ac.n -= 16

        @blk.scalar
        def _(scalar, g=g):
            for ci, dst in enumerate([qkin, vin]):
                for c in range(DC):
                    bank = (ci * DC + c) % 2
                    scalar.wait_ge(s_pe.h, s_pe.n - 16 + ci * DC + c + 1)
                    scalar.activation(out=dst[:, c * 512:(c + 1) * 512],
                                      in_=b.psv(bank, 1, F32),
                                      func=AF.Copy).then_inc(s_ac.inc(), 1)

        barrier(blk, sems)
        @blk.tensor
        def _(tensor, g=g):
            for ci, (wname, bname) in enumerate([("wqT", "q_b"), ("wkT", "k_b")]):
                for c in range(DC):
                    bank = (ci * DC + c) % 2
                    if ci * DC + c >= 2:
                        tensor.wait_ge(s_ac.h, s_ac.n - 1)
                    emit_matmul_chain(tensor, b, bank, lhs_w(wname, c),
                                      lambda k: qkin[:, k * 512:(k + 1) * 512], DC,
                                      extra=[(cm[bname][:, c * P:(c + 1) * P], cm["ones_row_f32"])]
                                      ).then_inc(s_pe.inc(), 1)
                    s_ac.n += 1
            s_ac.n -= 16

        @blk.scalar
        def _(scalar, g=g):
            for ci, dst in enumerate([q_t, k_t]):
                for c in range(DC):
                    bank = (ci * DC + c) % 2
                    scalar.wait_ge(s_pe.h, s_pe.n - 16 + ci * DC + c + 1)
                    scalar.activation(out=dst[:, c * 512:(c + 1) * 512],
                                      in_=b.psv(bank, 1, F32),
                                      func=AF.Copy).then_inc(s_ac.inc(), 1)

        @blk.tensor
        def _(tensor, g=g):
            for sb in range(4):
                for half in range(2):
                    u = sb * 2 + half
                    bank = 2 + u % 2
                    if u >= 2:
                        tensor.wait_ge(s_dv.h, s_dv.n - 1)
                    pv = b.psv(bank, 1, F32)
                    for k in range(DC):
                        tensor.matmul(pv, vin[:, k * 512 + sb * P: k * 512 + (sb + 1) * P],
                                      wsb["wvT"][:, k * D + half * 512:k * D + (half + 1) * 512],
                                      start=(k == 0), stop=False)
                    tensor.matmul(pv, cm["ones_col_f32"],
                                  cm["bv_row"][:, half * 512:(half + 1) * 512],
                                  start=False, stop=True).then_inc(s_pe.inc(), 1)
                    s_dv.n += 1
            s_dv.n -= 8

        @blk.vector
        def _(vector, g=g):
            for sb in range(4):
                for half in range(2):
                    u = sb * 2 + half
                    bank = 2 + u % 2
                    vector.wait_ge(s_pe.h, s_pe.n - 8 + u + 1)
                    vector.tensor_copy(out=vtok[:, sb * D + half * 512: sb * D + (half + 1) * 512],
                                       in_=b.psv(bank, 1, F32)).then_inc(s_dv.inc(), 1)

        barrier(blk, sems)

        for h in range(H0):
            pe_h = s_pe.n

            @blk.tensor
            def _(tensor, g=g, h=h):
                if h >= 1:
                    tensor.wait_ge(s_dv.h, s_dv.n)
                Sv = b.psv(4, 1, F32)
                for sb in range(4):
                    tensor.matmul(Sv[:, sb * P:(sb + 1) * P],
                                  k_t[:, h * 512 + sb * P: h * 512 + (sb + 1) * P],
                                  q_t[:, h * 512 + sb * P: h * 512 + (sb + 1) * P],
                                  start=(sb == 0), stop=False)
                    li = tensor.matmul(Sv[:, sb * P:(sb + 1) * P], cm["negCI_bf"], cm["maskA_bf"],
                                       start=False, stop=(sb == 3))
                li.then_inc(s_pe.inc(), 1)

            @blk.scalar
            def _(scalar, g=g, h=h, pe_h=pe_h):
                scalar.wait_ge(s_pe.h, pe_h + 1)
                scalar.activation(out=P_sb, in_=b.psv(4, 1, F32), func=AF.Exp,
                                  scale=scale0).then_inc(s_ac.inc(), 1)

            ac_h = s_ac.n

            @blk.tensor
            def _(tensor, g=g, h=h, ac_h=ac_h):
                tensor.wait_ge(s_ac.h, ac_h)
                lv = b.psv(5, 1, F32, [1, 512])
                for sb in range(4):
                    li = tensor.matmul(lv[:, sb * P:(sb + 1) * P], cm["ones_col_bf"],
                                       P_sb[:, sb * P:(sb + 1) * P], start=(sb == 0),
                                       stop=(sb == 3))
                li.then_inc(s_pe.inc(), 1)

            pe_l = s_pe.n

            @blk.vector
            def _(vector, g=g, h=h, pe_l=pe_l):
                vector.wait_ge(s_pe.h, pe_l)
                vector.reciprocal(out=linv, in_=b.psv(5, 1, F32, [1, 512])).then_inc(s_dv.inc(), 1)

            dv_l = s_dv.n

            @blk.tensor
            def _(tensor, g=g, h=h, dv_l=dv_l):
                tensor.wait_ge(s_dv.h, dv_l)
                tensor.matmul(b.psv(6, 1, F32), cm["ones_col_f32"], linv,
                              start=True, stop=True).then_inc(s_pe.inc(), 1)

            pe_bc = s_pe.n

            @blk.scalar
            def _(scalar, g=g, h=h, pe_bc=pe_bc):
                scalar.wait_ge(s_pe.h, pe_bc)
                scalar.activation(out=bcb, in_=b.psv(6, 1, F32), func=AF.Copy).then_inc(s_ac.inc(), 1)

            ac_bc = s_ac.n

            @blk.vector
            def _(vector, g=g, h=h, ac_bc=ac_bc):
                vector.wait_ge(s_ac.h, ac_bc)
                vector.tensor_tensor(out=Pn_sb, in0=P_sb, in1=bcb, op=ALU.mult).then_inc(s_dv.inc(), 1)

            dv_pn = s_dv.n

            @blk.tensor
            def _(tensor, g=g, h=h, dv_pn=dv_pn):
                tensor.wait_ge(s_dv.h, dv_pn)
                av = b.psv(7, 1, F32)
                for sb in range(4):
                    li = tensor.matmul(av[:, sb * P:(sb + 1) * P],
                                       vtok[:, sb * D + h * P: sb * D + (h + 1) * P],
                                       Pn_sb[:, sb * P:(sb + 1) * P], start=(sb == 0),
                                       stop=(sb == 3))
                li.then_inc(s_pe.inc(), 1)

            pe_av = s_pe.n

            @blk.vector
            def _(vector, g=g, h=h, pe_av=pe_av):
                vector.wait_ge(s_pe.h, pe_av)
                vector.tensor_copy(out=attnO[:, h * 512:(h + 1) * 512],
                                   in_=b.psv(7, 1, F32)).then_inc(s_dv.inc(), 1)

        barrier(blk, sems)

        @blk.tensor
        def _(tensor, g=g):
            for c in range(DC):
                bank = c % 2
                if c >= 2:
                    tensor.wait_ge(s_ac.h, s_ac.n - 1)
                emit_matmul_chain(tensor, b, bank, lhs_w("owT", c),
                                  lambda k: attnO[:, k * 512:(k + 1) * 512], DC,
                                  extra=[(cm["ob_b"][:, c * P:(c + 1) * P], cm["ones_row_f32"])]
                                  ).then_inc(s_pe.inc(), 1)
                s_ac.n += 1
            s_ac.n -= 8

        @blk.scalar
        def _(scalar, g=g):
            for c in range(DC):
                bank = c % 2
                scalar.wait_ge(s_pe.h, s_pe.n - 8 + c + 1)
                scalar.activation(
                    out=out0_fm.rearrange("p (c t) -> p c t", t=T_CORE)[:, c, g * 512:(g + 1) * 512],
                    in_=b.psv(bank, 1, F32),
                    func=AF.Copy).then_inc(s_ac.inc(), 1)

        barrier(blk, sems)

    ar.pop()


def emit_ffn_combine(b, blk, cm, stage, xn_src, w_dram, x_res_d, out_res_d, other_fm,
                     ds_d, xq_fm=None, fp4tmp=None):
    nc, ar, sems = b.nc, b.ar, b.sems
    s_ld, s_gp, s_pe, s_dv, s_ac, s_st = (sems["ld"], sems["gp"], sems["pe"],
                                          sems["dv"], sems["ac"], sems["st"])
    HID = FFN0_H if stage == "A" else FFN1_H
    HH = HID // 2
    HC = HH // P
    ar.push()
    wg_sb = ar.alloc([P, DC * HH], BF16, "f_wg")
    wu_sb = ar.alloc([P, DC * HH], BF16, "f_wu")
    wd_sb = ar.alloc([P, HC * D], BF16, "f_wd")
    if stage == "A":
        stage_i = ar.alloc([P, HH], F32, "f_stg")
        g_sb = ar.alloc([P, HC * 512], BF16, "f_g")
        u_sb = ar.alloc([P, HC * 512], BF16, "f_u")
        h_fm = ar.alloc([P, HC * 512], BF16, "f_h")
        xn_t = ar.alloc([P, DC, 512], BF16, "f_xn")
    else:
        xq_t = ar.alloc([P, DC, 512], BF16, "f_xqt")
        g1 = ar.alloc([P, HH], F32, "f_gt")
        u1 = ar.alloc([P, HH], F32, "f_ut")
        h1 = ar.alloc([P, HH], F32, "f_ht")
        hq1 = ar.alloc([P, HH], BF16, "f_hqt")
        g_t = [g1, g1]; u_t = [u1, u1]; h_t = [h1, h1]; hq_t = [hq1, hq1]
        h_fm = ar.alloc([P, HC * 512], BF16, "f_hfm")
        oth_t = ar.alloc([P, DC * 256], F32, "f_oth")
    ds_t = ar.alloc([P, DC * 256], F32, "f_ds")
    sfm = ar.alloc([P, DC * 256], F32, "f_sfm")
    sfm2 = ds_t
    x0_t = sfm[:, 0:D]
    x1_t = sfm[:, D:2 * D]

    xnv = fm_view(xn_src) if stage == "A" else None
    dsv = fm_view(ds_d)
    othv = fm_view(other_fm) if stage == "B" else None
    x0v = x_res_d.rearrange("(n p) d -> n p d", p=P)
    o0v = out_res_d.rearrange("(n p) d -> n p d", p=P)
    xqv_d = fm_view(xq_fm) if stage == "B" else None

    for half in range(2):
        if stage == "A":
            for wname, wsb_, nk, wid in (("wg", wg_sb, DC, HH), ("wu", wu_sb, DC, HH),
                                         ("wd", wd_sb, HC, D)):
                for k in range(nk):
                    gp_k = s_gp.n

                    @blk.gpsimd
                    def _(gpsimd, half=half, k=k, wname=wname, wid=wid):
                        gpsimd.wait_ge(s_dv.h, s_dv.n)
                        if wname == "wd":
                            srcv = w_dram["wd"][half * HH + k * P: half * HH + (k + 1) * P, :]
                        else:
                            srcv = w_dram[wname][k * P:(k + 1) * P, half * HH:(half + 1) * HH]
                        gpsimd.dma_start(out=stage_i[:, 0:wid], in_=srcv).then_inc(s_gp.inc(16), 16)

                    @blk.vector
                    def _(vector, k=k, gp_k=gp_k, wsb_=wsb_, wid=wid):
                        vector.wait_ge(s_gp.h, gp_k + 16)
                        vector.tensor_scalar(out=wsb_[:, k * wid:(k + 1) * wid],
                                             in0=stage_i[:, 0:wid],
                                             scalar1=1.0 / 127.0, scalar2=None,
                                             op0=ALU.mult).then_inc(s_dv.inc(), 1)
        else:
            @blk.gpsimd
            def _(gpsimd, half=half):
                gpsimd.dma_start(
                    out=wg_sb.rearrange("p (k o) -> p k o", o=HH),
                    in_=fm_view(w_dram["wg"])[:, :, half * HH:(half + 1) * HH]
                ).then_inc(s_gp.inc(16), 16)
                gpsimd.dma_start(
                    out=wu_sb.rearrange("p (k o) -> p k o", o=HH),
                    in_=fm_view(w_dram["wu"])[:, :, half * HH:(half + 1) * HH]
                ).then_inc(s_gp.inc(16), 16)
                gpsimd.dma_start(
                    out=wd_sb.rearrange("p (k o) -> p k o", o=D),
                    in_=fm_view(w_dram["wd"])[:, half * HC:(half + 1) * HC, :]
                ).then_inc(s_gp.inc(16), 16)

        barrier(blk, sems)

        for tt in range(4):
            if stage == "A":
                ld_t = s_ld.n

                @blk.sync
                def _(sync, half=half, tt=tt):
                    sync.wait_ge(s_pe.h, s_pe.n)
                    sync.dma_start(out=xn_t, in_=xnv[:, :, tt * 512:(tt + 1) * 512]).then_inc(s_ld.inc(16), 16)

                @blk.tensor
                def _(tensor, half=half, tt=tt, ld_t=ld_t):
                    tensor.wait_ge(s_ld.h, ld_t + 16)
                    tensor.wait_ge(s_dv.h, s_dv.n)
                    for ci, wsb_ in enumerate([wg_sb, wu_sb]):
                        for hc in range(HC):
                            u = ci * HC + hc
                            bank = u % 2
                            if u >= 2:
                                tensor.wait_ge(s_ac.h, s_ac.n - 1)
                            pv = b.psv(bank, 1, F32)
                            for k in range(DC):
                                li = tensor.matmul(pv, wsb_[:, k * HH + hc * P: k * HH + (hc + 1) * P],
                                                   xn_t[:, k, :], start=(k == 0), stop=(k == DC - 1))
                            li.then_inc(s_pe.inc(), 1)
                            s_ac.n += 1
                    s_ac.n -= 2 * HC

                @blk.scalar
                def _(scalar, half=half, tt=tt):
                    for ci, (dst, fn) in enumerate([(g_sb, AF.Silu), (u_sb, AF.Copy)]):
                        for hc in range(HC):
                            u = ci * HC + hc
                            bank = u % 2
                            scalar.wait_ge(s_pe.h, s_pe.n - 2 * HC + u + 1)
                            scalar.activation(out=dst[:, hc * 512:(hc + 1) * 512],
                                              in_=b.psv(bank, 1, F32), func=fn).then_inc(s_ac.inc(), 1)

                ac_gu = s_ac.n

                @blk.vector
                def _(vector, half=half, tt=tt, ac_gu=ac_gu):
                    vector.wait_ge(s_ac.h, ac_gu)
                    vector.wait_ge(s_pe.h, s_pe.n)
                    vector.tensor_tensor(out=h_fm, in0=g_sb, in1=u_sb, op=ALU.mult).then_inc(s_dv.inc(), 1)
            else:
                ld_q = s_ld.n

                @blk.sync
                def _(sync, tt=tt):
                    sync.wait_ge(s_pe.h, s_pe.n)
                    sync.dma_start(out=xq_t, in_=xqv_d[:, :, tt * 512:(tt + 1) * 512]).then_inc(s_ld.inc(16), 16)

                dv_marks = {}
                for ts in range(4):
                    t0 = tt * 512 + ts * P
                    j = ts % 2

                    @blk.tensor
                    def _(tensor, half=half, tt=tt, ts=ts, t0=t0, ld_q=ld_q):
                        tensor.wait_ge(s_ld.h, ld_q + 16)
                        tensor.wait_ge(s_ac.h, s_ac.n)
                        if ts == 0 and not (half == 0 and tt == 0):
                            tensor.wait_ge(s_dv.h, s_dv.n)
                        for ci in range(2):
                            pv = b.psall[:, ci * 1536:ci * 1536 + HH]
                            for ht in range(3):
                                n = min(512, HH - ht * 512)
                                for k in range(DC):
                                    li = tensor.matmul(pv[:, ht * 512:ht * 512 + n],
                                                       xq_t[:, k, ts * P:(ts + 1) * P],
                                                       (wg_sb if ci == 0 else wu_sb)[:, k * HH + ht * 512: k * HH + ht * 512 + n],
                                                       start=(k == 0), stop=(k == DC - 1))
                        li.then_inc(s_pe.inc(), 1)

                    pe_gu = s_pe.n

                    @blk.scalar
                    def _(scalar, half=half, tt=tt, ts=ts, j=j, pe_gu=pe_gu):
                        scalar.wait_ge(s_pe.h, pe_gu)
                        if (half, tt, ts) in dv_marks:
                            pass
                        mk = dv_marks.get(("h_tt", ts - 1))
                        if mk is not None:
                            scalar.wait_ge(s_dv.h, mk)
                        scalar.activation(out=g_t[j], in_=b.psall[:, 0:HH], func=AF.Silu).then_inc(s_ac.inc(), 1)
                        scalar.activation(out=u_t[j], in_=b.psall[:, 1536:1536 + HH],
                                          func=AF.Copy).then_inc(s_ac.inc(), 1)

                    ac_gu = s_ac.n

                    @blk.vector
                    def _(vector, half=half, tt=tt, ts=ts, j=j, ac_gu=ac_gu):
                        vector.wait_ge(s_ac.h, ac_gu)
                        vector.tensor_tensor(out=h_t[j], in0=g_t[j], in1=u_t[j], op=ALU.mult).then_inc(s_dv.inc(), 1)
                        dv_marks[("h_tt", ts)] = s_dv.n
                        mk_tr = dv_marks.get(("pe_tr", ts - 1))
                        if mk_tr is not None:
                            vector.wait_ge(s_pe.h, mk_tr)
                        li = emit_fp4_quant(vector, h_t[j], hq_t[j], fp4tmp, HH // 16)
                        li.then_inc(s_dv.inc(FP4_TMP_OPS), FP4_TMP_OPS)

                    dv_q = s_dv.n
                    dv_marks[("gu", ts)] = dv_q

                    @blk.tensor
                    def _(tensor, half=half, tt=tt, ts=ts, j=j, dv_q=dv_q):
                        tensor.wait_ge(s_dv.h, dv_q)
                        if not (half == 0 and tt == 0 and ts == 0):
                            tensor.wait_ge(s_dv.h, s_dv.n)
                        pv = b.psv(6, 2, BF16)
                        for k in range(HC):
                            li = tensor.transpose(pv[:, k * P:(k + 1) * P],
                                                  hq_t[j][:, k * P:(k + 1) * P], cm["ident_bf"])
                        li.then_inc(s_pe.inc(), 1)

                    pe_tr = s_pe.n
                    dv_marks[("pe_tr", ts)] = pe_tr

                    @blk.vector
                    def _(vector, half=half, tt=tt, ts=ts, pe_tr=pe_tr):
                        vector.wait_ge(s_pe.h, pe_tr)
                        pv = b.psv(6, 2, BF16)
                        vector.tensor_copy(
                            out=h_fm.rearrange("p (k t) -> p k t", t=512)[:, :, ts * P:(ts + 1) * P],
                            in_=pv.rearrange("p (k t) -> p k t", t=P)[:, 0:HC, :]).then_inc(s_dv.inc(), 1)

            dv_h = s_dv.n

            for ttq in range(2):
                @blk.tensor
                def _(tensor, half=half, tt=tt, ttq=ttq, dv_h=dv_h):
                    tensor.wait_ge(s_dv.h, dv_h)
                    if not (half == 0 and tt == 0 and ttq == 0):
                        tensor.wait_ge(s_dv.h, s_dv.n)
                        tensor.wait_ge(s_ac.h, s_ac.n)
                    dbank = 2 if stage == "A" else 0
                    dpv = b.psv(dbank, 4, F32)
                    for c in range(DC):
                        for k in range(HC):
                            li = tensor.matmul(
                                dpv[:, c * 256:(c + 1) * 256],
                                wd_sb[:, k * D + c * P: k * D + (c + 1) * P],
                                h_fm[:, k * 512 + ttq * 256: k * 512 + ttq * 256 + 256],
                                start=(k == 0 and c % 2 == 0),
                                stop=(k == HC - 1 and c == DC - 1))
                    li.then_inc(s_pe.inc(), 1)

                pe_d = s_pe.n
                toff = tt * 512 + ttq * 256

                if half == 0:
                    @blk.scalar
                    def _(scalar, tt=tt, ttq=ttq, pe_d=pe_d):
                        scalar.wait_ge(s_pe.h, pe_d)
                        scalar.wait_ge(s_st.h, s_st.n)
                        scalar.activation(out=ds_t, in_=b.psv(2 if stage == "A" else 0, 4, F32),
                                          func=AF.Copy).then_inc(s_ac.inc(), 1)

                    ac_d = s_ac.n

                    @blk.sync
                    def _(sync, ac_d=ac_d, toff=toff):
                        sync.wait_ge(s_ac.h, ac_d)
                        sync.dma_start(
                            out=dsv[:, :, toff:toff + 256],
                            in_=ds_t.rearrange("p (c t) -> p c t", t=256)).then_inc(s_st.inc(16), 16)


                else:
                    @blk.sync
                    def _(sync, toff=toff, stage=stage):
                        sync.wait_ge(s_dv.h, s_dv.n)
                        sync.dma_start(out=ds_t,
                                       in_=dsv[:, :, toff:toff + 256]).then_inc(s_ld.inc(16), 16)
                        if stage == "B":
                            sync.dma_start(out=oth_t,
                                           in_=othv[:, :, toff:toff + 256]).then_inc(s_ld.inc(16), 16)

                    ld_c2 = s_ld.n

                    @blk.vector
                    def _(vector, half=half, tt=tt, ttq=ttq, pe_d=pe_d, ld_c2=ld_c2, toff=toff):
                        vector.wait_ge(s_pe.h, pe_d)
                        vector.wait_ge(s_ld.h, ld_c2)
                        vector.tensor_tensor(out=sfm, in0=b.psv(2 if stage == "A" else 0, 4, F32),
                                             in1=ds_t, op=ALU.add).then_inc(s_dv.inc(), 1)
                        if stage == "A":
                            o3 = other_fm.rearrange("p (c t) -> p c t", t=T_CORE)[:, :, toff:toff + 256]
                        else:
                            o3 = oth_t.rearrange("p (c t) -> p c t", t=256)
                        vector.tensor_tensor(out=sfm2.rearrange("p (c t) -> p c t", t=256),
                                             in0=sfm.rearrange("p (c t) -> p c t", t=256),
                                             in1=o3, op=ALU.add).then_inc(s_dv.inc(), 1)

                    dv_s = s_dv.n

                    for tk in range(2):
                        ti = toff // P + tk

                        @blk.tensor
                        def _(tensor, tk=tk, dv_s=dv_s, half=half, tt=tt, ttq=ttq, stage=stage):
                            tensor.wait_ge(s_dv.h, dv_s)
                            first = (tt == 0 and ttq == 0 and tk == 0)
                            if not first:
                                tensor.wait_ge(s_dv.h, s_dv.n)
                            pv = b.psv((0 if stage == "A" else 4) if tk == 0 else 6, 2, F32)
                            for c in range(DC):
                                li = tensor.transpose(
                                    pv[:, c * P:(c + 1) * P],
                                    sfm2.rearrange("p (c t) -> p c t", t=256)[:, c, tk * P:(tk + 1) * P],
                                    cm["ident_f32"])
                            li.then_inc(s_pe.inc(), 1)

                        pe_tr = s_pe.n

                        @blk.sync
                        def _(sync, ti=ti):
                            sync.wait_ge(s_dv.h, s_dv.n)
                            sync.dma_start(out=x0_t, in_=x0v[ti]).then_inc(s_ld.inc(16), 16)

                        ld_x2 = s_ld.n

                        @blk.vector
                        def _(vector, tk=tk, pe_tr=pe_tr, ld_x2=ld_x2):
                            vector.wait_ge(s_pe.h, pe_tr)
                            vector.wait_ge(s_ld.h, ld_x2)
                            vector.wait_ge(s_st.h, s_st.n)
                            vector.tensor_tensor(out=x1_t,
                                                 in0=b.psv((0 if stage == "A" else 4) if tk == 0 else 6, 2, F32),
                                                 in1=x0_t, op=ALU.add).then_inc(s_dv.inc(), 1)

                        dv_x1 = s_dv.n

                        @blk.sync
                        def _(sync, ti=ti, dv_x1=dv_x1):
                            sync.wait_ge(s_dv.h, dv_x1)
                            sync.dma_start(out=o0v[ti], in_=x1_t).then_inc(s_st.inc(16), 16)

        barrier(blk, sems)
    ar.pop()


def emit_fp4_weight_quant(b, blk, cm, w_in, wT_out, rows, cols, fp4tmp):
    nc, ar, sems = b.nc, b.ar, b.sems
    s_ld, s_pe, s_dv, s_ac, s_st = (sems["ld"], sems["pe"], sems["dv"], sems["ac"], sems["st"])
    RC = rows // P
    CW = 1408 if cols % 1408 == 0 else cols
    NSUB = cols // CW
    ar.push()
    wstg = [ar.alloc([P, cols], F32, "wq_s0"), ar.alloc([P, cols], F32, "wq_s1")]
    wqt = ar.alloc([P, cols], BF16, "wq_q")
    wqfm = [ar.alloc([P, cols], BF16, "wq_f0"), ar.alloc([P, cols], BF16, "wq_f1")]
    dst = wT_out.rearrange("(j p) o -> p j o", p=P)

    for r in range(RC):
        j = r % 2
        ld_r = s_ld.n

        @blk.sync
        def _(sync, r=r, j=j):
            if r >= 2:
                sync.wait_ge(s_dv.h, s_dv.n)
            sync.dma_start(out=wstg[j], in_=w_in[r * P:(r + 1) * P, :]).then_inc(s_ld.inc(16), 16)

        @blk.vector
        def _(vector, r=r, j=j, ld_r=ld_r):
            vector.wait_ge(s_ld.h, ld_r + 16)
            if r >= 1:
                vector.wait_ge(s_pe.h, s_pe.n)
            for s in range(NSUB):
                li = emit_fp4_quant(vector, wstg[j][:, s * CW:(s + 1) * CW],
                                    wqt[:, s * CW:(s + 1) * CW], fp4tmp, CW // 16)
            li.then_inc(s_dv.inc(NSUB * FP4_TMP_OPS), NSUB * FP4_TMP_OPS)

        dv_r = s_dv.n

        @blk.tensor
        def _(tensor, r=r, j=j, dv_r=dv_r):
            tensor.wait_ge(s_dv.h, dv_r)
            if r >= 2:
                tensor.wait_ge(s_dv.h, s_dv.n)
            pv = b.psv((r % 2) * 3, 3, BF16)
            for c in range(cols // P):
                li = tensor.transpose(pv[:, c * P:(c + 1) * P], wqt[:, c * P:(c + 1) * P],
                                      cm["ident_bf"])
            li.then_inc(s_pe.inc(), 1)

        pe_r = s_pe.n

        @blk.vector
        def _(vector, r=r, j=j, pe_r=pe_r):
            vector.wait_ge(s_pe.h, pe_r)
            if r >= 2:
                vector.wait_ge(s_st.h, s_st.n)
            pv = b.psv((r % 2) * 3, 3, BF16)
            vector.tensor_copy(out=wqfm[j], in_=pv[:, 0:cols]).then_inc(s_dv.inc(), 1)

        dv_f = s_dv.n

        @blk.sync
        def _(sync, r=r, j=j, dv_f=dv_f):
            sync.wait_ge(s_dv.h, dv_f)
            sync.dma_start(out=dst[:, :, r * P:(r + 1) * P],
                           in_=wqfm[j].rearrange("p (c t) -> p c t", t=P)).then_inc(s_st.inc(16), 16)

    barrier(blk, sems)
    ar.pop()


def emit_qkv1(b, blk, cm, ag_out_d, w_sb_map, q_sb, k_sb, vt_sb):
    nc, ar, sems = b.nc, b.ar, b.sems
    s_ld, s_gp, s_pe, s_dv, s_ac, s_st = (sems["ld"], sems["gp"], sems["pe"],
                                          sems["dv"], sems["ac"], sems["st"])
    ar.push()
    xg_t = [ar.alloc([P, DC, 512], BF16, "qk_xg0"), ar.alloc([P, DC, 512], BF16, "qk_xg1")]
    agv = ag_out_d.rearrange("(r c p) t -> p r c t", p=P, c=DC)

    @blk.vector
    def _(vector):
        ov = vt_sb.rearrange("p (n g e) -> p n g e", g=HPC, e=65)
        vector.memset(ov[:, :, :, 64:65], 1.0).then_inc(s_dv.inc(), 1)

    HD2 = HPC * DH1

    for tt in range(SEQ // 512):
        j = tt % 2
        rnk = (tt * 512) // T_CORE
        toff = (tt * 512) % T_CORE
        ld_t = s_ld.n

        @blk.sync
        def _(sync, tt=tt, j=j, rnk=rnk, toff=toff):
            if tt >= 2:
                sync.wait_ge(s_pe.h, s_pe.n)
            sync.dma_start(out=xg_t[j], in_=agv[:, rnk, :, toff:toff + 512]).then_inc(s_ld.inc(16), 16)

        @blk.tensor
        def _(tensor, tt=tt, j=j, ld_t=ld_t):
            tensor.wait_ge(s_ld.h, ld_t + 16)
            for ci, wname in enumerate(("wq1", "wk1")):
                for c in range(2):
                    u = ci * 2 + c
                    bank = u % 2
                    if not (tt == 0 and u < 2):
                        tensor.wait_ge(s_ac.h, s_ac.n - 1)
                    pv = b.psv(bank, 1, F32)
                    for k in range(DC):
                        li = tensor.matmul(pv, w_sb_map[wname][:, k * HD2 + c * P: k * HD2 + (c + 1) * P],
                                           xg_t[j][:, k, :], start=(k == 0), stop=(k == DC - 1))
                    li.then_inc(s_pe.inc(), 1)
                    s_ac.n += 1
            s_ac.n -= 4

        @blk.scalar
        def _(scalar, tt=tt):
            for ci, dst in enumerate((q_sb, k_sb)):
                for c in range(2):
                    u = ci * 2 + c
                    bank = u % 2
                    scalar.wait_ge(s_pe.h, s_pe.n - 4 + u + 1)
                    scale = 1.0 / 8.0 if ci == 0 else 1.0
                    scalar.activation(out=dst[:, c * SEQ + tt * 512: c * SEQ + tt * 512 + 512],
                                      in_=b.psv(bank, 1, F32), func=AF.Copy,
                                      scale=scale).then_inc(s_ac.inc(), 1)

        @blk.tensor
        def _(tensor, tt=tt, j=j):
            for sb in range(4):
                bank = 4 + sb % 2
                if not (tt == 0 and sb < 2):
                    tensor.wait_ge(s_dv.h, s_dv.n - 1)
                pv = b.psv(bank, 1, F32, [P, HD2])
                for k in range(DC):
                    li = tensor.matmul(pv, xg_t[j][:, k, sb * P:(sb + 1) * P],
                                       w_sb_map["wv1"][:, k * HD2:(k + 1) * HD2],
                                       start=(k == 0), stop=(k == DC - 1))
                li.then_inc(s_pe.inc(), 1)
                s_dv.n += 1
            s_dv.n -= 4

        @blk.vector
        def _(vector, tt=tt):
            for sb in range(4):
                bank = 4 + sb % 2
                vector.wait_ge(s_pe.h, s_pe.n - 4 + sb + 1)
                n = tt * 4 + sb
                dst = vt_sb.rearrange("p (m g e) -> p m g e", g=HPC, e=65)[:, n, :, 0:64]
                vector.tensor_copy(out=dst,
                                   in_=b.psv(bank, 1, F32, [P, HD2]).rearrange(
                                       "p (g e) -> p g e", e=64)).then_inc(s_dv.inc(), 1)

    barrier(blk, sems)
    ar.pop()


def emit_flash1(b, blk, cm, q_sb, k_sb, vt_sb, o_sb):
    nc, ar, sems = b.nc, b.ar, b.sems
    s_pe, s_dv, s_ac = sems["pe"], sems["dv"], sems["ac"]
    ar.push()
    P_h = [[ar.alloc([P, 512], BF16, f"fl_P{h}_{z}") for z in range(2)] for h in range(2)]
    linv = ar.alloc([1, 512], F32, "fl_linv")
    O_t = ar.alloc([64, 512], F32, "fl_O")

    for pair in range(2):
        for qi in range(SEQ // 512):
            nkb = (qi + 1) * 4
            pe0, ac0 = s_pe.n, s_ac.n
            dv_prev = s_dv.n

            def emit_qk(kb):
                z = kb % 2
                diag = kb >= nkb - 4
                doff = (qi * 512) - (kb * P)

                @blk.tensor
                def _(tensor, pair=pair, qi=qi, kb=kb, z=z, diag=diag, doff=doff):
                    if kb >= 2:
                        tensor.wait_ge(s_ac.h, ac0 + 2 * kb - 2)
                    if kb == 0:
                        tensor.wait_ge(s_dv.h, dv_prev)
                    for h in range(2):
                        Sv = b.psv((0 if h == 0 else 2) + z, 1, F32)
                        lh = k_sb[h * 64:(h + 1) * 64,
                                  pair * SEQ + kb * P: pair * SEQ + (kb + 1) * P]
                        rh = q_sb[h * 64:(h + 1) * 64,
                                  pair * SEQ + qi * 512: pair * SEQ + (qi + 1) * 512]
                        li = tensor.matmul(Sv, lh, rh, start=True, stop=not diag,
                                           tile_position=(h * 64, 0))
                    if diag:
                        u_sl = cm["U_big_bf"][:, doff + 384: doff + 384 + 512]
                        for h in range(2):
                            Sv = b.psv((0 if h == 0 else 2) + z, 1, F32)
                            li = tensor.matmul(Sv, cm["negCI_bf"], u_sl, start=False, stop=True)
                    li.then_inc(s_pe.inc(), 1)

                pe_qk = s_pe.n

                @blk.scalar
                def _(scalar, pair=pair, qi=qi, kb=kb, z=z, pe_qk=pe_qk):
                    scalar.wait_ge(s_pe.h, pe_qk)
                    for h in range(2):
                        scalar.activation(out=P_h[h][z], in_=b.psv((0 if h == 0 else 2) + z, 1, F32),
                                          func=AF.Exp).then_inc(s_ac.inc(), 1)

            def emit_av(kb, ac_e):
                z = kb % 2

                @blk.tensor
                def _(tensor, pair=pair, qi=qi, kb=kb, z=z, nkb=nkb, ac_e=ac_e):
                    tensor.wait_ge(s_ac.h, ac_e)
                    for h in range(2):
                        g = pair * 2 + h
                        av = b.psv(6 + h, 1, F32, [65, 512])
                        li = tensor.matmul(av, vt_sb[:, kb * 260 + g * 65: kb * 260 + (g + 1) * 65],
                                           P_h[h][z], start=(kb == 0), stop=(kb == nkb - 1))
                    li.then_inc(s_pe.inc(), 1)

            ac_es = {}
            for kb in range(nkb):
                emit_qk(kb)
                ac_es[kb] = s_ac.n
                if kb >= 1:
                    emit_av(kb - 1, ac_es[kb - 1])
            emit_av(nkb - 1, ac_es[nkb - 1])

            for h in range(2):
                pe_av = s_pe.n
                dv_mark = s_dv.n

                @blk.vector
                def _(vector, h=h, pe_av=pe_av):
                    vector.wait_ge(s_pe.h, pe_av)
                    av = b.psv(6 + h, 1, F32, [65, 512])
                    vector.reciprocal(out=linv, in_=av[64:65, :]).then_inc(s_dv.inc(), 1)

                dv_r = s_dv.n

                @blk.tensor
                def _(tensor, h=h, dv_r=dv_r):
                    tensor.wait_ge(s_dv.h, dv_r)
                    tensor.matmul(b.psv(4, 1, F32, [64, 512]), cm["ones_col_f32"][:, 0:64],
                                  linv, start=True, stop=True).then_inc(s_pe.inc(), 1)

                pe_bc = s_pe.n

                @blk.scalar
                def _(scalar, h=h, pe_bc=pe_bc, dv_mark=dv_mark):
                    scalar.wait_ge(s_pe.h, pe_bc)
                    scalar.wait_ge(s_dv.h, dv_mark)
                    av = b.psv(6 + h, 1, F32, [65, 512])
                    scalar.activation(out=O_t, in_=av[0:64, :], func=AF.Copy).then_inc(s_ac.inc(), 1)

                ac_o = s_ac.n

                @blk.vector
                def _(vector, pair=pair, qi=qi, h=h, ac_o=ac_o, pe_bc=pe_bc):
                    vector.wait_ge(s_ac.h, ac_o)
                    vector.wait_ge(s_pe.h, pe_bc)
                    dst = o_sb[h * 64:(h + 1) * 64,
                               pair * SEQ + qi * 512: pair * SEQ + (qi + 1) * 512]
                    vector.tensor_tensor(out=dst, in0=O_t, in1=b.psv(4, 1, F32, [64, 512]),
                                         op=ALU.mult).then_inc(s_dv.inc(), 1)

    barrier(blk, sems)
    ar.pop()


def emit_outproj1(b, blk, cm, o_sb, wo_sb, rs_in_d):
    nc, ar, sems = b.nc, b.ar, b.sems
    s_pe, s_dv, s_ac, s_st = sems["pe"], sems["dv"], sems["ac"], sems["st"]
    ar.push()
    po_t = [ar.alloc([P, 512], F32, "op_t0"), ar.alloc([P, 512], F32, "op_t1")]
    rsv = rs_in_d.rearrange("(r c p) t -> p r c t", p=P, c=DC)

    for tt in range(SEQ // 512):
        rnk = (tt * 512) // T_CORE
        toff = (tt * 512) % T_CORE
        for c in range(DC):
            u = tt * DC + c
            bank = u % 2
            jj = u % 2

            @blk.tensor
            def _(tensor, tt=tt, c=c, u=u, bank=bank):
                if u >= 2:
                    tensor.wait_ge(s_ac.h, s_ac.n - 1)
                pv = b.psv(bank, 1, F32)
                for k in range(2):
                    li = tensor.matmul(pv, wo_sb[:, k * D + c * P: k * D + (c + 1) * P],
                                       o_sb[:, k * SEQ + tt * 512:k * SEQ + (tt + 1) * 512],
                                       start=(k == 0), stop=(k == 1))
                li.then_inc(s_pe.inc(), 1)

            pe_o = s_pe.n

            @blk.scalar
            def _(scalar, u=u, bank=bank, jj=jj, pe_o=pe_o):
                scalar.wait_ge(s_pe.h, pe_o)
                if u >= 2:
                    scalar.wait_ge(s_st.h, s_st.n - 16)
                scalar.activation(out=po_t[jj], in_=b.psv(bank, 1, F32),
                                  func=AF.Copy).then_inc(s_ac.inc(), 1)

            ac_o = s_ac.n

            @blk.sync
            def _(sync, rnk=rnk, toff=toff, c=c, jj=jj, ac_o=ac_o):
                sync.wait_ge(s_ac.h, ac_o)
                sync.dma_start(out=rsv[:, rnk, c, toff:toff + 512],
                               in_=po_t[jj]).then_inc(s_st.inc(16), 16)

    barrier(blk, sems)
    ar.pop()


def build_program(stages=("A", "B")):
    nc = bass.Bass("TRN2", target_bir_lowering=False)
    b = Builder(nc)
    ar, sems = b.ar, b.sems
    s_ld, s_gp, s_cc = sems["ld"], sems["gp"], sems["cc"]

    x_in = b.din("x_slice", [T_CORE, D], F32)
    wdram = {}
    for n in ("qkwT", "vwT", "wqT", "wkT", "wvT", "owT"):
        wdram[n] = b.din(n, [D, D], F32)
    bias_in = {}
    for n in ("qk_b", "v_b", "q_b", "k_b", "ob_b", "bv_row"):
        bias_in[n] = b.din(n, [1, D], F32)
    w0 = {"wg": b.din("wg0T", [D, FFN0_H], I32),
          "wu": b.din("wu0T", [D, FFN0_H], I32),
          "wd": b.din("wd0T", [FFN0_H, D], I32)}
    wg1 = b.din("wg1", [FFN1_H, D], F32)
    wu1 = b.din("wu1", [FFN1_H, D], F32)
    wd1 = b.din("wd1", [D, FFN1_H], F32)
    wq1 = b.din("wq1", [D, HPC * DH1], F32)
    wk1 = b.din("wk1", [D, HPC * DH1], F32)
    wv1 = b.din("wv1", [D, HPC * DH1], F32)
    wo1 = b.din("wo1", [HPC * DH1, D], F32)
    cin = {}
    cdefs = [("ident_bf", [P, P], BF16), ("ident_f32", [P, P], F32),
             ("negCI_bf", [P, P], BF16), ("maskA_bf", [P, P], BF16),
             ("U_big_bf", [P, 896], BF16), ("ones_row_f32", [1, 512], F32),
             ("ones_col_f32", [1, P], F32), ("ones_col_bf", [P, 1], BF16),
             ("maskpos", [P, 1], I32), ("masksign", [P, 1], I32),
             ("eps_col", [P, 1], F32)]
    for n, sh, dt in cdefs:
        cin[n] = b.din(n, sh, dt)

    only_a = "A" in stages and "B" not in stages
    only_b = "B" in stages and "A" not in stages
    if "B" in stages:
        out_d = b.dout("out_slice", [T_CORE, D], F32)
    if only_a:
        x1_d = b.dout("x1_slice", [T_CORE, D], F32)
    elif only_b:
        x1_d = b.din("x1_slice", [T_CORE, D], F32)
    else:
        x1_d = b.dint("x1_buf", [T_CORE, D], F32)
    mirrors = []

    def dtap(name, shape, dtype):
        t = b.dint(name, shape, dtype)
        if name in DEBUG_TAPS:
            mirrors.append((b.dout(name + "_tap", shape, dtype), t))
        return t

    def emit_mirrors(blk):
        if not mirrors:
            return
        @blk.sync
        def _(sync):
            for dst, srct in mirrors:
                sync.dma_start(out=dst, in_=srct).then_inc(b.sems["st"].inc(16), 16)
        barrier(blk, b.sems)
    xn_d = dtap("xn_buf", [D, T_CORE], BF16)
    ds_d = dtap("ds_buf", [D, T_CORE], F32)
    if "A" in stages and ("xn_buf" in DEBUG_TAPS or "ds_buf" in DEBUG_TAPS):
        pass
    if "B" in stages:
        xq_d = dtap("xq_buf", [D, T_CORE], BF16)
        ag_out_d = dtap("ag_out", [4 * D, T_CORE], BF16)
        rs_in_d = dtap("rs_in", [4 * D, T_CORE], F32)
        rs_out_d = dtap("rs_out", [D, T_CORE], F32)
        w1q = {"wg": dtap("wg1T_q", [D, FFN1_H], BF16),
               "wu": dtap("wu1T_q", [D, FFN1_H], BF16),
               "wd": dtap("wd1T_q", [FFN1_H, D], BF16)}

    cm = {n: ar.alloc(sh, dt, "c_" + n) for n, sh, dt in cdefs}
    for n in ("qk_b", "v_b", "q_b", "k_b", "ob_b", "bv_row"):
        cm[n] = ar.alloc([1, D], F32, "c_" + n)

    with nc.Block() as blk:
        @blk.sync
        def _(sync):
            for n, sh, dt in cdefs:
                sync.dma_start(out=cm[n], in_=cin[n]).then_inc(s_ld.inc(16), 16)
            for n in ("qk_b", "v_b", "q_b", "k_b", "ob_b", "bv_row"):
                sync.dma_start(out=cm[n], in_=bias_in[n]).then_inc(s_ld.inc(16), 16)

        barrier(blk, sems)

        if "A" in stages:
            emit_ln_quant(b, blk, "A", x_in, xn_d, cm, dbg_d=(b.dout("ln_dbg", [P, 8*16], F32) if "ln_dbg" in DEBUG_TAPS else None))
            ar.push()
            out0 = ar.alloc([P, DC * T_CORE], BF16, "out0")
            if "attn" in A_PHASES:
                emit_attn0(b, blk, cm, xn_d, wdram, out0)
                if "out0" in DEBUG_TAPS:
                    o0d = b.dout("out0", [P, DC * T_CORE], BF16)

                    @blk.sync
                    def _(sync):
                        sync.dma_start(out=o0d, in_=out0).then_inc(b.sems["st"].inc(16), 16)
                    barrier(blk, b.sems)
            if "ffn" in A_PHASES:
                emit_ffn_combine(b, blk, cm, "A", xn_d, w0, x_in, x1_d, out0, ds_d)
            else:
                @blk.sync
                def _(sync):
                    sync.dma_start(out=x1_d[0:1, 0:8],
                                   in_=cm["ones_row_f32"][0:1, 0:8]).then_inc(b.sems["st"].inc(16), 16)
                    sync.wait_ge(b.sems["st"].h, b.sems["st"].n)
            ar.pop()

        if "B" in stages:
            ar.push()
            fp4tmp_w = alloc_fp4_tmp(ar, 1408, cm, sems["dv"])
            emit_fp4_weight_quant(b, blk, cm, wg1, w1q["wg"], FFN1_H, D, fp4tmp_w)
            emit_fp4_weight_quant(b, blk, cm, wu1, w1q["wu"], FFN1_H, D, fp4tmp_w)
            emit_fp4_weight_quant(b, blk, cm, wd1, w1q["wd"], D, FFN1_H, fp4tmp_w)
            ar.pop()
            ar.push()
            fp4tmp_x = alloc_fp4_tmp(ar, D, cm, sems["dv"])
            emit_ln_quant(b, blk, "B", x1_d, xn_d, cm, xq_fm=xq_d, fp4tmp=fp4tmp_x)
            ar.pop()
            @blk.gpsimd
            def _(gpsimd):
                gpsimd.collective_compute(
                    "AllGather", ALU.bypass,
                    replica_groups=[[0, 1, 2, 3], [4, 5, 6, 7]],
                    ins=[xn_d], outs=[ag_out_d],
                ).then_inc(s_cc.inc(), 1)
                gpsimd.wait_ge(s_cc.h, s_cc.n)
            barrier(blk, sems)
            ar.push()
            w1sb = {}
            for n, src in (("wq1", wq1), ("wk1", wk1), ("wv1", wv1)):
                w1sb[n] = ar.alloc([P, DC * HPC * DH1], BF16, "w_" + n)
            wo_sb = ar.alloc([P, 2 * D], BF16, "w_wo1")

            @blk.gpsimd
            def _(gpsimd):
                for n, src in (("wq1", wq1), ("wk1", wk1), ("wv1", wv1)):
                    gpsimd.dma_start(out=w1sb[n].rearrange("p (k o) -> p k o", o=HPC * DH1),
                                     in_=src.rearrange("(k p) o -> p k o", p=P)).then_inc(s_gp.inc(16), 16)
                gpsimd.dma_start(out=wo_sb.rearrange("p (k o) -> p k o", o=D),
                                 in_=wo1.rearrange("(k p) o -> p k o", p=P)).then_inc(s_gp.inc(16), 16)
            barrier(blk, sems)

            q_sb = ar.alloc([P, 2 * SEQ], BF16, "q_sb")
            k_sb = ar.alloc([P, 2 * SEQ], BF16, "k_sb")
            vt_sb = ar.alloc([P, (SEQ // P) * 260], BF16, "vt_sb")
            o_sb = ar.alloc([P, 2 * SEQ], BF16, "o_sb")
            emit_qkv1(b, blk, cm, ag_out_d, w1sb, q_sb, k_sb, vt_sb)
            emit_flash1(b, blk, cm, q_sb, k_sb, vt_sb, o_sb)
            emit_outproj1(b, blk, cm, o_sb, wo_sb, rs_in_d)
            ar.pop()
            @blk.gpsimd
            def _(gpsimd):
                gpsimd.collective_compute(
                    "ReduceScatter", ALU.add,
                    replica_groups=[[0, 1, 2, 3], [4, 5, 6, 7]],
                    ins=[rs_in_d], outs=[rs_out_d],
                ).then_inc(s_cc.inc(), 1)
                gpsimd.wait_ge(s_cc.h, s_cc.n)
            barrier(blk, sems)
            ar.push()
            fp4tmp_h = alloc_fp4_tmp(ar, FFN1_H // 2, cm, sems["dv"])
            emit_ffn_combine(b, blk, cm, "B", None, w1q, x1_d, out_d, rs_out_d, ds_d,
                             xq_fm=xq_d, fp4tmp=fp4tmp_h)
            ar.pop()
        emit_mirrors(blk)

    nc._input_names = list(b.input_names)
    return nc


A_PHASES = ("attn", "ffn")
DEBUG_TAPS = ()

_PROG_CACHE = {}


def _get_prog(stages):
    key = tuple(stages)
    if key not in _PROG_CACHE:
        _PROG_CACHE[key] = build_program(stages)
    return _PROG_CACHE[key]


FUSED = True


def _run(nc, per_core):
    names = set(nc._input_names)
    maps = [{k: v for k, v in m.items() if k in names} for m in per_core]
    return run_bass_kernel_spmd(nc, maps, core_ids=list(range(NCORES)))


def kernel(**inputs):
    per_core = prep_inputs(inputs)
    if FUSED:
        nc = _get_prog(("A", "B"))
        res = _run(nc, per_core)
        out = np.concatenate([res.results[r]["out_slice"] for r in range(NCORES)], axis=0)
        return out.reshape(2, SEQ, D)
    nca = _get_prog(("A",))
    res_a = _run(nca, per_core)
    for r in range(NCORES):
        per_core[r]["x1_slice"] = np.ascontiguousarray(res_a.results[r]["x1_slice"])
    ncb = _get_prog(("B",))
    res_b = _run(ncb, per_core)
    out = np.concatenate([res_b.results[r]["out_slice"] for r in range(NCORES)], axis=0)
    return out.reshape(2, SEQ, D)


# version 04
# speedup vs baseline: 1.2105x; 1.1709x over previous
import math
import numpy as np
import ml_dtypes
import concourse.bass as bass
import concourse.mybir as mybir
from concourse.bass_utils import run_bass_kernel_spmd

F32 = mybir.dt.float32
BF16 = mybir.dt.bfloat16
I32 = mybir.dt.int32
U8 = mybir.dt.uint8
AF = mybir.ActivationFunctionType
ALU = mybir.AluOpType
AX = mybir.AxisListType

P = 128
D = 1024
DC = D // P
T_CORE = 2048
SEQ = 8192
NCORES = 8
H0 = 8
FFN0_H = 3072
FFN1_H = 2816
H1 = 16
DH1 = 64
HPC = 4
MASK_C = 1000.0

bf16 = ml_dtypes.bfloat16



def _pp(b):
    return np.ascontiguousarray(b.reshape(-1, P).T.astype(np.float32))


def _row(b):
    return np.ascontiguousarray(b.reshape(1, -1).astype(np.float32))


def make_constants():
    c = {}
    c["ident_bf"] = np.eye(P, dtype=np.float32).astype(bf16)
    c["ident_f32"] = np.eye(P, dtype=np.float32)
    c["negCI_bf"] = (-MASK_C * np.eye(P, dtype=np.float32)).astype(bf16)
    kk, qq = np.meshgrid(np.arange(P), np.arange(P), indexing="ij")
    sameseg = (kk // 4) == (qq // 4)
    a0 = (~(sameseg & (kk != qq))).astype(np.float32)
    c["maskA_bf"] = a0.astype(bf16)
    ii = np.arange(P)[:, None]
    cp = np.arange(896)[None, :]
    c["U_big_bf"] = (ii > (cp - 384)).astype(np.float32).astype(bf16)
    c["ones_row_f32"] = np.ones((1, 512), dtype=np.float32)
    c["ones_col_f32"] = np.ones((1, P), dtype=np.float32)
    c["ones_col_bf"] = np.ones((P, 1), dtype=np.float32).astype(bf16)
    c["maskpos"] = np.full((P, 1), 0x7FFFFFFF, dtype=np.int32)
    c["eps_col"] = np.full((P, 1), 1e-5, dtype=np.float32)
    c["masksign"] = np.full((P, 1), -2147483648, dtype=np.int32)
    return c


def prep_inputs(inputs):
    g = {}
    x = np.ascontiguousarray(np.asarray(inputs["L0_attn_x"], dtype=np.float32)
                             .reshape(16384, D))
    g["qkwT"] = np.ascontiguousarray(inputs["L0_attn_proj_qk_w"].T.astype(np.float32))
    g["vwT"] = np.ascontiguousarray(inputs["L0_attn_proj_v_w"].T.astype(np.float32))
    wq, wk, wv = np.split(np.asarray(inputs["L0_attn_in_proj_w"]), 3, axis=0)
    g["wqT"] = np.ascontiguousarray(wq.T.astype(np.float32))
    g["wkT"] = np.ascontiguousarray(wk.T.astype(np.float32))
    g["wvT"] = np.ascontiguousarray(wv.T.astype(np.float32))
    g["owT"] = np.ascontiguousarray(inputs["L0_attn_out_proj_w"].T.astype(np.float32))
    bq, bk, bv = np.split(np.asarray(inputs["L0_attn_in_proj_b"]), 3)
    g["qk_b"] = _row(np.asarray(inputs["L0_attn_proj_qk_b"]))
    g["v_b"] = _row(np.asarray(inputs["L0_attn_proj_v_b"]))
    g["q_b"] = _row(bq)
    g["k_b"] = _row(bk)
    g["bv_row"] = _row(bv)
    g["ob_b"] = _row(np.asarray(inputs["L0_attn_out_proj_b"]))
    g["wg0T"] = np.ascontiguousarray(np.asarray(inputs["L0_ffn_w_gate"]).T)
    g["wu0T"] = np.ascontiguousarray(np.asarray(inputs["L0_ffn_w_up"]).T)
    g["wd0T"] = np.ascontiguousarray(np.asarray(inputs["L0_ffn_w_down"]).T)
    g["wg1"] = np.ascontiguousarray(np.asarray(inputs["L1_ffn_w_gate"], dtype=np.float32))
    g["wu1"] = np.ascontiguousarray(np.asarray(inputs["L1_ffn_w_up"], dtype=np.float32))
    g["wd1"] = np.ascontiguousarray(np.asarray(inputs["L1_ffn_w_down"], dtype=np.float32))
    g.update(make_constants())

    Wq = np.asarray(inputs["L1_attn_Wq"], dtype=np.float32)
    Wk = np.asarray(inputs["L1_attn_Wk"], dtype=np.float32)
    Wv = np.asarray(inputs["L1_attn_Wv"], dtype=np.float32)
    Wo = np.asarray(inputs["L1_attn_Wout"], dtype=np.float32)

    per_core = []
    for r in range(NCORES):
        m = dict(g)
        m["x_slice"] = np.ascontiguousarray(x[r * T_CORE:(r + 1) * T_CORE])
        hs = 4 * (r % 4)
        m["wq1"] = np.ascontiguousarray(Wq[:, hs:hs + HPC, :].reshape(D, HPC * DH1))
        m["wk1"] = np.ascontiguousarray(Wk[:, hs:hs + HPC, :].reshape(D, HPC * DH1))
        m["wv1"] = np.ascontiguousarray(Wv[:, hs:hs + HPC, :].reshape(D, HPC * DH1))
        m["wo1"] = np.ascontiguousarray(Wo[hs:hs + HPC].reshape(HPC * DH1, D))
        per_core.append(m)
    return per_core



class Arena:

    def __init__(self, nc, nbytes, name="arena"):
        self.nc = nc
        self.t = nc.alloc_sbuf_tensor(name, [P, nbytes], U8)
        self.nbytes = nbytes
        self.off = 0
        self.marks = []

    def alloc(self, shape, dtype, name=None):
        esz = mybir.dt.size(dtype)
        free = 1
        for s in shape[1:]:
            free *= s
        nb = free * esz
        off = (self.off + 31) & ~31
        assert off + nb <= self.nbytes, f"arena overflow: {name} {off}+{nb}"
        self.off = off + nb
        v = self.t[0:shape[0], off:off + nb].bitcast(dtype)
        if len(shape) > 2:
            pat = " ".join(f"d{i}" for i in range(1, len(shape)))
            v = v.rearrange(f"p ({pat}) -> p {pat}",
                            **{f"d{i}": shape[i] for i in range(1, len(shape))})
        return v

    def push(self):
        self.marks.append(self.off)

    def pop(self):
        self.off = self.marks.pop()


class Sem:

    def __init__(self, nc, name):
        self.h = nc.alloc_semaphore(name) if hasattr(nc, "alloc_semaphore") else None
        if self.h is None:
            self.cm = nc.semaphore(name)
            self.h = self.cm.__enter__()
        self.n = 0

    def inc(self, k=1):
        self.n += k
        return self.h


def barrier(block, sems):
    targets = [(sems[k], sems[k].n) for k in ("pe", "dv", "ac", "gp", "ld", "st", "cc")]
    for e in ("tensor", "vector", "scalar", "gpsimd", "sync"):
        sec = getattr(block, e)

        @sec
        def _(eng):
            for s, v in targets:
                if v > 0:
                    eng.wait_ge(s.h, v)



def emit_fp4_quant(eng, x, out_bf, tmp, nblk, nparts=P, scale_src=None):
    n = nblk * 16
    bmax = tmp["bmax"][0:nparts, 0:nblk]
    scale = tmp["scale"][0:nparts, 0:nblk]
    rscale = tmp["rscale"][0:nparts, 0:nblk]
    t = tmp["t"][0:nparts, 0:n]
    ta = tmp["ta"][0:nparts, 0:n]
    sb_t = tmp["sb"][0:nparts, 0:n]
    y1 = tmp["y1"][0:nparts, 0:n]
    y2 = tmp["y2"][0:nparts, 0:n]
    y3 = tmp["y3"][0:nparts, 0:n]
    c1 = tmp["c1"][0:nparts, 0:n]
    c2 = tmp["c2"][0:nparts, 0:n]
    c3 = tmp["c3"][0:nparts, 0:n]
    s12 = tmp["s12"][0:nparts, 0:n]
    qn = tmp["qn"][0:nparts, 0:n]
    maskpos, masksign = tmp["maskpos"], tmp["masksign"]
    xv = x.rearrange("p (b k) -> p b k", k=16)
    sw = tmp["selfwait"]
    sw(eng, eng.tensor_reduce(out=bmax, in_=xv, axis=AX.X, op=ALU.max,
                              apply_absolute_value=True))
    sw(eng, eng.tensor_scalar(out=scale, in0=bmax, scalar1=1.0 / 6.0, scalar2=1e-12,
                              op0=ALU.mult, op1=ALU.max))
    sw(eng, eng.reciprocal(out=rscale, in_=scale))
    rs_exp = rscale.unsqueeze(2).broadcast_to([nparts, nblk, 16])
    sc_exp = scale.unsqueeze(2).broadcast_to([nparts, nblk, 16])
    eng.tensor_tensor(out=t.rearrange("p (b k) -> p b k", k=16), in0=xv,
                      in1=rs_exp, op=ALU.mult)
    eng.tensor_scalar(out=ta.bitcast(I32), in0=t.bitcast(I32),
                      scalar1=maskpos[0:nparts], scalar2=None, op0=ALU.bitwise_and)
    eng.tensor_scalar(out=y1, in0=ta, scalar1=2.0, scalar2=128.0, op0=ALU.mult, op1=ALU.add)
    eng.tensor_scalar(out=y2, in0=ta, scalar1=128.0, scalar2=None, op0=ALU.add)
    eng.tensor_scalar(out=y3, in0=ta, scalar1=0.5, scalar2=128.0, op0=ALU.mult, op1=ALU.add)
    eng.tensor_scalar(out=c1, in0=y1, scalar1=132.0, scalar2=0.5, op0=ALU.min, op1=ALU.mult)
    eng.tensor_scalar(out=c2, in0=y2, scalar1=130.0, scalar2=132.0, op0=ALU.max, op1=ALU.min)
    eng.tensor_scalar(out=c3, in0=y3, scalar1=130.0, scalar2=2.0, op0=ALU.max, op1=ALU.mult)
    eng.tensor_tensor(out=s12, in0=c1, in1=c2, op=ALU.add)
    eng.scalar_tensor_tensor(out=qn, in0=c3, scalar=-454.0, in1=s12,
                             op0=ALU.add, op1=ALU.add)
    eng.tensor_scalar(out=sb_t.bitcast(I32), in0=t.bitcast(I32),
                      scalar1=masksign[0:nparts], scalar2=None, op0=ALU.bitwise_and)
    eng.tensor_tensor(out=qn.bitcast(I32), in0=qn.bitcast(I32),
                      in1=sb_t.bitcast(I32), op=ALU.bitwise_or)
    return eng.tensor_tensor(out=out_bf.rearrange("p (b k) -> p b k", k=16),
                              in0=qn.rearrange("p (b k) -> p b k", k=16),
                              in1=sc_exp, op=ALU.mult)


FP4_TMP_OPS = 18


def alloc_fp4_tmp(ar, n, cmaps, sem=None):
    tmp = {}
    if sem is not None:
        def sw(eng, inst, sem=sem):
            inst.then_inc(sem.inc(), 1)
            eng.wait_ge(sem.h, sem.n)
        tmp["selfwait"] = sw
    tmp["bmax"] = ar.alloc([P, n // 16], F32, "q_bmax")
    tmp["scale"] = ar.alloc([P, n // 16], F32, "q_scale")
    tmp["rscale"] = ar.alloc([P, n // 16], F32, "q_rscale")
    for k in ("t", "ta", "s12", "qn"):
        tmp[k] = ar.alloc([P, n], F32, "q_" + k)
    tmp["sb"] = tmp["ta"]
    for k in ("y1", "y2", "y3", "c1", "c2", "c3"):
        tmp[k] = ar.alloc([P, n], BF16, "q_" + k)
    tmp["maskpos"] = cmaps["maskpos"]
    tmp["masksign"] = cmaps["masksign"]
    return tmp




class Builder:
    def __init__(self, nc):
        self.nc = nc
        self.ar = Arena(nc, 207 * 1024)
        self.psall = nc.alloc_psum_tensor("psall", [P, 4096], F32)
        self.input_names = []
        self.sems = {}
        for name in ("ld", "gp", "pe", "dv", "ac", "st", "bar", "cc", "sc"):
            self.sems[name] = Sem(nc, "s_" + name)

    def psv(self, bank, nbanks=1, dtype=F32, shape=None):
        v = self.psall[:, bank * 512:(bank + nbanks) * 512]
        if dtype != F32:
            v = v.bitcast(dtype)
        if shape is not None:
            assert len(shape) == 2
            v = v[0:shape[0], 0:shape[1]]
        return v

    def din(self, name, shape, dtype):
        self.input_names.append(name)
        return self.nc.dram_tensor(name, list(shape), dtype, kind="ExternalInput").ap()

    def dout(self, name, shape, dtype):
        return self.nc.dram_tensor(name, list(shape), dtype, kind="ExternalOutput").ap()

    def dint(self, name, shape, dtype):
        return self.nc.dram_tensor(name, list(shape), dtype, kind="Internal").ap()


def fm_view(dram_ap):
    return dram_ap.rearrange("(c p) t -> p c t", p=P)


def emit_const_loads(b, blk, cm, loads):
    s_ld = b.sems["ld"]

    @blk.sync
    def _(sync):
        for name, src in loads:
            sync.dma_start(out=cm[name], in_=src).then_inc(s_ld.inc(16), 16)


def emit_ln_quant(b, blk, stage, src_d, xn_d, cm, xq_fm=None, fp4tmp=None, dbg_d=None):
    nc, ar, sems = b.nc, b.ar, b.sems
    s_ld, s_pe, s_dv, s_ac, s_st = sems["ld"], sems["pe"], sems["dv"], sems["ac"], sems["st"]
    NT = T_CORE // P
    ar.push()
    x_t = [ar.alloc([P, D], F32, "ln_x0"), ar.alloc([P, D], F32, "ln_x1")]
    st6 = [ar.alloc([P, 12], F32, "ln_st0"), ar.alloc([P, 12], F32, "ln_st1")]
    sq4 = ar.alloc([P, 4], F32, "ln_sq4")
    mg4 = ar.alloc([P, 4], F32, "ln_mg4")
    M24 = ar.alloc([P, 4], F32, "ln_M24")
    ex2a = ar.alloc([P, 4], F32, "ln_ex2a")
    esum2 = ar.alloc([P, 1], F32, "ln_es2")
    msq2 = ar.alloc([P, 1], F32, "ln_msq2")
    ex2 = ar.alloc([P, 4], F32, "ln_ex2")
    msum = [ar.alloc([P, 1], F32, "ln_ms0"), ar.alloc([P, 1], F32, "ln_ms1")]
    esum = ar.alloc([P, 1], F32, "ln_es")
    msq = ar.alloc([P, 1], F32, "ln_msq")
    var = [ar.alloc([P, 1], F32, "ln_va0"), ar.alloc([P, 1], F32, "ln_va1")]
    mneg = [ar.alloc([P, 1], F32, "ln_mn0"), ar.alloc([P, 1], F32, "ln_mn1")]
    lv = [ar.alloc([P, 1], F32, "ln_lv0"), ar.alloc([P, 1], F32, "ln_lv1")]
    rstd = [ar.alloc([P, 1], F32, "ln_rs0"), ar.alloc([P, 1], F32, "ln_rs1")]
    nm = [ar.alloc([P, 1], F32, "ln_nm0"), ar.alloc([P, 1], F32, "ln_nm1")]
    xn_bf = [ar.alloc([P, D], BF16, "ln_xb0"), ar.alloc([P, D], BF16, "ln_xb1")]
    quant = xq_fm is not None
    if quant:
        xn_f = [ar.alloc([P, D], F32, "ln_xf0"), ar.alloc([P, D], F32, "ln_xf1")]
        xq_t = [ar.alloc([P, D], BF16, "ln_xq0"), ar.alloc([P, D], BF16, "ln_xq1")]
        xq_t2 = [ar.alloc([P, D], BF16, "ln_xr0"), ar.alloc([P, D], BF16, "ln_xr1")]
    fm_t = [ar.alloc([P, D], BF16, "ln_fm0"), ar.alloc([P, D], BF16, "ln_fm1")]

    xv = src_d.rearrange("(n p) d -> n p d", p=P)
    xnv = fm_view(xn_d)
    xqv = fm_view(xq_fm) if quant else None

    mk = [{} for _ in range(NT)]

    for i in range(NT):
        j = i % 2

        @blk.sync
        def _(sync, i=i, j=j):
            if i >= 2:
                sync.wait_ge(s_dv.h, mk[i - 2]["x_done"])
            sync.dma_start(out=x_t[j], in_=xv[i]).then_inc(s_ld.inc(16), 16)

        ld_i = s_ld.n

        @blk.vector
        def _(vector, i=i, j=j, ld_i=ld_i):
            vector.wait_ge(s_ld.h, ld_i)
            if i >= 2:
                vector.wait_ge(s_ac.h, mk[i - 2]["stats_done"])
            def step(inst):
                inst.then_inc(s_dv.inc(), 1)
                vector.wait_ge(s_dv.h, s_dv.n)
            step(vector.bn_stats(out=st6[j][:, 0:6], in_=x_t[j][:, 0:512]))
            step(vector.bn_stats(out=st6[j][:, 6:12], in_=x_t[j][:, 512:1024]))
            mg0 = st6[j].rearrange("p (g k) -> p g k", k=3)[:, :, 1]
            M2g0 = st6[j].rearrange("p (g k) -> p g k", k=3)[:, :, 2]
            step(vector.tensor_copy(out=mg4, in_=mg0))
            step(vector.tensor_copy(out=M24, in_=M2g0))
            step(vector.tensor_tensor(out=sq4, in0=mg4, in1=mg4, op=ALU.mult))
            step(vector.tensor_scalar(out=ex2a, in0=M24, scalar1=1.0 / 256.0,
                                      scalar2=None, op0=ALU.mult))
            step(vector.tensor_tensor(out=ex2, in0=ex2a, in1=sq4, op=ALU.add))
            step(vector.reduce_sum(out=msum[j], in_=mg4, axis=AX.X))
            step(vector.reduce_sum(out=esum, in_=ex2, axis=AX.X))
            step(vector.tensor_scalar(out=mneg[j], in0=msum[j], scalar1=-0.25,
                                      scalar2=None, op0=ALU.mult))
            step(vector.tensor_tensor(out=msq, in0=msum[j], in1=msum[j], op=ALU.mult))
            step(vector.tensor_scalar(out=esum2, in0=esum, scalar1=0.25, scalar2=None,
                                      op0=ALU.mult))
            step(vector.tensor_scalar(out=msq2, in0=msq, scalar1=-0.0625, scalar2=None,
                                      op0=ALU.mult))
            step(vector.tensor_tensor(out=var[j], in0=esum2, in1=msq2, op=ALU.add))

        mk[i]["stats"] = s_dv.n

        @blk.scalar
        def _(scalar, i=i, j=j):
            scalar.wait_ge(s_dv.h, mk[i]["stats"])
            scalar.activation(out=lv[j], in_=var[j], func=AF.Ln, bias=cm["eps_col"][:, 0:1]).then_inc(s_ac.inc(), 1)
            scalar.activation(out=rstd[j], in_=lv[j], func=AF.Exp, scale=-0.5).then_inc(s_ac.inc(), 1)
        mk[i]["stats_done"] = s_ac.n

        if dbg_d is not None:
            @blk.sync
            def _(sync, i=i, j=j):
                sync.wait_ge(s_ac.h, mk[i]["stats_done"])
                for col, t_ in enumerate([msum[j], esum, esum2, msq, msq2, var[j], mneg[j], rstd[j]]):
                    sync.dma_start(out=dbg_d[:, i * 8 + col:i * 8 + col + 1],
                                   in_=t_[:, 0:1]).then_inc(s_st.inc(16), 16)

        @blk.vector
        def _(vector, i=i, j=j):
            vector.wait_ge(s_ac.h, mk[i]["stats_done"])
            vector.tensor_scalar(out=nm[j], in0=mneg[j], scalar1=rstd[j][:, 0:1],
                                 scalar2=None, op0=ALU.mult).then_inc(s_dv.inc(), 1)
            vector.wait_ge(s_dv.h, s_dv.n)
            if i >= 2:
                vector.wait_ge(s_pe.h, mk[i - 2]["fm_tr"])
            vector.tensor_scalar(out=xn_bf[j], in0=x_t[j], scalar1=rstd[j][:, 0:1],
                                 scalar2=nm[j][:, 0:1], op0=ALU.mult,
                                 op1=ALU.add).then_inc(s_dv.inc(), 1)
            if quant:
                vector.tensor_scalar(out=xn_f[j], in0=x_t[j], scalar1=rstd[j][:, 0:1],
                                     scalar2=nm[j][:, 0:1], op0=ALU.mult,
                                     op1=ALU.add).then_inc(s_dv.inc(), 1)
        mk[i]["x_done"] = s_dv.n

        @blk.tensor
        def _(tensor, i=i, j=j):
            tensor.wait_ge(s_dv.h, mk[i]["x_done"] - (1 if quant else 0))
            if i >= 2:
                tensor.wait_ge(s_dv.h, mk[i - 2]["fm_evac"])
            pv = b.psv(6 + j, 1, BF16)
            for c in range(DC):
                li = tensor.transpose(pv[:, c * P:(c + 1) * P], xn_bf[j][:, c * P:(c + 1) * P],
                                      cm["ident_bf"])
            li.then_inc(s_pe.inc(), 1)
        mk[i]["fm_tr"] = s_pe.n

        @blk.vector
        def _(vector, i=i, j=j):
            vector.wait_ge(s_pe.h, mk[i]["fm_tr"])
            if i >= 2:
                vector.wait_ge(s_st.h, mk[i - 2]["fm_store"])
            pv = b.psv(6 + j, 1, BF16)
            vector.tensor_copy(out=fm_t[j], in_=pv).then_inc(s_dv.inc(), 1)
        mk[i]["fm_evac"] = s_dv.n

        @blk.sync
        def _(sync, i=i, j=j):
            sync.wait_ge(s_dv.h, mk[i]["fm_evac"])
            sync.dma_start(out=xnv[:, :, i * P:(i + 1) * P],
                           in_=fm_t[j].rearrange("p (c t) -> p c t", t=P)).then_inc(s_st.inc(16), 16)
        mk[i]["fm_store"] = s_st.n

        if quant:
            @blk.vector
            def _(vector, i=i, j=j):
                if i >= 2:
                    vector.wait_ge(s_pe.h, mk[i - 2]["xq_tr"])
                li = emit_fp4_quant(vector, xn_f[j], xq_t[j], fp4tmp, D // 16)
                li.then_inc(s_dv.inc(FP4_TMP_OPS), FP4_TMP_OPS)
            mk[i]["quant"] = s_dv.n

            @blk.tensor
            def _(tensor, i=i, j=j):
                tensor.wait_ge(s_dv.h, mk[i]["quant"])
                if i >= 2:
                    tensor.wait_ge(s_dv.h, mk[i - 2]["xq_evac"])
                pv = b.psv(4 + j, 1, BF16)
                for c in range(DC):
                    li = tensor.transpose(pv[:, c * P:(c + 1) * P], xq_t[j][:, c * P:(c + 1) * P],
                                          cm["ident_bf"])
                li.then_inc(s_pe.inc(), 1)
            mk[i]["xq_tr"] = s_pe.n

            @blk.vector
            def _(vector, i=i, j=j):
                vector.wait_ge(s_pe.h, mk[i]["xq_tr"])
                if i >= 2:
                    vector.wait_ge(s_st.h, mk[i - 2]["xq_store"])
                pv = b.psv(4 + j, 1, BF16)
                vector.tensor_copy(out=xq_t2[j], in_=pv).then_inc(s_dv.inc(), 1)
            mk[i]["xq_evac"] = s_dv.n

            @blk.sync
            def _(sync, i=i, j=j):
                sync.wait_ge(s_dv.h, mk[i]["xq_evac"])
                sync.dma_start(out=xqv[:, :, i * P:(i + 1) * P],
                               in_=xq_t2[j].rearrange("p (c t) -> p c t", t=P)).then_inc(s_st.inc(16), 16)
            mk[i]["xq_store"] = s_st.n

    barrier(blk, sems)
    ar.pop()


def emit_matmul_chain(tensor, b, bank, lhsT_sb, rhs_sb, nk, extra=None, n=512):
    pv = b.psv(bank, 1, F32, [P, n]) if n <= 512 else None
    for k in range(nk):
        li = tensor.matmul(pv, lhsT_sb(k), rhs_sb(k), start=(k == 0),
                           stop=(k == nk - 1 and not extra))
    if extra:
        for idx, (l_ap, r_ap) in enumerate(extra):
            li = tensor.matmul(pv, l_ap, r_ap, start=False, stop=(idx == len(extra) - 1))
    return li


def emit_attn0(b, blk, cm, xn_d, wdram, out0_fm):
    nc, ar, sems = b.nc, b.ar, b.sems
    s_ld, s_gp, s_pe, s_dv, s_ac, s_st = (sems["ld"], sems["gp"], sems["pe"],
                                          sems["dv"], sems["ac"], sems["st"])
    ar.push()
    wnames = ["qkwT", "vwT", "wqT", "wkT", "wvT", "owT"]
    wsb = {n: ar.alloc([P, DC * D], BF16, "w_" + n) for n in wnames}
    xn_t = ar.alloc([P, DC, 512], BF16, "a2_xn")
    qkin = ar.alloc([P, DC * 512], BF16, "a2_qkin")
    vin = ar.alloc([P, DC * 512], BF16, "a2_vin")
    q_t = ar.alloc([P, H0 * 512], BF16, "a2_q")
    k_t = ar.alloc([P, H0 * 512], BF16, "a2_k")
    vtok = ar.alloc([P, 4 * D], BF16, "a2_vtok")
    attnO = qkin
    P_sb = vin[:, 0:512]
    Pn_sb = vin[:, 512:1024]
    bcb = vin[:, 1024:1536]
    linv = vin[0:1, 2048:3072].bitcast(F32)

    gp0 = s_gp.n

    @blk.gpsimd
    def _(gpsimd):
        for n in wnames:
            src = fm_view(wdram[n])
            gpsimd.dma_start(out=wsb[n].rearrange("p (k o) -> p k o", o=D),
                             in_=src).then_inc(s_gp.inc(16), 16)

    xnv = fm_view(xn_d)
    scale0 = 1.0 / math.sqrt(P)

    for g in range(4):
        ld_g = s_ld.n

        @blk.sync
        def _(sync, g=g):
            if g >= 1:
                sync.wait_ge(s_dv.h, s_dv.n)
            sync.dma_start(out=xn_t, in_=xnv[:, :, g * 512:(g + 1) * 512]).then_inc(s_ld.inc(16), 16)

        def lhs_w(name, c):
            return lambda k, name=name, c=c: wsb[name][:, k * D + c * P: k * D + (c + 1) * P]

        @blk.tensor
        def _(tensor, g=g, ld_g=ld_g):
            tensor.wait_ge(s_ld.h, ld_g + 16)
            tensor.wait_ge(s_gp.h, gp0 + 6 * 16)
            for ci, (wname, bname) in enumerate([("qkwT", "qk_b"), ("vwT", "v_b")]):
                for c in range(DC):
                    bank = (ci * DC + c) % 2
                    if ci * DC + c >= 2:
                        tensor.wait_ge(s_ac.h, s_ac.n - 1)
                    emit_matmul_chain(tensor, b, bank, lhs_w(wname, c),
                                      lambda k: xn_t[:, k, :], DC,
                                      extra=[(cm[bname][:, c * P:(c + 1) * P], cm["ones_row_f32"])]
                                      ).then_inc(s_pe.inc(), 1)
                    s_ac.n += 1
            s_ac.n -= 16

        @blk.scalar
        def _(scalar, g=g):
            for ci, dst in enumerate([qkin, vin]):
                for c in range(DC):
                    bank = (ci * DC + c) % 2
                    scalar.wait_ge(s_pe.h, s_pe.n - 16 + ci * DC + c + 1)
                    scalar.activation(out=dst[:, c * 512:(c + 1) * 512],
                                      in_=b.psv(bank, 1, F32),
                                      func=AF.Copy).then_inc(s_ac.inc(), 1)

        barrier(blk, sems)
        @blk.tensor
        def _(tensor, g=g):
            for ci, (wname, bname) in enumerate([("wqT", "q_b"), ("wkT", "k_b")]):
                for c in range(DC):
                    bank = (ci * DC + c) % 2
                    if ci * DC + c >= 2:
                        tensor.wait_ge(s_ac.h, s_ac.n - 1)
                    emit_matmul_chain(tensor, b, bank, lhs_w(wname, c),
                                      lambda k: qkin[:, k * 512:(k + 1) * 512], DC,
                                      extra=[(cm[bname][:, c * P:(c + 1) * P], cm["ones_row_f32"])]
                                      ).then_inc(s_pe.inc(), 1)
                    s_ac.n += 1
            s_ac.n -= 16

        @blk.scalar
        def _(scalar, g=g):
            for ci, dst in enumerate([q_t, k_t]):
                for c in range(DC):
                    bank = (ci * DC + c) % 2
                    scalar.wait_ge(s_pe.h, s_pe.n - 16 + ci * DC + c + 1)
                    scalar.activation(out=dst[:, c * 512:(c + 1) * 512],
                                      in_=b.psv(bank, 1, F32),
                                      func=AF.Copy).then_inc(s_ac.inc(), 1)

        @blk.tensor
        def _(tensor, g=g):
            for sb in range(4):
                for half in range(2):
                    u = sb * 2 + half
                    bank = 2 + u % 2
                    if u >= 2:
                        tensor.wait_ge(s_dv.h, s_dv.n - 1)
                    pv = b.psv(bank, 1, F32)
                    for k in range(DC):
                        tensor.matmul(pv, vin[:, k * 512 + sb * P: k * 512 + (sb + 1) * P],
                                      wsb["wvT"][:, k * D + half * 512:k * D + (half + 1) * 512],
                                      start=(k == 0), stop=False)
                    tensor.matmul(pv, cm["ones_col_f32"],
                                  cm["bv_row"][:, half * 512:(half + 1) * 512],
                                  start=False, stop=True).then_inc(s_pe.inc(), 1)
                    s_dv.n += 1
            s_dv.n -= 8

        @blk.vector
        def _(vector, g=g):
            for sb in range(4):
                for half in range(2):
                    u = sb * 2 + half
                    bank = 2 + u % 2
                    vector.wait_ge(s_pe.h, s_pe.n - 8 + u + 1)
                    vector.tensor_copy(out=vtok[:, sb * D + half * 512: sb * D + (half + 1) * 512],
                                       in_=b.psv(bank, 1, F32)).then_inc(s_dv.inc(), 1)

        barrier(blk, sems)

        for h in range(H0):
            pe_h = s_pe.n

            @blk.tensor
            def _(tensor, g=g, h=h):
                if h >= 1:
                    tensor.wait_ge(s_dv.h, s_dv.n)
                Sv = b.psv(4, 1, F32)
                for sb in range(4):
                    tensor.matmul(Sv[:, sb * P:(sb + 1) * P],
                                  k_t[:, h * 512 + sb * P: h * 512 + (sb + 1) * P],
                                  q_t[:, h * 512 + sb * P: h * 512 + (sb + 1) * P],
                                  start=(sb == 0), stop=False)
                    li = tensor.matmul(Sv[:, sb * P:(sb + 1) * P], cm["negCI_bf"], cm["maskA_bf"],
                                       start=False, stop=(sb == 3))
                li.then_inc(s_pe.inc(), 1)

            @blk.scalar
            def _(scalar, g=g, h=h, pe_h=pe_h):
                scalar.wait_ge(s_pe.h, pe_h + 1)
                scalar.activation(out=P_sb, in_=b.psv(4, 1, F32), func=AF.Exp,
                                  scale=scale0).then_inc(s_ac.inc(), 1)

            ac_h = s_ac.n

            @blk.tensor
            def _(tensor, g=g, h=h, ac_h=ac_h):
                tensor.wait_ge(s_ac.h, ac_h)
                lv = b.psv(5, 1, F32, [1, 512])
                for sb in range(4):
                    li = tensor.matmul(lv[:, sb * P:(sb + 1) * P], cm["ones_col_bf"],
                                       P_sb[:, sb * P:(sb + 1) * P], start=(sb == 0),
                                       stop=(sb == 3))
                li.then_inc(s_pe.inc(), 1)

            pe_l = s_pe.n

            @blk.vector
            def _(vector, g=g, h=h, pe_l=pe_l):
                vector.wait_ge(s_pe.h, pe_l)
                vector.reciprocal(out=linv, in_=b.psv(5, 1, F32, [1, 512])).then_inc(s_dv.inc(), 1)

            dv_l = s_dv.n

            @blk.tensor
            def _(tensor, g=g, h=h, dv_l=dv_l):
                tensor.wait_ge(s_dv.h, dv_l)
                tensor.matmul(b.psv(6, 1, F32), cm["ones_col_f32"], linv,
                              start=True, stop=True).then_inc(s_pe.inc(), 1)

            pe_bc = s_pe.n

            @blk.scalar
            def _(scalar, g=g, h=h, pe_bc=pe_bc):
                scalar.wait_ge(s_pe.h, pe_bc)
                scalar.activation(out=bcb, in_=b.psv(6, 1, F32), func=AF.Copy).then_inc(s_ac.inc(), 1)

            ac_bc = s_ac.n

            @blk.vector
            def _(vector, g=g, h=h, ac_bc=ac_bc):
                vector.wait_ge(s_ac.h, ac_bc)
                vector.tensor_tensor(out=Pn_sb, in0=P_sb, in1=bcb, op=ALU.mult).then_inc(s_dv.inc(), 1)

            dv_pn = s_dv.n

            @blk.tensor
            def _(tensor, g=g, h=h, dv_pn=dv_pn):
                tensor.wait_ge(s_dv.h, dv_pn)
                av = b.psv(7, 1, F32)
                for sb in range(4):
                    li = tensor.matmul(av[:, sb * P:(sb + 1) * P],
                                       vtok[:, sb * D + h * P: sb * D + (h + 1) * P],
                                       Pn_sb[:, sb * P:(sb + 1) * P], start=(sb == 0),
                                       stop=(sb == 3))
                li.then_inc(s_pe.inc(), 1)

            pe_av = s_pe.n

            @blk.vector
            def _(vector, g=g, h=h, pe_av=pe_av):
                vector.wait_ge(s_pe.h, pe_av)
                vector.tensor_copy(out=attnO[:, h * 512:(h + 1) * 512],
                                   in_=b.psv(7, 1, F32)).then_inc(s_dv.inc(), 1)

        barrier(blk, sems)

        @blk.tensor
        def _(tensor, g=g):
            for c in range(DC):
                bank = c % 2
                if c >= 2:
                    tensor.wait_ge(s_ac.h, s_ac.n - 1)
                emit_matmul_chain(tensor, b, bank, lhs_w("owT", c),
                                  lambda k: attnO[:, k * 512:(k + 1) * 512], DC,
                                  extra=[(cm["ob_b"][:, c * P:(c + 1) * P], cm["ones_row_f32"])]
                                  ).then_inc(s_pe.inc(), 1)
                s_ac.n += 1
            s_ac.n -= 8

        @blk.scalar
        def _(scalar, g=g):
            for c in range(DC):
                bank = c % 2
                scalar.wait_ge(s_pe.h, s_pe.n - 8 + c + 1)
                scalar.activation(
                    out=out0_fm.rearrange("p (c t) -> p c t", t=T_CORE)[:, c, g * 512:(g + 1) * 512],
                    in_=b.psv(bank, 1, F32),
                    func=AF.Copy).then_inc(s_ac.inc(), 1)

        barrier(blk, sems)

    ar.pop()


def emit_ffn_combine(b, blk, cm, stage, xn_src, w_dram, x_res_d, out_res_d, other_fm,
                     ds_d, xq_fm=None, fp4tmp=None):
    nc, ar, sems = b.nc, b.ar, b.sems
    s_ld, s_gp, s_pe, s_dv, s_ac, s_st = (sems["ld"], sems["gp"], sems["pe"],
                                          sems["dv"], sems["ac"], sems["st"])
    HID = FFN0_H if stage == "A" else FFN1_H
    HH = HID // 2
    HC = HH // P
    ar.push()
    wg_sb = ar.alloc([P, DC * HH], BF16, "f_wg")
    wu_sb = ar.alloc([P, DC * HH], BF16, "f_wu")
    wd_sb = ar.alloc([P, HC * D], BF16, "f_wd")
    if stage == "A":
        stage_i = ar.alloc([P, HH], F32, "f_stg")
        g_sb = ar.alloc([P, HC * 512], BF16, "f_g")
        u_sb = ar.alloc([P, HC * 512], BF16, "f_u")
        h_fm = ar.alloc([P, HC * 512], BF16, "f_h")
        xn_t = ar.alloc([P, DC, 512], BF16, "f_xn")
    else:
        xq_t = ar.alloc([P, DC, 512], BF16, "f_xqt")
        g1 = ar.alloc([P, HH], F32, "f_gt")
        u1 = ar.alloc([P, HH], F32, "f_ut")
        h1 = ar.alloc([P, HH], F32, "f_ht")
        hq1 = ar.alloc([P, HH], BF16, "f_hqt")
        g_t = [g1, g1]; u_t = [u1, u1]; h_t = [h1, h1]; hq_t = [hq1, hq1]
        h_fm = ar.alloc([P, HC * 512], BF16, "f_hfm")
        oth_t = ar.alloc([P, DC * 256], F32, "f_oth")
    ds_t = ar.alloc([P, DC * 256], F32, "f_ds")
    sfm = ar.alloc([P, DC * 256], F32, "f_sfm")
    sfm2 = ds_t
    x0_t = sfm[:, 0:D]
    x1_t = sfm[:, D:2 * D]

    xnv = fm_view(xn_src) if stage == "A" else None
    dsv = fm_view(ds_d)
    othv = fm_view(other_fm) if stage == "B" else None
    x0v = x_res_d.rearrange("(n p) d -> n p d", p=P)
    o0v = out_res_d.rearrange("(n p) d -> n p d", p=P)
    xqv_d = fm_view(xq_fm) if stage == "B" else None

    for half in range(2):
        if stage == "A":
            for wname, wsb_, nk, wid in (("wg", wg_sb, DC, HH), ("wu", wu_sb, DC, HH),
                                         ("wd", wd_sb, HC, D)):
                for k in range(nk):
                    gp_k = s_gp.n

                    @blk.gpsimd
                    def _(gpsimd, half=half, k=k, wname=wname, wid=wid):
                        gpsimd.wait_ge(s_dv.h, s_dv.n)
                        if wname == "wd":
                            srcv = w_dram["wd"][half * HH + k * P: half * HH + (k + 1) * P, :]
                        else:
                            srcv = w_dram[wname][k * P:(k + 1) * P, half * HH:(half + 1) * HH]
                        gpsimd.dma_start(out=stage_i[:, 0:wid], in_=srcv).then_inc(s_gp.inc(16), 16)

                    @blk.vector
                    def _(vector, k=k, gp_k=gp_k, wsb_=wsb_, wid=wid):
                        vector.wait_ge(s_gp.h, gp_k + 16)
                        vector.tensor_scalar(out=wsb_[:, k * wid:(k + 1) * wid],
                                             in0=stage_i[:, 0:wid],
                                             scalar1=1.0 / 127.0, scalar2=None,
                                             op0=ALU.mult).then_inc(s_dv.inc(), 1)
        else:
            @blk.gpsimd
            def _(gpsimd, half=half):
                gpsimd.dma_start(
                    out=wg_sb.rearrange("p (k o) -> p k o", o=HH),
                    in_=fm_view(w_dram["wg"])[:, :, half * HH:(half + 1) * HH]
                ).then_inc(s_gp.inc(16), 16)
                gpsimd.dma_start(
                    out=wu_sb.rearrange("p (k o) -> p k o", o=HH),
                    in_=fm_view(w_dram["wu"])[:, :, half * HH:(half + 1) * HH]
                ).then_inc(s_gp.inc(16), 16)
                gpsimd.dma_start(
                    out=wd_sb.rearrange("p (k o) -> p k o", o=D),
                    in_=fm_view(w_dram["wd"])[:, half * HC:(half + 1) * HC, :]
                ).then_inc(s_gp.inc(16), 16)

        barrier(blk, sems)

        for tt in range(4):
            if stage == "A":
                ld_t = s_ld.n

                @blk.sync
                def _(sync, half=half, tt=tt):
                    sync.wait_ge(s_pe.h, s_pe.n)
                    sync.dma_start(out=xn_t, in_=xnv[:, :, tt * 512:(tt + 1) * 512]).then_inc(s_ld.inc(16), 16)

                @blk.tensor
                def _(tensor, half=half, tt=tt, ld_t=ld_t):
                    tensor.wait_ge(s_ld.h, ld_t + 16)
                    tensor.wait_ge(s_dv.h, s_dv.n)
                    for ci, wsb_ in enumerate([wg_sb, wu_sb]):
                        for hc in range(HC):
                            u = ci * HC + hc
                            bank = u % 2
                            if u >= 2:
                                tensor.wait_ge(s_ac.h, s_ac.n - 1)
                            pv = b.psv(bank, 1, F32)
                            for k in range(DC):
                                li = tensor.matmul(pv, wsb_[:, k * HH + hc * P: k * HH + (hc + 1) * P],
                                                   xn_t[:, k, :], start=(k == 0), stop=(k == DC - 1))
                            li.then_inc(s_pe.inc(), 1)
                            s_ac.n += 1
                    s_ac.n -= 2 * HC

                @blk.scalar
                def _(scalar, half=half, tt=tt):
                    for ci, (dst, fn) in enumerate([(g_sb, AF.Silu), (u_sb, AF.Copy)]):
                        for hc in range(HC):
                            u = ci * HC + hc
                            bank = u % 2
                            scalar.wait_ge(s_pe.h, s_pe.n - 2 * HC + u + 1)
                            scalar.activation(out=dst[:, hc * 512:(hc + 1) * 512],
                                              in_=b.psv(bank, 1, F32), func=fn).then_inc(s_ac.inc(), 1)

                ac_gu = s_ac.n

                @blk.vector
                def _(vector, half=half, tt=tt, ac_gu=ac_gu):
                    vector.wait_ge(s_ac.h, ac_gu)
                    vector.wait_ge(s_pe.h, s_pe.n)
                    vector.tensor_tensor(out=h_fm, in0=g_sb, in1=u_sb, op=ALU.mult).then_inc(s_dv.inc(), 1)
            else:
                ld_q = s_ld.n

                @blk.sync
                def _(sync, tt=tt):
                    sync.wait_ge(s_pe.h, s_pe.n)
                    sync.dma_start(out=xq_t, in_=xqv_d[:, :, tt * 512:(tt + 1) * 512]).then_inc(s_ld.inc(16), 16)

                dv_marks = {}
                for ts in range(4):
                    t0 = tt * 512 + ts * P
                    j = ts % 2

                    @blk.tensor
                    def _(tensor, half=half, tt=tt, ts=ts, t0=t0, ld_q=ld_q):
                        tensor.wait_ge(s_ld.h, ld_q + 16)
                        tensor.wait_ge(s_ac.h, s_ac.n)
                        if ts == 0 and not (half == 0 and tt == 0):
                            tensor.wait_ge(s_dv.h, s_dv.n)
                        for ci in range(2):
                            pv = b.psall[:, ci * 1536:ci * 1536 + HH]
                            for ht in range(3):
                                n = min(512, HH - ht * 512)
                                for k in range(DC):
                                    li = tensor.matmul(pv[:, ht * 512:ht * 512 + n],
                                                       xq_t[:, k, ts * P:(ts + 1) * P],
                                                       (wg_sb if ci == 0 else wu_sb)[:, k * HH + ht * 512: k * HH + ht * 512 + n],
                                                       start=(k == 0), stop=(k == DC - 1))
                        li.then_inc(s_pe.inc(), 1)

                    pe_gu = s_pe.n

                    @blk.scalar
                    def _(scalar, half=half, tt=tt, ts=ts, j=j, pe_gu=pe_gu):
                        scalar.wait_ge(s_pe.h, pe_gu)
                        if (half, tt, ts) in dv_marks:
                            pass
                        mk = dv_marks.get(("h_tt", ts - 1))
                        if mk is not None:
                            scalar.wait_ge(s_dv.h, mk)
                        scalar.activation(out=g_t[j], in_=b.psall[:, 0:HH], func=AF.Silu).then_inc(s_ac.inc(), 1)
                        scalar.activation(out=u_t[j], in_=b.psall[:, 1536:1536 + HH],
                                          func=AF.Copy).then_inc(s_ac.inc(), 1)

                    ac_gu = s_ac.n

                    @blk.vector
                    def _(vector, half=half, tt=tt, ts=ts, j=j, ac_gu=ac_gu):
                        vector.wait_ge(s_ac.h, ac_gu)
                        vector.tensor_tensor(out=h_t[j], in0=g_t[j], in1=u_t[j], op=ALU.mult).then_inc(s_dv.inc(), 1)
                        dv_marks[("h_tt", ts)] = s_dv.n
                        mk_tr = dv_marks.get(("pe_tr", ts - 1))
                        if mk_tr is not None:
                            vector.wait_ge(s_pe.h, mk_tr)
                        li = emit_fp4_quant(vector, h_t[j], hq_t[j], fp4tmp, HH // 16)
                        li.then_inc(s_dv.inc(FP4_TMP_OPS), FP4_TMP_OPS)

                    dv_q = s_dv.n
                    dv_marks[("gu", ts)] = dv_q

                    @blk.tensor
                    def _(tensor, half=half, tt=tt, ts=ts, j=j, dv_q=dv_q):
                        tensor.wait_ge(s_dv.h, dv_q)
                        if not (half == 0 and tt == 0 and ts == 0):
                            tensor.wait_ge(s_dv.h, s_dv.n)
                        pv = b.psv(6, 2, BF16)
                        for k in range(HC):
                            li = tensor.transpose(pv[:, k * P:(k + 1) * P],
                                                  hq_t[j][:, k * P:(k + 1) * P], cm["ident_bf"])
                        li.then_inc(s_pe.inc(), 1)

                    pe_tr = s_pe.n
                    dv_marks[("pe_tr", ts)] = pe_tr

                    @blk.vector
                    def _(vector, half=half, tt=tt, ts=ts, pe_tr=pe_tr):
                        vector.wait_ge(s_pe.h, pe_tr)
                        pv = b.psv(6, 2, BF16)
                        vector.tensor_copy(
                            out=h_fm.rearrange("p (k t) -> p k t", t=512)[:, :, ts * P:(ts + 1) * P],
                            in_=pv.rearrange("p (k t) -> p k t", t=P)[:, 0:HC, :]).then_inc(s_dv.inc(), 1)

            dv_h = s_dv.n

            for ttq in range(2):
                @blk.tensor
                def _(tensor, half=half, tt=tt, ttq=ttq, dv_h=dv_h):
                    tensor.wait_ge(s_dv.h, dv_h)
                    if not (half == 0 and tt == 0 and ttq == 0):
                        tensor.wait_ge(s_dv.h, s_dv.n)
                        tensor.wait_ge(s_ac.h, s_ac.n)
                    dbank = 2 if stage == "A" else 0
                    dpv = b.psv(dbank, 4, F32)
                    for c in range(DC):
                        for k in range(HC):
                            li = tensor.matmul(
                                dpv[:, c * 256:(c + 1) * 256],
                                wd_sb[:, k * D + c * P: k * D + (c + 1) * P],
                                h_fm[:, k * 512 + ttq * 256: k * 512 + ttq * 256 + 256],
                                start=(k == 0 and c % 2 == 0),
                                stop=(k == HC - 1 and c == DC - 1))
                    li.then_inc(s_pe.inc(), 1)

                pe_d = s_pe.n
                toff = tt * 512 + ttq * 256

                if half == 0:
                    @blk.scalar
                    def _(scalar, tt=tt, ttq=ttq, pe_d=pe_d):
                        scalar.wait_ge(s_pe.h, pe_d)
                        scalar.wait_ge(s_st.h, s_st.n)
                        scalar.activation(out=ds_t, in_=b.psv(2 if stage == "A" else 0, 4, F32),
                                          func=AF.Copy).then_inc(s_ac.inc(), 1)

                    ac_d = s_ac.n

                    @blk.sync
                    def _(sync, ac_d=ac_d, toff=toff):
                        sync.wait_ge(s_ac.h, ac_d)
                        sync.dma_start(
                            out=dsv[:, :, toff:toff + 256],
                            in_=ds_t.rearrange("p (c t) -> p c t", t=256)).then_inc(s_st.inc(16), 16)


                else:
                    @blk.sync
                    def _(sync, toff=toff, stage=stage):
                        sync.wait_ge(s_dv.h, s_dv.n)
                        sync.dma_start(out=ds_t,
                                       in_=dsv[:, :, toff:toff + 256]).then_inc(s_ld.inc(16), 16)
                        if stage == "B":
                            sync.dma_start(out=oth_t,
                                           in_=othv[:, :, toff:toff + 256]).then_inc(s_ld.inc(16), 16)

                    ld_c2 = s_ld.n

                    @blk.vector
                    def _(vector, half=half, tt=tt, ttq=ttq, pe_d=pe_d, ld_c2=ld_c2, toff=toff):
                        vector.wait_ge(s_pe.h, pe_d)
                        vector.wait_ge(s_ld.h, ld_c2)
                        vector.tensor_tensor(out=sfm, in0=b.psv(2 if stage == "A" else 0, 4, F32),
                                             in1=ds_t, op=ALU.add).then_inc(s_dv.inc(), 1)
                        if stage == "A":
                            o3 = other_fm.rearrange("p (c t) -> p c t", t=T_CORE)[:, :, toff:toff + 256]
                        else:
                            o3 = oth_t.rearrange("p (c t) -> p c t", t=256)
                        vector.tensor_tensor(out=sfm2.rearrange("p (c t) -> p c t", t=256),
                                             in0=sfm.rearrange("p (c t) -> p c t", t=256),
                                             in1=o3, op=ALU.add).then_inc(s_dv.inc(), 1)

                    dv_s = s_dv.n

                    for tk in range(2):
                        ti = toff // P + tk

                        @blk.tensor
                        def _(tensor, tk=tk, dv_s=dv_s, half=half, tt=tt, ttq=ttq, stage=stage):
                            tensor.wait_ge(s_dv.h, dv_s)
                            first = (tt == 0 and ttq == 0 and tk == 0)
                            if not first:
                                tensor.wait_ge(s_dv.h, s_dv.n)
                            pv = b.psv((0 if stage == "A" else 4) if tk == 0 else 6, 2, F32)
                            for c in range(DC):
                                li = tensor.transpose(
                                    pv[:, c * P:(c + 1) * P],
                                    sfm2.rearrange("p (c t) -> p c t", t=256)[:, c, tk * P:(tk + 1) * P],
                                    cm["ident_f32"])
                            li.then_inc(s_pe.inc(), 1)

                        pe_tr = s_pe.n

                        @blk.sync
                        def _(sync, ti=ti):
                            sync.wait_ge(s_dv.h, s_dv.n)
                            sync.dma_start(out=x0_t, in_=x0v[ti]).then_inc(s_ld.inc(16), 16)

                        ld_x2 = s_ld.n

                        @blk.vector
                        def _(vector, tk=tk, pe_tr=pe_tr, ld_x2=ld_x2):
                            vector.wait_ge(s_pe.h, pe_tr)
                            vector.wait_ge(s_ld.h, ld_x2)
                            vector.wait_ge(s_st.h, s_st.n)
                            vector.tensor_tensor(out=x1_t,
                                                 in0=b.psv((0 if stage == "A" else 4) if tk == 0 else 6, 2, F32),
                                                 in1=x0_t, op=ALU.add).then_inc(s_dv.inc(), 1)

                        dv_x1 = s_dv.n

                        @blk.sync
                        def _(sync, ti=ti, dv_x1=dv_x1):
                            sync.wait_ge(s_dv.h, dv_x1)
                            sync.dma_start(out=o0v[ti], in_=x1_t).then_inc(s_st.inc(16), 16)

        barrier(blk, sems)
    ar.pop()


def emit_fp4_weight_quant(b, blk, cm, w_in, wT_out, rows, cols, fp4tmp):
    nc, ar, sems = b.nc, b.ar, b.sems
    s_ld, s_pe, s_dv, s_ac, s_st = (sems["ld"], sems["pe"], sems["dv"], sems["ac"], sems["st"])
    RC = rows // P
    CW = 1408 if cols % 1408 == 0 else cols
    NSUB = cols // CW
    ar.push()
    wstg = [ar.alloc([P, cols], F32, "wq_s0"), ar.alloc([P, cols], F32, "wq_s1")]
    wqt = ar.alloc([P, cols], BF16, "wq_q")
    wqfm = [ar.alloc([P, cols], BF16, "wq_f0"), ar.alloc([P, cols], BF16, "wq_f1")]
    dst = wT_out.rearrange("(j p) o -> p j o", p=P)

    for r in range(RC):
        j = r % 2
        ld_r = s_ld.n

        @blk.sync
        def _(sync, r=r, j=j):
            if r >= 2:
                sync.wait_ge(s_dv.h, s_dv.n)
            sync.dma_start(out=wstg[j], in_=w_in[r * P:(r + 1) * P, :]).then_inc(s_ld.inc(16), 16)

        @blk.vector
        def _(vector, r=r, j=j, ld_r=ld_r):
            vector.wait_ge(s_ld.h, ld_r + 16)
            if r >= 1:
                vector.wait_ge(s_pe.h, s_pe.n)
            for s in range(NSUB):
                li = emit_fp4_quant(vector, wstg[j][:, s * CW:(s + 1) * CW],
                                    wqt[:, s * CW:(s + 1) * CW], fp4tmp, CW // 16)
            li.then_inc(s_dv.inc(NSUB * FP4_TMP_OPS), NSUB * FP4_TMP_OPS)

        dv_r = s_dv.n

        @blk.tensor
        def _(tensor, r=r, j=j, dv_r=dv_r):
            tensor.wait_ge(s_dv.h, dv_r)
            if r >= 2:
                tensor.wait_ge(s_dv.h, s_dv.n)
            pv = b.psv((r % 2) * 3, 3, BF16)
            for c in range(cols // P):
                li = tensor.transpose(pv[:, c * P:(c + 1) * P], wqt[:, c * P:(c + 1) * P],
                                      cm["ident_bf"])
            li.then_inc(s_pe.inc(), 1)

        pe_r = s_pe.n

        @blk.vector
        def _(vector, r=r, j=j, pe_r=pe_r):
            vector.wait_ge(s_pe.h, pe_r)
            if r >= 2:
                vector.wait_ge(s_st.h, s_st.n)
            pv = b.psv((r % 2) * 3, 3, BF16)
            vector.tensor_copy(out=wqfm[j], in_=pv[:, 0:cols]).then_inc(s_dv.inc(), 1)

        dv_f = s_dv.n

        @blk.sync
        def _(sync, r=r, j=j, dv_f=dv_f):
            sync.wait_ge(s_dv.h, dv_f)
            sync.dma_start(out=dst[:, :, r * P:(r + 1) * P],
                           in_=wqfm[j].rearrange("p (c t) -> p c t", t=P)).then_inc(s_st.inc(16), 16)

    barrier(blk, sems)
    ar.pop()


def emit_qkv1(b, blk, cm, ag_out_d, w_sb_map, q_sb, k_sb, vt_sb):
    nc, ar, sems = b.nc, b.ar, b.sems
    s_ld, s_gp, s_pe, s_dv, s_ac, s_st = (sems["ld"], sems["gp"], sems["pe"],
                                          sems["dv"], sems["ac"], sems["st"])
    ar.push()
    xg_t = [ar.alloc([P, DC, 512], BF16, "qk_xg0"), ar.alloc([P, DC, 512], BF16, "qk_xg1")]
    agv = ag_out_d.rearrange("(r c p) t -> p r c t", p=P, c=DC)

    @blk.vector
    def _(vector):
        ov = vt_sb.rearrange("p (n g e) -> p n g e", g=HPC, e=65)
        vector.memset(ov[:, :, :, 64:65], 1.0).then_inc(s_dv.inc(), 1)

    HD2 = HPC * DH1

    for tt in range(SEQ // 512):
        j = tt % 2
        rnk = (tt * 512) // T_CORE
        toff = (tt * 512) % T_CORE
        ld_t = s_ld.n

        @blk.sync
        def _(sync, tt=tt, j=j, rnk=rnk, toff=toff):
            if tt >= 2:
                sync.wait_ge(s_pe.h, s_pe.n)
            sync.dma_start(out=xg_t[j], in_=agv[:, rnk, :, toff:toff + 512]).then_inc(s_ld.inc(16), 16)

        @blk.tensor
        def _(tensor, tt=tt, j=j, ld_t=ld_t):
            tensor.wait_ge(s_ld.h, ld_t + 16)
            for ci, wname in enumerate(("wq1", "wk1")):
                for c in range(2):
                    u = ci * 2 + c
                    bank = u % 2
                    if not (tt == 0 and u < 2):
                        tensor.wait_ge(s_ac.h, s_ac.n - 1)
                    pv = b.psv(bank, 1, F32)
                    for k in range(DC):
                        li = tensor.matmul(pv, w_sb_map[wname][:, k * HD2 + c * P: k * HD2 + (c + 1) * P],
                                           xg_t[j][:, k, :], start=(k == 0), stop=(k == DC - 1))
                    li.then_inc(s_pe.inc(), 1)
                    s_ac.n += 1
            s_ac.n -= 4

        @blk.scalar
        def _(scalar, tt=tt):
            for ci, dst in enumerate((q_sb, k_sb)):
                for c in range(2):
                    u = ci * 2 + c
                    bank = u % 2
                    scalar.wait_ge(s_pe.h, s_pe.n - 4 + u + 1)
                    scale = 1.0 / 8.0 if ci == 0 else 1.0
                    scalar.activation(out=dst[:, c * SEQ + tt * 512: c * SEQ + tt * 512 + 512],
                                      in_=b.psv(bank, 1, F32), func=AF.Copy,
                                      scale=scale).then_inc(s_ac.inc(), 1)

        @blk.tensor
        def _(tensor, tt=tt, j=j):
            for sb in range(4):
                bank = 4 + sb % 2
                if not (tt == 0 and sb < 2):
                    tensor.wait_ge(s_dv.h, s_dv.n - 1)
                pv = b.psv(bank, 1, F32, [P, HD2])
                for k in range(DC):
                    li = tensor.matmul(pv, xg_t[j][:, k, sb * P:(sb + 1) * P],
                                       w_sb_map["wv1"][:, k * HD2:(k + 1) * HD2],
                                       start=(k == 0), stop=(k == DC - 1))
                li.then_inc(s_pe.inc(), 1)
                s_dv.n += 1
            s_dv.n -= 4

        @blk.vector
        def _(vector, tt=tt):
            for sb in range(4):
                bank = 4 + sb % 2
                vector.wait_ge(s_pe.h, s_pe.n - 4 + sb + 1)
                n = tt * 4 + sb
                dst = vt_sb.rearrange("p (m g e) -> p m g e", g=HPC, e=65)[:, n, :, 0:64]
                vector.tensor_copy(out=dst,
                                   in_=b.psv(bank, 1, F32, [P, HD2]).rearrange(
                                       "p (g e) -> p g e", e=64)).then_inc(s_dv.inc(), 1)

    barrier(blk, sems)
    ar.pop()


def emit_flash1(b, blk, cm, q_sb, k_sb, vt_sb, o_sb):
    nc, ar, sems = b.nc, b.ar, b.sems
    s_pe, s_dv, s_ac = sems["pe"], sems["dv"], sems["ac"]
    ar.push()
    P_both = [ar.alloc([P, 1024], BF16, f"fl_P{z}") for z in range(2)]
    linv = ar.alloc([1, 512], F32, "fl_linv")
    O_t = ar.alloc([64, 512], F32, "fl_O")

    for pair in range(2):
        for qi in range(SEQ // 512):
            nkb = (qi + 1) * 4
            pe0, ac0 = s_pe.n, s_ac.n
            dv_prev = s_dv.n

            def emit_qk(kb):
                z = kb % 2
                diag = kb >= nkb - 4
                doff = (qi * 512) - (kb * P)

                @blk.tensor
                def _(tensor, pair=pair, qi=qi, kb=kb, z=z, diag=diag, doff=doff):
                    if kb >= 2:
                        tensor.wait_ge(s_ac.h, ac0 + kb - 1)
                    if kb == 0:
                        tensor.wait_ge(s_dv.h, dv_prev)
                    for h in range(2):
                        Sv = b.psv(2 * z + h, 1, F32)
                        lh = k_sb[h * 64:(h + 1) * 64,
                                  pair * SEQ + kb * P: pair * SEQ + (kb + 1) * P]
                        rh = q_sb[h * 64:(h + 1) * 64,
                                  pair * SEQ + qi * 512: pair * SEQ + (qi + 1) * 512]
                        li = tensor.matmul(Sv, lh, rh, start=True, stop=not diag,
                                           tile_position=(h * 64, 0))
                    if diag:
                        u_sl = cm["U_big_bf"][:, doff + 384: doff + 384 + 512]
                        for h in range(2):
                            Sv = b.psv(2 * z + h, 1, F32)
                            li = tensor.matmul(Sv, cm["negCI_bf"], u_sl, start=False, stop=True)
                    li.then_inc(s_pe.inc(), 1)

                pe_qk = s_pe.n

                @blk.scalar
                def _(scalar, pair=pair, qi=qi, kb=kb, z=z, pe_qk=pe_qk):
                    scalar.wait_ge(s_pe.h, pe_qk)
                    scalar.activation(out=P_both[z], in_=b.psv(2 * z, 2, F32),
                                      func=AF.Exp).then_inc(s_ac.inc(), 1)

            def emit_av(kb, ac_e):
                z = kb % 2

                @blk.tensor
                def _(tensor, pair=pair, qi=qi, kb=kb, z=z, nkb=nkb, ac_e=ac_e):
                    tensor.wait_ge(s_ac.h, ac_e)
                    for h in range(2):
                        g = pair * 2 + h
                        av = b.psv(6 + h, 1, F32, [65, 512])
                        li = tensor.matmul(av, vt_sb[:, kb * 260 + g * 65: kb * 260 + (g + 1) * 65],
                                           P_both[z][:, h * 512:(h + 1) * 512],
                                           start=(kb == 0), stop=(kb == nkb - 1))
                    li.then_inc(s_pe.inc(), 1)

            ac_es = {}
            for kb in range(nkb):
                emit_qk(kb)
                ac_es[kb] = s_ac.n
                if kb >= 1:
                    emit_av(kb - 1, ac_es[kb - 1])
            emit_av(nkb - 1, ac_es[nkb - 1])

            for h in range(2):
                pe_av = s_pe.n
                dv_mark = s_dv.n

                @blk.vector
                def _(vector, h=h, pe_av=pe_av):
                    vector.wait_ge(s_pe.h, pe_av)
                    av = b.psv(6 + h, 1, F32, [65, 512])
                    vector.reciprocal(out=linv, in_=av[64:65, :]).then_inc(s_dv.inc(), 1)

                dv_r = s_dv.n

                @blk.tensor
                def _(tensor, h=h, dv_r=dv_r):
                    tensor.wait_ge(s_dv.h, dv_r)
                    tensor.matmul(b.psv(4, 1, F32, [64, 512]), cm["ones_col_f32"][:, 0:64],
                                  linv, start=True, stop=True).then_inc(s_pe.inc(), 1)

                pe_bc = s_pe.n

                @blk.scalar
                def _(scalar, h=h, pe_bc=pe_bc, dv_mark=dv_mark):
                    scalar.wait_ge(s_pe.h, pe_bc)
                    scalar.wait_ge(s_dv.h, dv_mark)
                    av = b.psv(6 + h, 1, F32, [65, 512])
                    scalar.activation(out=O_t, in_=av[0:64, :], func=AF.Copy).then_inc(s_ac.inc(), 1)

                ac_o = s_ac.n

                @blk.vector
                def _(vector, pair=pair, qi=qi, h=h, ac_o=ac_o, pe_bc=pe_bc):
                    vector.wait_ge(s_ac.h, ac_o)
                    vector.wait_ge(s_pe.h, pe_bc)
                    dst = o_sb[h * 64:(h + 1) * 64,
                               pair * SEQ + qi * 512: pair * SEQ + (qi + 1) * 512]
                    vector.tensor_tensor(out=dst, in0=O_t, in1=b.psv(4, 1, F32, [64, 512]),
                                         op=ALU.mult).then_inc(s_dv.inc(), 1)

    barrier(blk, sems)
    ar.pop()


def emit_outproj1(b, blk, cm, o_sb, wo_sb, rs_in_d):
    nc, ar, sems = b.nc, b.ar, b.sems
    s_pe, s_dv, s_ac, s_st = sems["pe"], sems["dv"], sems["ac"], sems["st"]
    ar.push()
    po_t = [ar.alloc([P, 512], F32, "op_t0"), ar.alloc([P, 512], F32, "op_t1")]
    rsv = rs_in_d.rearrange("(r c p) t -> p r c t", p=P, c=DC)

    for tt in range(SEQ // 512):
        rnk = (tt * 512) // T_CORE
        toff = (tt * 512) % T_CORE
        for c in range(DC):
            u = tt * DC + c
            bank = u % 2
            jj = u % 2

            @blk.tensor
            def _(tensor, tt=tt, c=c, u=u, bank=bank):
                if u >= 2:
                    tensor.wait_ge(s_ac.h, s_ac.n - 1)
                pv = b.psv(bank, 1, F32)
                for k in range(2):
                    li = tensor.matmul(pv, wo_sb[:, k * D + c * P: k * D + (c + 1) * P],
                                       o_sb[:, k * SEQ + tt * 512:k * SEQ + (tt + 1) * 512],
                                       start=(k == 0), stop=(k == 1))
                li.then_inc(s_pe.inc(), 1)

            pe_o = s_pe.n

            @blk.scalar
            def _(scalar, u=u, bank=bank, jj=jj, pe_o=pe_o):
                scalar.wait_ge(s_pe.h, pe_o)
                if u >= 2:
                    scalar.wait_ge(s_st.h, s_st.n - 16)
                scalar.activation(out=po_t[jj], in_=b.psv(bank, 1, F32),
                                  func=AF.Copy).then_inc(s_ac.inc(), 1)

            ac_o = s_ac.n

            @blk.sync
            def _(sync, rnk=rnk, toff=toff, c=c, jj=jj, ac_o=ac_o):
                sync.wait_ge(s_ac.h, ac_o)
                sync.dma_start(out=rsv[:, rnk, c, toff:toff + 512],
                               in_=po_t[jj]).then_inc(s_st.inc(16), 16)

    barrier(blk, sems)
    ar.pop()


def build_program(stages=("A", "B")):
    nc = bass.Bass("TRN2", target_bir_lowering=False)
    b = Builder(nc)
    ar, sems = b.ar, b.sems
    s_ld, s_gp, s_cc = sems["ld"], sems["gp"], sems["cc"]

    x_in = b.din("x_slice", [T_CORE, D], F32)
    wdram = {}
    for n in ("qkwT", "vwT", "wqT", "wkT", "wvT", "owT"):
        wdram[n] = b.din(n, [D, D], F32)
    bias_in = {}
    for n in ("qk_b", "v_b", "q_b", "k_b", "ob_b", "bv_row"):
        bias_in[n] = b.din(n, [1, D], F32)
    w0 = {"wg": b.din("wg0T", [D, FFN0_H], I32),
          "wu": b.din("wu0T", [D, FFN0_H], I32),
          "wd": b.din("wd0T", [FFN0_H, D], I32)}
    wg1 = b.din("wg1", [FFN1_H, D], F32)
    wu1 = b.din("wu1", [FFN1_H, D], F32)
    wd1 = b.din("wd1", [D, FFN1_H], F32)
    wq1 = b.din("wq1", [D, HPC * DH1], F32)
    wk1 = b.din("wk1", [D, HPC * DH1], F32)
    wv1 = b.din("wv1", [D, HPC * DH1], F32)
    wo1 = b.din("wo1", [HPC * DH1, D], F32)
    cin = {}
    cdefs = [("ident_bf", [P, P], BF16), ("ident_f32", [P, P], F32),
             ("negCI_bf", [P, P], BF16), ("maskA_bf", [P, P], BF16),
             ("U_big_bf", [P, 896], BF16), ("ones_row_f32", [1, 512], F32),
             ("ones_col_f32", [1, P], F32), ("ones_col_bf", [P, 1], BF16),
             ("maskpos", [P, 1], I32), ("masksign", [P, 1], I32),
             ("eps_col", [P, 1], F32)]
    for n, sh, dt in cdefs:
        cin[n] = b.din(n, sh, dt)

    only_a = "A" in stages and "B" not in stages
    only_b = "B" in stages and "A" not in stages
    if "B" in stages:
        out_d = b.dout("out_slice", [T_CORE, D], F32)
    if only_a:
        x1_d = b.dout("x1_slice", [T_CORE, D], F32)
    elif only_b:
        x1_d = b.din("x1_slice", [T_CORE, D], F32)
    else:
        x1_d = b.dint("x1_buf", [T_CORE, D], F32)
    mirrors = []

    def dtap(name, shape, dtype):
        t = b.dint(name, shape, dtype)
        if name in DEBUG_TAPS:
            mirrors.append((b.dout(name + "_tap", shape, dtype), t))
        return t

    def emit_mirrors(blk):
        if not mirrors:
            return
        @blk.sync
        def _(sync):
            for dst, srct in mirrors:
                sync.dma_start(out=dst, in_=srct).then_inc(b.sems["st"].inc(16), 16)
        barrier(blk, b.sems)
    xn_d = dtap("xn_buf", [D, T_CORE], BF16)
    ds_d = dtap("ds_buf", [D, T_CORE], F32)
    if "A" in stages and ("xn_buf" in DEBUG_TAPS or "ds_buf" in DEBUG_TAPS):
        pass
    if "B" in stages:
        xq_d = dtap("xq_buf", [D, T_CORE], BF16)
        ag_out_d = dtap("ag_out", [4 * D, T_CORE], BF16)
        rs_in_d = dtap("rs_in", [4 * D, T_CORE], F32)
        rs_out_d = dtap("rs_out", [D, T_CORE], F32)
        w1q = {"wg": dtap("wg1T_q", [D, FFN1_H], BF16),
               "wu": dtap("wu1T_q", [D, FFN1_H], BF16),
               "wd": dtap("wd1T_q", [FFN1_H, D], BF16)}

    cm = {n: ar.alloc(sh, dt, "c_" + n) for n, sh, dt in cdefs}
    for n in ("qk_b", "v_b", "q_b", "k_b", "ob_b", "bv_row"):
        cm[n] = ar.alloc([1, D], F32, "c_" + n)

    with nc.Block() as blk:
        @blk.sync
        def _(sync):
            for n, sh, dt in cdefs:
                sync.dma_start(out=cm[n], in_=cin[n]).then_inc(s_ld.inc(16), 16)
            for n in ("qk_b", "v_b", "q_b", "k_b", "ob_b", "bv_row"):
                sync.dma_start(out=cm[n], in_=bias_in[n]).then_inc(s_ld.inc(16), 16)

        barrier(blk, sems)

        if "A" in stages:
            emit_ln_quant(b, blk, "A", x_in, xn_d, cm, dbg_d=(b.dout("ln_dbg", [P, 8*16], F32) if "ln_dbg" in DEBUG_TAPS else None))
            ar.push()
            out0 = ar.alloc([P, DC * T_CORE], BF16, "out0")
            if "attn" in A_PHASES:
                emit_attn0(b, blk, cm, xn_d, wdram, out0)
                if "out0" in DEBUG_TAPS:
                    o0d = b.dout("out0", [P, DC * T_CORE], BF16)

                    @blk.sync
                    def _(sync):
                        sync.dma_start(out=o0d, in_=out0).then_inc(b.sems["st"].inc(16), 16)
                    barrier(blk, b.sems)
            if "ffn" in A_PHASES:
                emit_ffn_combine(b, blk, cm, "A", xn_d, w0, x_in, x1_d, out0, ds_d)
            else:
                @blk.sync
                def _(sync):
                    sync.dma_start(out=x1_d[0:1, 0:8],
                                   in_=cm["ones_row_f32"][0:1, 0:8]).then_inc(b.sems["st"].inc(16), 16)
                    sync.wait_ge(b.sems["st"].h, b.sems["st"].n)
            ar.pop()

        if "B" in stages:
            ar.push()
            fp4tmp_w = alloc_fp4_tmp(ar, 1408, cm, sems["dv"])
            emit_fp4_weight_quant(b, blk, cm, wg1, w1q["wg"], FFN1_H, D, fp4tmp_w)
            emit_fp4_weight_quant(b, blk, cm, wu1, w1q["wu"], FFN1_H, D, fp4tmp_w)
            emit_fp4_weight_quant(b, blk, cm, wd1, w1q["wd"], D, FFN1_H, fp4tmp_w)
            ar.pop()
            ar.push()
            fp4tmp_x = alloc_fp4_tmp(ar, D, cm, sems["dv"])
            emit_ln_quant(b, blk, "B", x1_d, xn_d, cm, xq_fm=xq_d, fp4tmp=fp4tmp_x)
            ar.pop()
            @blk.gpsimd
            def _(gpsimd):
                gpsimd.collective_compute(
                    "AllGather", ALU.bypass,
                    replica_groups=[[0, 1, 2, 3], [4, 5, 6, 7]],
                    ins=[xn_d], outs=[ag_out_d],
                ).then_inc(s_cc.inc(), 1)
                gpsimd.wait_ge(s_cc.h, s_cc.n)
            barrier(blk, sems)
            ar.push()
            w1sb = {}
            for n, src in (("wq1", wq1), ("wk1", wk1), ("wv1", wv1)):
                w1sb[n] = ar.alloc([P, DC * HPC * DH1], BF16, "w_" + n)
            wo_sb = ar.alloc([P, 2 * D], BF16, "w_wo1")

            @blk.gpsimd
            def _(gpsimd):
                for n, src in (("wq1", wq1), ("wk1", wk1), ("wv1", wv1)):
                    gpsimd.dma_start(out=w1sb[n].rearrange("p (k o) -> p k o", o=HPC * DH1),
                                     in_=src.rearrange("(k p) o -> p k o", p=P)).then_inc(s_gp.inc(16), 16)
                gpsimd.dma_start(out=wo_sb.rearrange("p (k o) -> p k o", o=D),
                                 in_=wo1.rearrange("(k p) o -> p k o", p=P)).then_inc(s_gp.inc(16), 16)
            barrier(blk, sems)

            q_sb = ar.alloc([P, 2 * SEQ], BF16, "q_sb")
            k_sb = ar.alloc([P, 2 * SEQ], BF16, "k_sb")
            vt_sb = ar.alloc([P, (SEQ // P) * 260], BF16, "vt_sb")
            o_sb = ar.alloc([P, 2 * SEQ], BF16, "o_sb")
            emit_qkv1(b, blk, cm, ag_out_d, w1sb, q_sb, k_sb, vt_sb)
            emit_flash1(b, blk, cm, q_sb, k_sb, vt_sb, o_sb)
            emit_outproj1(b, blk, cm, o_sb, wo_sb, rs_in_d)
            ar.pop()
            @blk.gpsimd
            def _(gpsimd):
                gpsimd.collective_compute(
                    "ReduceScatter", ALU.add,
                    replica_groups=[[0, 1, 2, 3], [4, 5, 6, 7]],
                    ins=[rs_in_d], outs=[rs_out_d],
                ).then_inc(s_cc.inc(), 1)
                gpsimd.wait_ge(s_cc.h, s_cc.n)
            barrier(blk, sems)
            ar.push()
            fp4tmp_h = alloc_fp4_tmp(ar, FFN1_H // 2, cm, sems["dv"])
            emit_ffn_combine(b, blk, cm, "B", None, w1q, x1_d, out_d, rs_out_d, ds_d,
                             xq_fm=xq_d, fp4tmp=fp4tmp_h)
            ar.pop()
        emit_mirrors(blk)

    nc._input_names = list(b.input_names)
    return nc


A_PHASES = ("attn", "ffn")
DEBUG_TAPS = ()

_PROG_CACHE = {}


def _get_prog(stages):
    key = tuple(stages)
    if key not in _PROG_CACHE:
        _PROG_CACHE[key] = build_program(stages)
    return _PROG_CACHE[key]


FUSED = True


def _run(nc, per_core):
    names = set(nc._input_names)
    maps = [{k: v for k, v in m.items() if k in names} for m in per_core]
    return run_bass_kernel_spmd(nc, maps, core_ids=list(range(NCORES)))


def kernel(**inputs):
    per_core = prep_inputs(inputs)
    if FUSED:
        nc = _get_prog(("A", "B"))
        res = _run(nc, per_core)
        out = np.concatenate([res.results[r]["out_slice"] for r in range(NCORES)], axis=0)
        return out.reshape(2, SEQ, D)
    nca = _get_prog(("A",))
    res_a = _run(nca, per_core)
    for r in range(NCORES):
        per_core[r]["x1_slice"] = np.ascontiguousarray(res_a.results[r]["x1_slice"])
    ncb = _get_prog(("B",))
    res_b = _run(ncb, per_core)
    out = np.concatenate([res_b.results[r]["out_slice"] for r in range(NCORES)], axis=0)
    return out.reshape(2, SEQ, D)
